# version 29
# baseline (speedup 1.0000x reference)
import math
import numpy as np
from contextlib import ExitStack

import concourse.bass as bass
import concourse.mybir as mybir
from concourse.bass_utils import run_bass_kernel_spmd

F32 = mybir.dt.float32
BF16 = mybir.dt.bfloat16
AF = mybir.ActivationFunctionType
ALU = mybir.AluOpType
AX = mybir.AxisListType

B, S, D = 4, 4096, 1024
T = 2048
NCH = D // 128
H, DK, DR, DC = 8, 128, 32, 32
NE, DFF = 8, 2048
SCALE = 1.0 / math.sqrt(DK + DR)
EPS = 1e-5
TH = 512
NTH = T // TH
DEBUG_OUTS = False
PHASES = "AB"

ENGINES = ("pe", "act", "dve", "pool", "sp")


class _Op:
    __slots__ = ("eng", "fn", "deps", "inc", "val", "dma_sem", "dma_val", "tag")

    def __init__(self, eng, fn, tag=""):
        self.eng = eng
        self.fn = fn
        self.deps = []
        self.inc = False
        self.val = None
        self.dma_sem = None
        self.dma_val = None
        self.tag = tag


class Stream:

    def __init__(self, name, n_sems):
        self.name = name
        self.n = n_sems
        self.uses = [0] * n_sems
        self.last_op = [None] * n_sems
        self.total = 0
        self.sems = None


class Prog:
    def __init__(self):
        self.ops = {e: [] for e in ENGINES}
        self.bufs = {}
        self.streams = []

    def stream(self, name, n_sems):
        s = Stream(name, n_sems)
        self.streams.append(s)
        return s

    def _dep(self, op, d):
        if d is None or d is op:
            return
        if d.eng == "pe" and op.eng == "pe" and d.dma_sem is None and op.dma_sem is None:
            return
        if d not in op.deps:
            op.deps.append(d)
            d.inc = True

    def _track(self, op, r, w, war):
        for name in r:
            ent = self.bufs.get(name)
            if ent is not None:
                self._dep(op, ent[0])
        for name in tuple(w) + tuple(war):
            ent = self.bufs.get(name)
            if ent is not None:
                self._dep(op, ent[0])
                for rd in ent[1]:
                    self._dep(op, rd)
        for name in r:
            ent = self.bufs.setdefault(name, [None, []])
            if op not in ent[1]:
                ent[1].append(op)
        for name in w:
            self.bufs[name] = [op, []]

    def op(self, eng, fn, r=(), w=(), war=(), tag=""):
        o = _Op(eng, fn, tag)
        self._track(o, r, w, war)
        for b in getattr(self, "_barrier", ()):
            self._dep(o, b)
        self.ops[eng].append(o)
        return o

    def dma(self, stream, eng, fn, r=(), w=(), war=(), tag=""):
        o = _Op(eng, fn, tag)
        idx = stream.total % stream.n
        stream.total += 1
        prev = stream.last_op[idx]
        if prev is not None:
            self._dep(o, prev)
        self._track(o, r, w, war)
        for b in getattr(self, "_barrier", ()):
            self._dep(o, b)
        stream.uses[idx] += 1
        o.dma_sem = (stream, idx)
        o.dma_val = 16 * stream.uses[idx]
        stream.last_op[idx] = o
        self.ops[eng].append(o)
        return o

    def barrier(self, eng, r, tag="barrier"):
        return self.op(eng, None, r=r, tag=tag)

    def frontier(self):
        f = []
        for e in ENGINES:
            for o in reversed(self.ops[e]):
                if o.fn is not None and o.dma_sem is None:
                    f.append(o)
                    break
        for st_ in self.streams:
            for o in st_.last_op:
                if o is not None:
                    f.append(o)
        return f

    def set_barrier(self):
        self._barrier = self.frontier()

    def lower(self, nc, ctx, block):
        E = ctx.enter_context
        esem = {e: E(nc.semaphore(f"sem_{e}")) for e in ENGINES}
        for s in self.streams:
            s.sems = [E(nc.semaphore(f"sem_{s.name}{i}")) for i in range(s.n)]
        for e in ENGINES:
            cnt = 0
            for o in self.ops[e]:
                if o.dma_sem is None and o.inc:
                    cnt += 1
                    o.val = cnt

        def wait_list(o):
            tgts = {}
            for d in o.deps:
                if d.dma_sem is not None:
                    sem = d.dma_sem[0].sems[d.dma_sem[1]]
                    v = d.dma_val
                else:
                    assert d.inc and d.val is not None, (d.tag, o.tag)
                    sem = esem[d.eng]
                    v = d.val
                k = id(sem)
                if k not in tgts or tgts[k][1] < v:
                    tgts[k] = (sem, v)
            return list(tgts.values())

        def run_engine(engobj, e):
            water = {}
            if e == "sp":
                self._water_sp = water
            for o in self.ops[e]:
                for sem, v in wait_list(o):
                    k = id(sem)
                    if water.get(k, 0) < v:
                        engobj.wait_ge(sem, v)
                        water[k] = v
                if o.fn is None:
                    assert not o.inc, o.tag
                    continue
                ins = o.fn(engobj)
                if o.dma_sem is not None:
                    sem = o.dma_sem[0].sems[o.dma_sem[1]]
                    ins.then_inc(sem, 16)
                elif o.inc:
                    ins.then_inc(esem[e], 1)

        @block.tensor
        def _(t):
            run_engine(t, "pe")

        @block.scalar
        def _(s):
            run_engine(s, "act")

        @block.vector
        def _(v):
            run_engine(v, "dve")

        @block.gpsimd
        def _(g):
            run_engine(g, "pool")

        @block.sync
        def _(sy):
            run_engine(sy, "sp")
            water = self._water_sp
            for st_ in self.streams:
                for j in range(st_.n):
                    if st_.uses[j]:
                        v = 16 * st_.uses[j]
                        k = id(st_.sems[j])
                        if water.get(k, 0) < v:
                            sy.wait_ge(st_.sems[j], v)




def dbg_pre_free(P, nc, dbg, kt, cqs, ckvtm, st_out):
    if not DEBUG_OUTS:
        return
    P.dma(st_out, "sp", lambda e: e.dma_start(out=dbg["kt"], in_=kt[:, :]),
          r=[f"kt_c{i}" for i in range(8)] + [f"kt_r{i}" for i in range(8)]
            + ["kt_ones"], w=["od_kt"], tag="st_kt")
    P.dma(st_out, "sp", lambda e: e.dma_start(out=dbg["cq"], in_=cqs[:, :]),
          r=[f"cq{i}" for i in range(4)], w=["od_cq"], tag="st_cq")
    P.dma(st_out, "sp", lambda e: e.dma_start(out=dbg["ctm"], in_=ckvtm[:, :]),
          r=[f"ckvtm{k}" for k in range(32)] + ["ckvtm_ones"], w=["od_ctm"],
          tag="st_ctm")


def build_program():
    nc = bass.Bass("TRN2", target_bir_lowering=False)
    P = Prog()

    def din(name, shape, dt=F32):
        return nc.dram_tensor(name, list(shape), dt, kind="ExternalInput").ap()

    xfm = din("xfm", (D, S))
    cfm = din("cfm", (128, NCH))
    wmod = din("wmod", (D, 6 * D))
    wdq = din("wdq", (D, DC))
    bdq = din("bdq", (DC, 1))
    wdkv = din("wdkv", (D, DC))
    bdkv = din("bdkv", (DC, 1))
    bdkvr = din("bdkvr", (1, DC))
    wkr = din("wkr", (D, DR))
    bkr = din("bkr", (DR, 1))
    wuq = din("wuq", (DC, H * DK))
    buq = din("buq", (128, H))
    wuk = din("wuk", (DC, H * DK))
    wuv = din("wuv", (DC, H * DK))
    buv = din("buv", (128, H))
    wqr = din("wqr", (DC, H * DR))
    bqr = din("bqr", (DR, H))
    wo = din("wo", (D, D))
    bo = din("bo", (128, NCH))
    gwt = din("gwt", (D, 12))
    geb = din("geb", (1, 12))
    w1t = din("w1t", (NE, D, DFF))
    w2t = din("w2t", (NE, DFF, D))
    eye = din("eye", (128, 128))
    sel = din("sel", (16, 9 * 128))

    out_fm = nc.dram_tensor("out_fm", [D, T], F32, kind="ExternalOutput").ap()
    w1c = nc.dram_tensor("w1c", [NE, D, DFF], BF16, kind="Internal").ap()
    w2c = nc.dram_tensor("w2c", [NE, DFF, D], BF16, kind="Internal").ap()
    dbg = {}
    if DEBUG_OUTS:
        dbg["kt"] = nc.dram_tensor("dbg_kt", [65, S], BF16, kind="ExternalOutput").ap()
        dbg["cq"] = nc.dram_tensor("dbg_cq", [DC, T], BF16, kind="ExternalOutput").ap()
        dbg["ctm"] = nc.dram_tensor("dbg_ctm", [128, 34 * (S // 128)], BF16,
                                    kind="ExternalOutput").ap()
        dbg["mod"] = nc.dram_tensor("dbg_mod", [128, 48], F32, kind="ExternalOutput").ap()
        dbg["h2"] = nc.dram_tensor("dbg_h2", [128, NCH * TH], BF16, kind="ExternalOutput").ap()
        dbg["wfm"] = nc.dram_tensor("dbg_wfm", [16, TH], F32, kind="ExternalOutput").ap()
        dbg["yini"] = nc.dram_tensor("dbg_yini", [128, NCH * TH], F32, kind="ExternalOutput").ap()
        dbg["xa"] = nc.dram_tensor("dbg_xa", [128, NCH * T], F32, kind="ExternalOutput").ap()

    ctx = ExitStack()
    E = ctx.enter_context

    xres = E(nc.sbuf_tensor("xres", [128, NCH * T], F32))
    mods = E(nc.sbuf_tensor("mods", [128, 48], F32))
    c1a = E(nc.sbuf_tensor("c1a", [128, NCH], F32))
    sav = E(nc.sbuf_tensor("sav", [128, NCH], F32))
    c1b = E(nc.sbuf_tensor("c1b", [128, NCH], F32))
    sbv = E(nc.sbuf_tensor("sbv", [128, NCH], F32))
    gav = E(nc.sbuf_tensor("gav", [128, NCH], F32))
    gbv = E(nc.sbuf_tensor("gbv", [128, NCH], F32))
    invgb = E(nc.sbuf_tensor("invgb", [128, NCH], F32))
    gab = E(nc.sbuf_tensor("gab", [128, NCH], F32))
    battn = E(nc.sbuf_tensor("battn", [128, NCH], F32))
    gtmp = E(nc.sbuf_tensor("gtmp", [128, 24], F32))
    cst = E(nc.sbuf_tensor("cst", [128, 2], F32))
    ones_c_f = E(nc.sbuf_tensor("ones_c_f", [128, 1], F32))
    ones_c_b = E(nc.sbuf_tensor("ones_c_b", [128, 1], BF16))
    ones128 = E(nc.sbuf_tensor("ones128", [128, 128], BF16))
    ones_r_f = E(nc.sbuf_tensor("ones_r_f", [1, 128], F32))
    ones_r_b = E(nc.sbuf_tensor("ones_r_b", [1, 128], BF16))
    eye_f = E(nc.sbuf_tensor("eye_f", [128, 128], F32))
    eye_b = E(nc.sbuf_tensor("eye_b", [128, 128], BF16))
    wdq_s = E(nc.sbuf_tensor("wdq_s", [128, NCH * DC], BF16))
    wdkv_s = E(nc.sbuf_tensor("wdkv_s", [128, NCH * DC], BF16))
    wkr_s = E(nc.sbuf_tensor("wkr_s", [128, NCH * DR], BF16))
    bdq_s = E(nc.sbuf_tensor("bdq_s", [DC, 1], F32))
    bdkv_s = E(nc.sbuf_tensor("bdkv_s", [DC, 1], F32))
    bdkvr_s = E(nc.sbuf_tensor("bdkvr_s", [1, DC], BF16))
    bkr_s = E(nc.sbuf_tensor("bkr_s", [DR, 1], F32))
    wqr_s = E(nc.sbuf_tensor("wqr_s", [DC, H * DR], BF16))
    buq_b = E(nc.sbuf_tensor("buq_b", [128, H], BF16))
    buv_b = E(nc.sbuf_tensor("buv_b", [128, H], BF16))
    bqr_sc = E(nc.sbuf_tensor("bqr_sc", [DR, H], F32))
    bo_s = E(nc.sbuf_tensor("bo_s", [128, NCH], F32))
    cfm_s = E(nc.sbuf_tensor("cfm_s", [128, NCH], F32))
    silu_b = E(nc.sbuf_tensor("silu_b", [128, NCH], BF16))
    gwt_s = E(nc.sbuf_tensor("gwt_s", [128, NCH * 12], F32))
    geb_s = E(nc.sbuf_tensor("geb_s", [1, 12], F32))
    bqr_s = E(nc.sbuf_tensor("bqr_s", [DR, H], F32))
    wuqT = E(nc.sbuf_tensor("wuqT", [128, H * DC], BF16))
    wukT = E(nc.sbuf_tensor("wukT", [128, H * DC], BF16))
    wuvT = E(nc.sbuf_tensor("wuvT", [128, H * DC], BF16))
    u1s = E(nc.sbuf_tensor("u1s", [DC, H], F32))
    t1f = [E(nc.sbuf_tensor(f"t1f{i}", [128, 512], F32)) for i in range(2)]
    rowa_ = [E(nc.sbuf_tensor(f"rowa{i}", [1, 512], F32)) for i in range(2)]
    rowb_ = [E(nc.sbuf_tensor(f"rowb{i}", [1, 512], F32)) for i in range(2)]
    rowc_ = [E(nc.sbuf_tensor(f"rowc{i}", [1, 512], F32)) for i in range(2)]
    rb_s = [E(nc.sbuf_tensor(f"rb_s{i}", [128, 512], F32)) for i in range(2)]
    mb_s = [E(nc.sbuf_tensor(f"mb_s{i}", [128, 512], F32)) for i in range(2)]
    mxk = E(nc.sbuf_tensor("mxk", [1, 20], F32))
    negm = E(nc.sbuf_tensor("negm", [128, 1], F32))
    negm2 = E(nc.sbuf_tensor("negm2", [128, 1], F32))
    sqk = E(nc.sbuf_tensor("sqk", [64, 2048], F32))
    sel_s = E(nc.sbuf_tensor("sel_s", [16, 9 * 128], F32))
    ctx_bc = ExitStack()
    Ebc = ctx_bc.enter_context
    cqs2 = None
    kt2 = None
    cqs = Ebc(nc.sbuf_tensor("cqs", [128, T], BF16))
    bst = Ebc(nc.sbuf_tensor("bst", [128, 4 * 128], BF16))
    kt = Ebc(nc.sbuf_tensor("kt", [128, S], BF16))
    ckvtm = Ebc(nc.sbuf_tensor("ckvtm", [128, 34 * (S // 128)], BF16))
    m2st = [Ebc(nc.sbuf_tensor(f"m2st{i}", [64, D], BF16)) for i in range(4)]
    ctx_ab = ExitStack()
    Eab = ctx_ab.enter_context
    wo_s = Eab(nc.sbuf_tensor("wo_s", [128, NCH * D], BF16))
    wuf = Eab(nc.sbuf_tensor("wuf", [DC, H * DK], F32))
    wmod_t = [Eab(nc.sbuf_tensor(f"wmod_t{i}", [128, NCH * 512], BF16)) for i in range(2)]
    xoth = [Eab(nc.sbuf_tensor("xoth0", [128, NCH * 512], F32))] * 2
    xbf = [Eab(nc.sbuf_tensor(f"xbf{i}", [128, 512], BF16)) for i in range(2)]
    xsq = [Eab(nc.sbuf_tensor(f"xsq{i}", [128, 512], BF16)) for i in range(2)]
    hbf = [Eab(nc.sbuf_tensor(f"hbf{i}", [128, NCH * 512], BF16)) for i in range(2)]

    psall = E(nc.psum_tensor("psall", [128, 8 * 512], F32))

    def psv(k, n=1):
        return psall[:, k * 512:(k + n) * 512]

    ps_mod = psv(0)
    ps_a = psv(1)
    ps_b = psv(2)
    ps_c = psv(3)
    ps_d = psv(4)
    ps_e = psv(5)
    ps_f = psv(6)
    ps_g = psv(6)

    st_small = P.stream("sm", 2)
    st_gp = P.stream("gp", 3)
    st_gp2 = P.stream("g2", 3)
    st_w1 = P.stream("w1", 3)
    st_w2 = P.stream("w2", 3)
    st_wb = P.stream("wb", 2)
    st_pp = P.stream("pp", 2)
    st_x = P.stream("x", 3)
    st_out = P.stream("o", 2)

    def r3(ap, p=128):
        return ap.rearrange("(c p) m -> p c m", p=p)

    P.op("dve", lambda v: v.memset(cst[:, 0:1], EPS), w=["cst"])
    P.op("dve", lambda v: v.memset(cst[:, 1:2], SCALE), w=["cst_s"])
    P.op("dve", lambda v: v.memset(ones_c_f[:, :], 1.0), w=["ones_c_f"])
    P.op("dve", lambda v: v.memset(ones_c_b[:, :], 1.0), w=["ones_c_b"])
    P.op("dve", lambda v: v.memset(ones128[:, :], 1.0), w=["ones128"])
    P.op("dve", lambda v: v.memset(ones_r_f[:, :], 1.0), w=["ones_r_f"])
    P.op("dve", lambda v: v.memset(ones_r_b[:, :], 1.0), w=["ones_r_b"])
    P.op("dve", lambda v: v.memset(
        ckvtm[:, :].rearrange("p (n c) -> p n c", c=34)[:, :, 32:34], 1.0),
        w=["ckvtm_ones"])
    P.op("dve", lambda v: v.memset(kt[64:128, :], 0.0), w=["ktz_"], tag="ktz_pre")
    P.op("dve", lambda v: v.memset(cqs[32:64, :], 0.0), w=["cqz"], tag="cqz1")
    P.op("dve", lambda v: v.memset(cqs[64:128, :], 0.0), r=["cqz"], w=["cqz"],
         tag="cqz2")
    P.op("dve", lambda v: v.memset(bst[32:64, :], 0.0), w=["bstz"], tag="bstz1")
    P.op("dve", lambda v: v.memset(bst[64:128, :], 0.0), r=["bstz"], w=["bstz"],
         tag="bstz2")
    P.op("dve", lambda v: v.memset(kt[64:65, :], 1.0), r=["ktz_"], w=["kt_ones"])

    for dst, src, nm in [
        (bdq_s[:, :], bdq, "bdq_s"), (bdkv_s[:, :], bdkv, "bdkv_s"),
        (bkr_s[:, :], bkr, "bkr_s"), (bqr_s[:, :], bqr, "bqr_s"),
        (bo_s[:, :], bo, "bo_s"), (cfm_s[:, :], cfm, "cfm_s"),
        (geb_s[:, :], geb, "geb_s"), (eye_f[:, :], eye, "eye_f"),
        (gwt_s[:, :].rearrange("p (c e) -> p c e", e=12), r3(gwt), "gwt_s"),
        (sel_s[:, :], sel, "sel_s"),
    ]:
        P.dma(st_small, "sp", (lambda d, s: lambda e: e.dma_start(out=d, in_=s))(dst, src),
              w=[nm], tag="ld_" + nm)

    for dst, src, nm in [
        (wdq_s[:, :].rearrange("p (c m) -> p c m", m=DC), r3(wdq), "wdq_s"),
        (wdkv_s[:, :].rearrange("p (c m) -> p c m", m=DC), r3(wdkv), "wdkv_s"),
        (wkr_s[:, :].rearrange("p (c m) -> p c m", m=DR), r3(wkr), "wkr_s"),
        (wqr_s[:, :], wqr, "wqr_s"),
        (buq_b[:, :], buq, "buq_b"), (buv_b[:, :], buv, "buv_b"),
        (bdkvr_s[:, :], bdkvr, "bdkvr_s"), (eye_b[:, :], eye, "eye_b"),
        (wo_s[:, :].rearrange("p (c n) -> p c n", n=D), r3(wo), "wo_s"),
    ]:
        P.dma(st_gp, "pool", (lambda d, s: lambda e: e.dma_start(out=d, in_=s))(dst, src),
              w=[nm], tag="cld_" + nm)

    P.op("act", lambda s: s.activation(silu_b[:, :], cfm_s[:, :], AF.Silu),
         r=["cfm_s"], w=["silu_b"], tag="siluc")

    for sl in range(12):
        par = sl % 2
        wt = wmod_t[par]
        src = wmod.rearrange("(c p) (s j) -> p c s j", p=128, j=512)[:, :, sl, :]
        P.dma(st_gp, "pool",
              (lambda d, s_: lambda e: e.dma_start(out=d, in_=s_))(
                  wt[:, :].rearrange("p (c j) -> p c j", j=512), src),
              w=[f"wmod_t{par}"], tag=f"ld_wmod{sl}")
        for mj in range(4):
            m = sl * 4 + mj
            for c in range(NCH):
                P.op("pe", (lambda wt_, c_, mj_, m_: lambda t: t.matmul(
                    ps_mod[:, m_:m_ + 1],
                    wt_[:, :].rearrange("p (c j) -> p c j", j=512)[:, c_, mj_ * 128:(mj_ + 1) * 128],
                    silu_b[:, c_:c_ + 1],
                    start=(c_ == 0), stop=(c_ == NCH - 1)))(wt, c, mj, m),
                    r=[f"wmod_t{par}", "silu_b"],
                    w=["psK0"] if (c == NCH - 1 and m == 47) else
                      (["psK0a"] if (c == NCH - 1 and m == 23) else ()),
                    war=["psK0"] if (c == 0 and m == 0) else (),
                    tag=f"mm_mod{m}_{c}")
        if sl == 5:
            P.op("act", lambda s: s.activation(mods[:, 0:24], ps_mod[:, 0:24],
                                               AF.Identity),
                 r=["psK0a"], w=["mods"], tag="ev_mod_a")
    P.op("act", lambda s: s.activation(mods[:, 24:48], ps_mod[:, 24:48],
                                       AF.Identity),
         r=["psK0"], w=["mods_b"], tag="ev_mod")

    P.op("dve", lambda v: v.tensor_scalar_add(c1a[:, :], mods[:, 8:16], 1.0),
         r=["mods"], w=["c1a"], tag="c1a")
    P.op("dve", lambda v: v.tensor_copy(sav[:, :], mods[:, 0:8]), r=["mods"], w=["sav"],
         tag="sav")
    P.op("dve", lambda v: v.tensor_scalar_add(c1b[:, :], mods[:, 32:40], 1.0),
         r=["mods_b"], w=["c1b"], tag="c1b")
    P.op("dve", lambda v: v.tensor_copy(sbv[:, :], mods[:, 24:32]), r=["mods_b"],
         w=["sbv"], tag="sbv")

    def clamp_gate(dst, lo, nm, src="mods"):
        P.op("dve", (lambda lo_: lambda v: v.tensor_scalar(
            gtmp[:, 0:8], mods[:, lo_:lo_ + 8], 0.0, None, op0=ALU.is_ge))(lo),
            r=[src], w=["g_ge"], tag=nm + "_ge")
        P.op("dve", lambda v: v.tensor_scalar(gtmp[:, 8:16], gtmp[:, 0:8],
                                              2.0, -1.0, op0=ALU.mult, op1=ALU.add),
             r=["g_ge"], w=["g_sgn"], tag=nm + "_sgn")
        P.op("act", (lambda lo_: lambda s: s.activation(
            gtmp[:, 16:24], mods[:, lo_:lo_ + 8], AF.Abs))(lo),
            r=[src], w=["g_mag"], tag=nm + "_abs")
        P.op("dve", lambda v: v.tensor_scalar(gtmp[:, 16:24], gtmp[:, 16:24],
                                              0.25, None, op0=ALU.max),
             r=["g_mag"], w=["g_mag"], tag=nm + "_mag")
        P.op("dve", (lambda dst_: lambda v: v.tensor_tensor(
            dst_[:, :], gtmp[:, 8:16], gtmp[:, 16:24], op=ALU.mult))(dst),
            r=["g_sgn", "g_mag"], w=[nm], tag=nm + "_mul")

    clamp_gate(gav, 16, "gav")
    clamp_gate(gbv, 40, "gbv", src="mods_b")
    P.op("dve", lambda v: v.reciprocal(invgb[:, :], gbv[:, :]), r=["gbv"], w=["invgb"],
         tag="invgb")
    P.op("dve", lambda v: v.tensor_scalar_mul(bqr_sc[:, :], bqr_s[:, :], SCALE),
         r=["bqr_s"], w=["bqr_sc"], tag="bqr_sc")

    for srcdram, dstT, dn in [(wuq, wuqT, "wuqT"), (wuk, wukT, "wukT"),
                              (wuv, wuvT, "wuvT")]:
        P.dma(st_small, "sp", (lambda s_: lambda e: e.dma_start(
            out=wuf[:, :], in_=s_))(srcdram), w=["wuf"], tag=f"ld_wuf_{dn}")
        for h in range(H):
            P.op("pe", (lambda h_: lambda t: t.transpose(
                psv(7)[:, 0:DC], wuf[:, h_ * DK:(h_ + 1) * DK], eye_f[0:DC, 0:DC]))(h),
                r=["wuf", "eye_f"], w=["psK7"], war=["psK7"],
                tag=f"tr_{dn}{h}")
            P.op("act", (lambda dstT_, h_: lambda s: s.activation(
                dstT_[:, h_ * DC:(h_ + 1) * DC], psv(7)[:, 0:DC], AF.Identity))(dstT, h),
                r=["psK7"], w=[f"{dn}{h}"], tag=f"ev_{dn}{h}")
    for h in range(H):
        P.op("pe", (lambda h_: lambda t: t.matmul(
            ps_a[0:DC, 0:DC], wuqT[:, h_ * DC:(h_ + 1) * DC],
            wukT[:, h_ * DC:(h_ + 1) * DC], start=True, stop=True))(h),
            r=[f"wuqT{h}", f"wukT{h}"], w=["psK1"], war=["psK1"], tag=f"mm_A{h}")
        P.op("act", (lambda h_: lambda s: s.activation(
            bst[0:DC, (h_ // 2) * 128 + (h_ % 2) * DC:
                (h_ // 2) * 128 + (h_ % 2) * DC + DC],
            ps_a[0:DC, 0:DC], AF.Identity))(h),
            r=["psK1"], w=[f"bstA{h}"], tag=f"ev_A{h}")
        P.op("pe", (lambda h_: lambda t: t.matmul(
            ps_b[0:DC, 0:1], wukT[:, h_ * DC:(h_ + 1) * DC],
            buq_b[:, h_:h_ + 1], start=True, stop=True))(h),
            r=[f"wukT{h}", "buq_b"], w=["psK2"], war=["psK2"], tag=f"mm_u1{h}")
        P.op("act", (lambda h_: lambda s: s.activation(
            u1s[:, h_:h_ + 1], ps_b[0:DC, 0:1], AF.Identity,
            scale=cst[0:DC, 1:2]))(h),
            r=["psK2", "cst_s"], w=[f"u1s{h}"], tag=f"ev_u1{h}")
        stk, row = h // 2, (h % 2) * 32
        for half in range(2):
            P.op("pe", (lambda h_, half_: lambda t: t.matmul(
                ps_c[0:DC, :], wuvT[:, h_ * DC:(h_ + 1) * DC],
                wo_s[:, :].rearrange("p (c n) -> p c n", n=D)[:, h_, half_ * 512:(half_ + 1) * 512],
                start=True, stop=True))(h, half),
                r=[f"wuvT{h}", "wo_s"], w=["psK3"], war=["psK3"], tag=f"mm_M2{h}_{half}")
            P.op("act", (lambda h_, half_, stk_, row_: lambda s: s.activation(
                m2st[stk_][row_:row_ + 32, half_ * 512:(half_ + 1) * 512],
                ps_c[0:DC, :], AF.Identity))(h, half, stk, row),
                r=["psK3"], w=[f"m2st{stk}_{row}_{half}"], tag=f"ev_M2{h}_{half}")
    for c in range(NCH):
        for h in range(H):
            P.op("pe", (lambda c_, h_: lambda t: t.matmul(
                ps_d[:, c_:c_ + 1],
                wo_s[:, :].rearrange("p (c n) -> p c n", n=D)[:, h_, c_ * 128:(c_ + 1) * 128],
                buv_b[:, h_:h_ + 1], start=(h_ == 0), stop=(h_ == H - 1)))(c, h),
                r=["wo_s", "buv_b"],
                w=["psK4"] if (h == H - 1 and c == NCH - 1) else (),
                war=["psK4"] if (h == 0 and c == 0) else (),
                tag=f"mm_bv{c}_{h}")
    P.op("dve", lambda v: v.tensor_tensor(battn[:, :], ps_d[:, 0:NCH], bo_s[:, :],
                                          op=ALU.add),
         r=["psK4", "bo_s"], w=["battn"], tag="battn")
    P.op("dve", lambda v: v.tensor_tensor(gab[:, :], gav[:, :], battn[:, :], op=ALU.mult),
         r=["gav", "battn"], w=["gab"], tag="gab")
    for h in range(H):
        P.op("dve", (lambda h_: lambda v: v.tensor_copy(
            bst[0:DC, (h_ // 2) * 128 + 64 + (h_ % 2) * DR:
                (h_ // 2) * 128 + 64 + (h_ % 2) * DR + DR],
            wqr_s[:, h_ * DR:(h_ + 1) * DR]))(h),
            r=["wqr_s"], w=[f"bstR{h}"], tag=f"bst_r{h}")

    for i in range(8):
        own = i < 4
        par = i % 2
        t0 = i * 512
        if own:
            xview = xres[:, :].rearrange("p (c t) -> p c t", t=T)[:, :, t0:t0 + 512]
            xbufn = f"xres_t{i}"
        else:
            xview = xoth[par][:, :].rearrange("p (c t) -> p c t", t=512)
            xbufn = "xoth"
        P.dma(st_x, "sp", (lambda d, s_: lambda e: e.dma_start(out=d, in_=s_))(
            xview, r3(xfm)[:, :, t0:t0 + 512]), w=[xbufn], tag=f"ld_x{i}")

        for c in range(NCH):
            pc = (i * NCH + c) % 2
            P.op("dve", (lambda xr, c_, pc_: lambda v: v.tensor_copy(
                xbf[pc_][:, :], xr[:, c_, :]))(xview, c, pc),
                r=[xbufn], w=[f"xbf{pc}"], tag=f"cast_x{i}_{c}")
            P.op("act", (lambda xr, c_, pc_: lambda s: s.activation(
                xsq[pc_][:, :], xr[:, c_, :], AF.Square))(xview, c, pc),
                r=[xbufn], w=[f"xsq{pc}"], tag=f"sq_x{i}_{c}")
            P.op("pe", (lambda c_, pc_: lambda t: t.matmul(
                ps_e[:, :], ones128[:, :], xbf[pc_][:, :],
                start=(c_ == 0), stop=(c_ == NCH - 1)))(c, pc),
                r=[f"xbf{pc}", "ones128"],
                w=["psK5"] if c == NCH - 1 else (), war=["psK5"] if c == 0 else (),
                tag=f"mm_sum{i}_{c}")
            P.op("pe", (lambda c_, pc_: lambda t: t.matmul(
                ps_f[:, :], ones128[:, :], xsq[pc_][:, :],
                start=(c_ == 0), stop=(c_ == NCH - 1)))(c, pc),
                r=[f"xsq{pc}", "ones128"],
                w=["psK6"] if c == NCH - 1 else (), war=["psK6"] if c == 0 else (),
                tag=f"mm_sumsq{i}_{c}")
        P.op("dve", (lambda par_: lambda v: v.tensor_scalar_mul(
            mb_s[par_][:, :], ps_e[:, :], 1.0 / D))(par),
            r=["psK5"], w=[f"mb_s{par}"], tag=f"mean{i}")
        P.op("dve", (lambda par_: lambda v: v.tensor_scalar_mul(
            rb_s[par_][:, :], ps_f[:, :], 1.0 / D))(par),
            r=["psK6"], w=[f"rb_s{par}"], tag=f"msq{i}")
        P.op("dve", (lambda par_: lambda v: v.tensor_tensor(
            t1f[par_][:, :], mb_s[par_][:, :], mb_s[par_][:, :], op=ALU.mult))(par),
            r=[f"mb_s{par}"], w=[f"t1f{par}"], tag=f"m2_{i}")
        P.op("dve", (lambda par_: lambda v: v.tensor_tensor(
            rb_s[par_][:, :], rb_s[par_][:, :], t1f[par_][:, :], op=ALU.subtract))(par),
            r=[f"rb_s{par}", f"t1f{par}"], w=[f"rb_s{par}"], tag=f"var_{i}")
        P.op("act", (lambda par_: lambda s: s.activation(
            rb_s[par_][:, :], rb_s[par_][:, :], AF.Ln, bias=cst[:, 0:1]))(par),
            r=[f"rb_s{par}", "cst"], w=[f"rb_s{par}"], tag=f"lnv_{i}")
        P.op("act", (lambda par_: lambda s: s.activation(
            rb_s[par_][:, :], rb_s[par_][:, :], AF.Exp, scale=-0.5))(par),
            r=[f"rb_s{par}"], w=[f"rb_s{par}"], tag=f"rstd_{i}")
        for c in range(NCH):
            pc = (i * NCH + c) % 2
            P.op("pool", (lambda xr, c_, pc_, par_: lambda g: g.tensor_tensor(
                t1f[pc_][:, :], xr[:, c_, :], mb_s[par_][:, :], op=ALU.subtract))(
                    xview, c, pc, par),
                r=[xbufn, f"mb_s{par}"], w=[f"t1f{pc}"], tag=f"hn1_{i}_{c}")
            P.op("dve", (lambda pc_, par_: lambda v: v.tensor_tensor(
                t1f[pc_][:, :], t1f[pc_][:, :], rb_s[par_][:, :], op=ALU.mult))(pc, par),
                r=[f"t1f{pc}", f"rb_s{par}"], w=[f"t1f{pc}"], tag=f"hn2_{i}_{c}")
            P.op("act", (lambda c_, pc_, par_: lambda s: s.activation(
                hbf[par_][:, :].rearrange("p (c t) -> p c t", t=512)[:, c_, :],
                t1f[pc_][:, :], AF.Identity, scale=c1a[:, c_:c_ + 1],
                bias=sav[:, c_:c_ + 1]))(c, pc, par),
                r=[f"t1f{pc}", "c1a", "sav"], w=[f"hbf{par}_{c}"], tag=f"hmod_{i}_{c}")
        hview = hbf[par][:, :].rearrange("p (c t) -> p c t", t=512)
        wdkv_v = wdkv_s[:, :].rearrange("p (c m) -> p c m", m=DC)
        wkr_v = wkr_s[:, :].rearrange("p (c m) -> p c m", m=DR)
        wdq_v = wdq_s[:, :].rearrange("p (c m) -> p c m", m=DC)
        for c in range(NCH):
            P.op("pe", (lambda hv, c_: lambda t: t.matmul(
                ps_a[0:DC, :], wdkv_v[:, c_, :], hv[:, c_, :],
                start=(c_ == 0), stop=(c_ == NCH - 1)))(hview, c),
                r=[f"hbf{par}_{c}", "wdkv_s"],
                w=["psK1"] if c == NCH - 1 else (), war=["psK1"] if c == 0 else (),
                tag=f"mm_ckv{i}_{c}")
        P.op("act", (lambda t0_: lambda s: s.activation(
            kt[0:DC, t0_:t0_ + 512], ps_a[0:DC, :], AF.Identity, bias=bdkv_s[:, 0:1]))(t0),
            r=["psK1", "bdkv_s"], w=[f"kt_c{i}"], tag=f"ev_ckv{i}")
        for c in range(NCH):
            P.op("pe", (lambda hv, c_: lambda t: t.matmul(
                ps_b[0:DR, :], wkr_v[:, c_, :], hv[:, c_, :],
                start=(c_ == 0), stop=(c_ == NCH - 1)))(hview, c),
                r=[f"hbf{par}_{c}", "wkr_s"],
                w=["psK2"] if c == NCH - 1 else (), war=["psK2"] if c == 0 else (),
                tag=f"mm_kr{i}_{c}")
        P.op("act", (lambda t0_: lambda s: s.activation(
            kt[32:64, t0_:t0_ + 512], ps_b[0:DR, :], AF.Identity, bias=bkr_s[:, 0:1]))(t0),
            r=["psK2", "bkr_s"], w=[f"kt_r{i}"], tag=f"ev_kr{i}")
        for tc in range(4):
            kvch = i * 4 + tc
            for c in range(NCH):
                P.op("pe", (lambda hv, c_, tc_: lambda t: t.matmul(
                    ps_c[:, 0:DC], hv[:, c_, tc_ * 128:(tc_ + 1) * 128],
                    wdkv_v[:, c_, :], start=(c_ == 0), stop=False))(hview, c, tc),
                    r=[f"hbf{par}_{c}", "wdkv_s"], war=["psK3"] if c == 0 else (),
                    tag=f"mm_ctm{i}_{tc}_{c}")
            P.op("pe", lambda t: t.matmul(ps_c[:, 0:DC], ones_r_b[:, :], bdkvr_s[:, :],
                                          start=False, stop=True),
                 r=["ones_r_b", "bdkvr_s"], w=["psK3"], tag=f"mm_ctmb{i}_{tc}")
            P.op("act", (lambda kvch_: lambda s: s.activation(
                ckvtm[:, kvch_ * 34:kvch_ * 34 + 32], ps_c[:, 0:DC], AF.Identity))(kvch),
                r=["psK3"], w=[f"ckvtm{kvch}"], tag=f"ev_ctm{kvch}")
        if own:
            for c in range(NCH):
                P.op("pe", (lambda hv, c_: lambda t: t.matmul(
                    ps_d[0:DC, :], wdq_v[:, c_, :], hv[:, c_, :],
                    start=(c_ == 0), stop=(c_ == NCH - 1)))(hview, c),
                    r=[f"hbf{par}_{c}", "wdq_s"],
                    w=["psK4"] if c == NCH - 1 else (), war=["psK4"] if c == 0 else (),
                    tag=f"mm_cq{i}_{c}")
            P.op("act", (lambda t0_: lambda s: s.activation(
                cqs[0:DC, t0_:t0_ + 512], ps_d[0:DC, :], AF.Identity,
                bias=bdq_s[:, 0:1]))(t0),
                r=["psK4", "bdq_s"], w=[f"cq{i}"], tag=f"ev_cq{i}")

    for c8 in range(8):
        P.op("act", (lambda c_: lambda s: s.activation(
            sqk[:, 0:512], kt[0:64, c_ * 512:(c_ + 1) * 512], AF.Square))(c8),
            r=[f"kt_c{c8}", f"kt_r{c8}"], w=["sqk"], tag=f"sq_k{c8}")
        P.op("pe", lambda t: t.matmul(ps_e[0:1, :], ones_c_f[0:64, :], sqk[:, 0:512],
                                      start=True, stop=True),
             r=["sqk", "ones_c_f"], w=["psK5"], tag=f"mm_n2k{c8}")
        P.op("dve", (lambda c_: lambda v: v.reduce_max(
            mxk[0:1, c_:c_ + 1], ps_e[0:1, :], axis=AX.X))(c8),
            r=["psK5"], w=[f"mxk{c8}"], tag=f"max_k{c8}")
    P.op("dve", lambda v: v.reduce_max(mxk[0:1, 8:9], mxk[0:1, 0:8], axis=AX.X),
         r=[f"mxk{c}" for c in range(8)], w=["mxk8"], tag="maxk_all")
    P.op("act", lambda s: s.activation(mxk[0:1, 8:9], mxk[0:1, 8:9], AF.Sqrt),
         r=["mxk8"], w=["mxk8"], tag="maxk_sqrt")

    ctx_ab.close()
    ctx_c = ExitStack()
    Ec = ctx_c.enter_context
    qt_s = Ec(nc.sbuf_tensor("qt_s", [128, T], BF16))
    qt_s2 = Ec(nc.sbuf_tensor("qt_s2", [128, T], BF16))
    qt_s3 = Ec(nc.sbuf_tensor("qt_s3", [128, T], BF16))
    qt_s4 = Ec(nc.sbuf_tensor("qt_s4", [128, T], BF16))
    pt_s = [Ec(nc.sbuf_tensor(f"pt_s{i}", [128, 1024], BF16)) for i in range(2)]
    zst = [Ec(nc.sbuf_tensor(f"zst{i}", [64, T], BF16)) for i in range(4)]
    dinv_bc = Ec(nc.sbuf_tensor("dinv_bc", [128, 512], F32))
    zraw = Ec(nc.sbuf_tensor("zraw", [128, 512], F32))
    P.set_barrier()
    P.op("dve", lambda v: v.memset(qt_s[64:128, :], 0.0), w=["qtA_z"], tag="qz_A")
    P.op("dve", lambda v: v.memset(qt_s2[64:128, :], 0.0), w=["qtB_z"], tag="qz_B")
    P.op("dve", lambda v: v.memset(qt_s3[64:128, :], 0.0), w=["qtA_z"], tag="qz_A2")
    P.op("dve", lambda v: v.memset(qt_s4[64:128, :], 0.0), w=["qtB_z"], tag="qz_B2")

    qts2 = [[qt_s, qt_s2], [qt_s3, qt_s4]]
    negm_ = [negm, negm2]

    def build_steps(hp, dense=False):
        pp = hp % 2
        bq = 6 if dense else 7
        steps = []
        bstr = ([f"bstA{2 * hp}", f"bstA{2 * hp + 1}",
                 f"bstR{2 * hp}", f"bstR{2 * hp + 1}", "bstz", "cqz"])
        for qt in range(4):
            q0 = qt * 512
            bqq = (6 + qt % 2) if dense else 7
            steps.append((lambda hp=hp, q0=q0, qt=qt, bqq=bqq: P.op(
                "pe", (lambda hp_, q0_: lambda t: t.matmul(
                    psv(bqq), bst[:, hp_ * 128:(hp_ + 1) * 128],
                    cqs[:, q0_:q0_ + 512], start=True, stop=True))(hp, q0),
                r=bstr + [f"cq{qt}"], w=[f"psK{bqq}"], war=[f"psK{bqq}"],
                tag=f"mm_q{hp}_{qt}")))
            for hx in range(2):
                h = 2 * hp + hx
                qtt = qts2[pp][hx]
                pfx = "AB"[hx] + str(pp)
                steps.append((lambda h=h, q0=q0, qt=qt, qtt=qtt, pfx=pfx,
                              hx=hx, bqq=bqq: P.op(
                    "act", (lambda qtt_, h_, q0_, hx_: lambda s: s.activation(
                        qtt_[0:DC, q0_:q0_ + 512],
                        psv(bqq)[hx_ * DC:(hx_ + 1) * DC, :], AF.Identity,
                        scale=cst[0:DC, 1:2], bias=u1s[:, h_:h_ + 1]))(qtt, h, q0, hx),
                    r=[f"psK{bqq}", "cst_s", f"u1s{h}"], w=[f"qt{pfx}_c{qt}"],
                    tag=f"ev_qc{h}_{qt}")))
                steps.append((lambda h=h, q0=q0, qt=qt, qtt=qtt, pfx=pfx,
                              hx=hx, bqq=bqq: P.op(
                    "act", (lambda qtt_, h_, q0_, hx_: lambda s: s.activation(
                        qtt_[32:64, q0_:q0_ + 512],
                        psv(bqq)[64 + hx_ * DR:64 + (hx_ + 1) * DR, :], AF.Identity,
                        scale=cst[0:DR, 1:2], bias=bqr_sc[:, h_:h_ + 1]))(qtt, h, q0, hx),
                    r=[f"psK{bqq}", "cst_s", "bqr_sc"], w=[f"qt{pfx}_r{qt}"],
                    tag=f"ev_qr{h}_{qt}")))

        for hx in range(2):
            h = 2 * hp + hx
            qtt = qts2[pp][hx]
            pfx = "AB"[hx] + str(pp)
            steps.append((lambda h=h, qtt=qtt, pfx=pfx: P.op(
                "act", (lambda qtt_: lambda s: s.activation(
                    sqk[:, :], qtt_[0:64, :], AF.Square))(qtt),
                r=[f"qt{pfx}_c{q}" for q in range(4)] +
                  [f"qt{pfx}_r{q}" for q in range(4)],
                w=["sqk"], tag=f"sqq{h}")))
            for qt in range(4):
                nb = 6 + qt % 2 if dense else 7
                steps.append((lambda h=h, qt=qt, nb=nb: P.op(
                    "pe", (lambda qt_, nb_: lambda t: t.matmul(
                        psv(nb_)[0:1, :], ones_c_f[0:64, :],
                        sqk[:, qt_ * 512:(qt_ + 1) * 512], start=True, stop=True))(qt, nb),
                    r=["sqk", "ones_c_f"], w=[f"psK{nb}"], war=[f"psK{nb}"],
                    tag=f"mm_n2q{h}_{qt}")))
                steps.append((lambda h=h, qt=qt, nb=nb, hx=hx: P.op(
                    "dve", (lambda nb_, hx_, qt_: lambda v: v.reduce_max(
                        mxk[0:1, 9 + hx_ * 4 + qt_:10 + hx_ * 4 + qt_],
                        psv(nb_)[0:1, :], axis=AX.X))(nb, hx, qt),
                    r=[f"psK{nb}"], w=[f"mxq{hx}_{qt}"], tag=f"mxq{h}_{qt}")))

        steps.append((lambda hp=hp: P.op(
            "dve", lambda v: v.reduce_max(mxk[0:1, 18:19], mxk[0:1, 9:17],
                                          axis=AX.X),
            r=[f"mxq{hx_}_{qt_}" for hx_ in range(2) for qt_ in range(4)],
            w=["mxp"], tag=f"mxp{hp}")))
        steps.append((lambda hp=hp: P.op(
            "act", lambda s: s.activation(mxk[0:1, 18:19], mxk[0:1, 18:19],
                                          AF.Sqrt),
            r=["mxp"], w=["mxp"], tag=f"mxps{hp}")))
        steps.append((lambda hp=hp: P.op(
            "dve", lambda v: v.tensor_scalar(mxk[0:1, 18:19], mxk[0:1, 18:19],
                                             mxk[0:1, 8:9], -1.0,
                                             op0=ALU.mult, op1=ALU.mult),
            r=["mxp", "mxk8"], w=["mxp"], tag=f"mxpn{hp}")))
        steps.append((lambda hp=hp: P.op(
            "pe", lambda t: t.matmul(psv(bq)[:, 0:1], ones_r_f[:, :],
                                     mxk[0:1, 18:19], start=True, stop=True),
            r=["mxp", "ones_r_f"], w=[f"psK{bq}"], war=[f"psK{bq}"],
            tag=f"bc_mp{hp}")))
        steps.append((lambda hp=hp, pp=pp: P.op(
            "act", (lambda pp_: lambda s: s.activation(
                negm_[pp_][:, :], psv(bq)[:, 0:1], AF.Identity))(pp),
            r=[f"psK{bq}"], w=[f"negm{pp}"], tag=f"ev_mp{hp}")))
        return steps

    BUILD_INTERLEAVE = True
    epi_defer = []
    for st_f in build_steps(0, dense=True):
        st_f()

    for hp in range(4):
        pp = hp % 2
        if hp == 1:
            for e_i in range(NE):
                P.dma(st_pp, "pool", (lambda e_: lambda g: g.dma_start(
                    out=w1c[e_], in_=w1t[e_]))(e_i),
                    w=[f"w1c_{e_i}"], tag=f"pp_w1_{e_i}")
                P.dma(st_pp, "pool", (lambda e_: lambda g: g.dma_start(
                    out=w2c[e_], in_=w2t[e_]))(e_i),
                    w=[f"w2c_{e_i}"], tag=f"pp_w2_{e_i}")
        nxt = build_steps(hp + 1) if hp + 1 < 4 else []
        if not BUILD_INTERLEAVE:
            for st_f in nxt:
                st_f()
            nxt = []
        cur = 0
        for qt in range(4):
            q0 = qt * 512
            qrd = [f"qt{p_}{pp}_{k_}{qt}" for p_ in "AB" for k_ in ("c", "r")]

            def emit_scores(c32):
                par = c32 % 2
                i8 = c32 // 4
                ktn = [f"kt_c{i8}", f"kt_r{i8}", "kt_ones"]
                for hx in range(2):
                    P.op("pe", (lambda qtt_, c_, q0_, b_: lambda t: t.matmul(
                        psv(b_), kt[:, c_ * 128:(c_ + 1) * 128],
                        qtt_[:, q0_:q0_ + 512], start=True, stop=True))(
                            qts2[pp][hx], c32, q0, 2 * par + hx),
                        r=ktn + qrd + ["ktz_", "qtA_z", "qtB_z"],
                        w=[f"psK{2 * par + hx}"],
                        war=[f"psK{2 * par + hx}"],
                        tag=f"mm_s{hp}_{qt}_{c32}_{hx}")
                P.op("act", (lambda b_, par_, pp_: lambda s: s.activation(
                    pt_s[par_][:, :], psv(b_, 2), AF.Exp,
                    bias=negm_[pp_][:, 0:1]))(2 * par, par, pp),
                    r=[f"psK{2 * par}", f"psK{2 * par + 1}", f"negm{pp}"],
                    w=[f"pt{par}"], tag=f"exp{hp}_{qt}_{c32}")

            def emit_pv(c32):
                par = c32 % 2
                P.op("pe", (lambda c_, par_: lambda t: t.matmul(
                    psall[0:33, 4 * 512:4 * 512 + 512],
                    ckvtm[:, c_ * 34:c_ * 34 + 33],
                    pt_s[par_][:, 0:512],
                    start=(c_ == 0), stop=(c_ == 31)))(c32, par),
                    r=[f"ckvtm{c32}", "ckvtm_ones", f"pt{par}"],
                    w=["psK4"] if c32 == 31 else (),
                    war=["psK4"] if c32 == 0 else (),
                    tag=f"mm_pv{hp}_{qt}_{c32}_0")
                P.op("pe", (lambda c_, par_: lambda t: t.matmul(
                    psall[64:97, 5 * 512:5 * 512 + 512],
                    ckvtm[:, c_ * 34:c_ * 34 + 33],
                    pt_s[par_][:, 512:1024],
                    start=(c_ == 0), stop=(c_ == 31),
                    tile_position=(0, 64)))(c32, par),
                    r=[f"ckvtm{c32}", "ckvtm_ones", f"pt{par}"],
                    w=["psK5"] if c32 == 31 else (),
                    war=["psK5"] if c32 == 0 else (),
                    tag=f"mm_pv{hp}_{qt}_{c32}_1")

            emit_scores(0)
            for c32 in range(32):
                if c32 < 31:
                    emit_scores(c32 + 1)
                emit_pv(c32)
                if epi_defer:
                    epi_defer.pop(0)()
                tgt = ((qt * 32 + c32 + 1) * len(nxt)) // 126
                while cur < min(tgt, len(nxt)):
                    nxt[cur]()
                    cur += 1
            for hx in range(2):
                h = 2 * hp + hx
                row = hx * 32
                zb = 4 + hx
                zrow = 32 if hx == 0 else 96
                zlo = 0 if hx == 0 else 64
                pq = hx
                rq, rqn = rowa_[pq], f"rowa{pq}"
                P.op("act", (lambda rq, zb_, zr_: lambda s: s.activation(
                    rq[:, :], psall[zr_:zr_ + 1, zb_ * 512:zb_ * 512 + 512],
                    AF.Ln))(rq, zb, zrow),
                    r=[f"psK{zb}"], w=[rqn], tag=f"dln{h}_{qt}")
                P.op("dve", (lambda zb_, zl_, hx_: lambda v: v.tensor_copy(
                    zraw[zl_:zl_ + 32, :],
                    psall[zl_:zl_ + 32, zb_ * 512:zb_ * 512 + 512]))(zb, zlo, hx),
                    r=[f"psK{zb}"], w=[f"zraw{hx}"], tag=f"zcp{h}_{qt}")
                epi_defer.append((lambda rq=rq, rqn=rqn, h=h, qt=qt: P.op(
                    "pe", (lambda rq_: lambda t: t.matmul(
                        psv(6), ones_r_f[:, :], rq_[:, :],
                        start=True, stop=True))(rq),
                    r=[rqn, "ones_r_f"], w=["psK6"], war=["psK6"],
                    tag=f"bc_d{h}_{qt}")))
                epi_defer.append((lambda h=h, qt=qt: P.op(
                    "act", lambda s: s.activation(dinv_bc[:, :], psv(6), AF.Exp,
                                                  scale=-1.0),
                    r=["psK6"], w=["dinv_bc"], tag=f"ev_d{h}_{qt}")))
                epi_defer.append((lambda hp=hp, row=row, qt=qt, zlo=zlo, hx=hx,
                                  h=h: P.op(
                    "dve", (lambda hp_, row_, qt_, zl_: lambda v: v.tensor_tensor(
                        zst[hp_][row_:row_ + 32, qt_ * 512:(qt_ + 1) * 512],
                        zraw[zl_:zl_ + 32, :], dinv_bc[zl_:zl_ + 32, :],
                        op=ALU.mult))(hp, row, qt, zlo),
                    r=[f"zraw{hx}", "dinv_bc"], w=[f"zst{hp}_{row}_{qt}"],
                    tag=f"ev_z{h}_{qt}")))

    for fn in epi_defer:
        fn()
    epi_defer = []

    for c in range(NCH):
        P.op("dve", (lambda c_: lambda v: v.tensor_scalar_add(
            xres[:, c_ * T:(c_ + 1) * T], xres[:, c_ * T:(c_ + 1) * T],
            gab[:, c_:c_ + 1]))(c),
            r=["gab"] + [f"xres_t{i}" for i in range(4)],
            w=[f"xg{c}"], tag=f"gabadd{c}")
    znames = [f"zst{st_}_{rw}_{q}" for st_ in range(4) for rw in (0, 32) for q in range(4)]
    for c in range(NCH):
        for qt in range(4):
            bank = qt % 2
            for sti in range(4):
                P.op("pe", (lambda c_, qt_, sti_, bank_: lambda t: t.matmul(
                    psv(bank_), m2st[sti_][:, c_ * 128:(c_ + 1) * 128],
                    zst[sti_][:, qt_ * 512:(qt_ + 1) * 512],
                    start=(sti_ == 0), stop=(sti_ == 3)))(c, qt, sti, bank),
                    r=[f"m2st{sti}_0_0", f"m2st{sti}_0_1", f"m2st{sti}_32_0",
                       f"m2st{sti}_32_1", f"zst{sti}_0_{qt}", f"zst{sti}_32_{qt}"],
                    w=[f"psK{bank}"] if sti == 3 else (),
                    war=[f"psK{bank}"] if sti == 0 else (),
                    tag=f"mm_ao{c}_{qt}_{sti}")
            P.op("dve", (lambda c_, qt_, bank_: lambda v: v.scalar_tensor_tensor(
                xres[:, c_ * T + qt_ * 512:c_ * T + (qt_ + 1) * 512],
                psv(bank_), gav[:, c_:c_ + 1],
                xres[:, c_ * T + qt_ * 512:c_ * T + (qt_ + 1) * 512],
                op0=ALU.mult, op1=ALU.add))(c, qt, bank),
                r=[f"psK{bank}", "gav", f"xg{c}"], w=[f"xg{c}"],
                tag=f"res_ao{c}_{qt}")

    dbg_pre_free(P, nc, dbg, kt, cqs, ckvtm, st_out)
    ctx_c.close()
    ctx_bc.close()
    ytmp = t1f[0]
    ctx_e = ExitStack()
    Ee = ctx_e.enter_context
    h2b = [Ee(nc.sbuf_tensor(f"h2b{i}", [128, NCH * TH], BF16)) for i in range(2)]
    w1b = [Ee(nc.sbuf_tensor(f"w1b{i}", [128, NCH * 512], BF16)) for i in range(2)]
    w2b = [Ee(nc.sbuf_tensor(f"w2b{i}", [128, 4 * 1024], BF16)) for i in range(2)]
    heb = [Ee(nc.sbuf_tensor(f"heb{i}", [128, 4 * TH], BF16)) for i in range(2)]
    yac = Ee(nc.sbuf_tensor("yac", [128, NCH * TH], F32))
    t2f = t1f
    xb2 = [Ee(nc.sbuf_tensor(f"xb2{i}", [128, TH], BF16)) for i in range(2)]
    xs2 = [Ee(nc.sbuf_tensor(f"xs2{i}", [128, TH], BF16)) for i in range(2)]
    wfm_ = [Ee(nc.sbuf_tensor(f"wfm{i}", [16, TH], F32)) for i in range(2)]
    wt16 = Ee(nc.sbuf_tensor("wt16", [128, 64], F32))
    glog = Ee(nc.sbuf_tensor("glog", [128, 48], F32))
    gsc = Ee(nc.sbuf_tensor("gsc", [128, 192], F32))
    wzb_ = [Ee(nc.sbuf_tensor(f"wzb{i}", [128, TH], BF16)) for i in range(2)]
    wbb = Ee(nc.sbuf_tensor("wbb", [128, TH], F32))
    h2f = Ee(nc.sbuf_tensor("h2f", [128, 4 * TH], F32))
    P.set_barrier()
    P.op("dve", lambda v: v.memset(wt16[:, :], 0.0), w=["wt16z"], tag="z_wt16")
    if DEBUG_OUTS:
        P.dma(st_out, "sp", lambda e: e.dma_start(out=dbg["xa"], in_=xres[:, :]),
              r=[f"xg{c}" for c in range(NCH)], w=["od_xa"], tag="st_xa")

    GSL = 108

    def prep_stage(th, stage):
        pt = th % 2
        q0 = th * TH
        xv = xres[:, :].rearrange("p (c t) -> p c t", t=T)[:, :, q0:q0 + TH]
        h2v = h2b[pt][:, :].rearrange("p (c t) -> p c t", t=TH)
        rowa, rowb = rowa_[pt], rowb_[pt]
        rowc = rowc_[pt]
        ran, rbn, rcn = f"rowa{pt}", f"rowb{pt}", f"rowc{pt}"
        wfm = wfm_[pt]
        if stage == 0:
            for c in range(NCH):
                pcx = c % 2
                P.op("dve", (lambda xv_, c_, pc_: lambda v: v.tensor_copy(
                    xb2[pc_][:, :], xv_[:, c_, :]))(xv, c, pcx),
                    r=[f"xg{c}"], w=[f"xb2{pcx}"], tag=f"cast2_{th}_{c}")
                P.op("act", (lambda xv_, c_, pc_: lambda s: s.activation(
                    xs2[pc_][:, :], xv_[:, c_, :], AF.Square))(xv, c, pcx),
                    r=[f"xg{c}"], w=[f"xs2{pcx}"], tag=f"sq2_{th}_{c}")
                P.op("pe", (lambda c_, pc_: lambda t: t.matmul(
                    psv(4), ones128[:, :], xb2[pc_][:, :],
                    start=(c_ == 0), stop=(c_ == NCH - 1)))(c, pcx),
                    r=[f"xb2{pcx}", "ones128"],
                    w=["psK4"] if c == NCH - 1 else (), war=["psK4"] if c == 0 else (),
                    tag=f"mm2_sum{th}_{c}")
                P.op("pe", (lambda c_, pc_: lambda t: t.matmul(
                    psv(5), ones128[:, :], xs2[pc_][:, :],
                    start=(c_ == 0), stop=(c_ == NCH - 1)))(c, pcx),
                    r=[f"xs2{pcx}", "ones128"],
                    w=["psK5"] if c == NCH - 1 else (), war=["psK5"] if c == 0 else (),
                    tag=f"mm2_sumsq{th}_{c}")
        elif stage == 1:
            P.op("dve", (lambda p_: lambda v: v.tensor_scalar_mul(
                mb_s[p_][:, :], psv(4), 1.0 / D))(pt),
                r=["psK4"], w=[f"mb_s{pt}"], tag=f"mean2_{th}")
            P.op("dve", (lambda p_: lambda v: v.tensor_scalar_mul(
                rb_s[p_][:, :], psv(5), 1.0 / D))(pt),
                r=["psK5"], w=[f"rb_s{pt}"], tag=f"msq2_{th}")
            P.op("dve", (lambda p_: lambda v: v.tensor_tensor(
                t1f[p_][:, :], mb_s[p_][:, :], mb_s[p_][:, :], op=ALU.mult))(pt),
                r=[f"mb_s{pt}"], w=[f"t1f{pt}"], tag=f"m2sq_{th}")
            P.op("dve", (lambda p_: lambda v: v.tensor_tensor(
                rb_s[p_][:, :], rb_s[p_][:, :], t1f[p_][:, :], op=ALU.subtract))(pt),
                r=[f"rb_s{pt}", f"t1f{pt}"], w=[f"rb_s{pt}"], tag=f"var2_{th}")
            P.op("act", (lambda p_: lambda s: s.activation(
                rb_s[p_][:, :], rb_s[p_][:, :], AF.Ln, bias=cst[:, 0:1]))(pt),
                r=[f"rb_s{pt}", "cst"], w=[f"rb_s{pt}"], tag=f"lnv2_{th}")
            P.op("act", (lambda p_: lambda s: s.activation(
                rb_s[p_][:, :], rb_s[p_][:, :], AF.Exp, scale=-0.5))(pt),
                r=[f"rb_s{pt}"], w=[f"rb_s{pt}"], tag=f"rstd2_{th}")
        elif stage in ("2a", "2b"):
            crange = range(0, 4) if stage == "2a" else range(4, 8)
            for c in crange:
                pc = c % 2
                P.op("pool", (lambda xv_, c_, pc_, p_: lambda g: g.tensor_tensor(
                    t1f[pc_][:, :], xv_[:, c_, :], mb_s[p_][:, :], op=ALU.subtract))(
                        xv, c, pc, pt),
                    r=[f"xg{c}", f"mb_s{pt}"], w=[f"t1f{pc}"], tag=f"h2n1_{th}_{c}")
                P.op("dve", (lambda pc_, p_: lambda v: v.tensor_tensor(
                    t1f[pc_][:, :], t1f[pc_][:, :], rb_s[p_][:, :], op=ALU.mult))(pc, pt),
                    r=[f"t1f{pc}", f"rb_s{pt}"], w=[f"t1f{pc}"], tag=f"h2n2_{th}_{c}")
                P.op("dve", (lambda c_, pc_: lambda v: v.tensor_scalar(
                    h2f[:, (c_ % 4) * TH:(c_ % 4 + 1) * TH], t1f[pc_][:, :],
                    c1b[:, c_:c_ + 1], sbv[:, c_:c_ + 1],
                    op0=ALU.mult, op1=ALU.add))(c, pc),
                    r=[f"t1f{pc}", "c1b", "sbv"], w=[f"h2fs_{c % 4}"],
                    tag=f"h2mod_{th}_{c}")
                P.op("act", (lambda h2v_, c_: lambda s: s.activation(
                    h2v_[:, c_, :], h2f[:, (c_ % 4) * TH:(c_ % 4 + 1) * TH],
                    AF.Identity))(h2v, c),
                    r=[f"h2fs_{c % 4}"], w=[f"h2{pt}_{c}"], tag=f"h2cast_{th}_{c}")
        elif stage in ("25a", "25b"):
            crange = range(0, 4) if stage == "25a" else range(4, 8)
            for c in crange:
                for tc in range(4):
                    P.op("pe", (lambda c_, tc_: lambda t: t.matmul(
                        psv(4)[:, tc_ * GSL + c_ * 12:tc_ * GSL + (c_ + 1) * 12],
                        h2f[:, (c_ % 4) * TH + tc_ * 128:(c_ % 4) * TH + (tc_ + 1) * 128],
                        gwt_s[:, :].rearrange("p (c e) -> p c e", e=12)[:, c_, :],
                        start=True, stop=True))(c, tc),
                        r=[f"h2fs_{c % 4}", "gwt_s"],
                        w=["psK4"] if (c == NCH - 1 and tc == 3) else (),
                        war=["psK4"] if (c == 0 and tc == 0) else (),
                        tag=f"mm_g{th}_{c}_{tc}")
            if stage == "25b":
                for tc in range(4):
                    P.op("pe", (lambda tc_: lambda t: t.matmul(
                        psv(4)[:, tc_ * GSL + 96:tc_ * GSL + 108],
                        ones_r_f[:, :], geb_s[:, :], start=True, stop=True))(tc),
                        r=["ones_r_f", "geb_s"], w=["psK4b"] if tc == 3 else (),
                        tag=f"mm_gb{th}_{tc}")
        elif stage == 3:
            for tc in range(4):
                gb = 48 * tc
                gl = glog[:, 12 * tc:12 * tc + 12]
                P.op("dve", (lambda tc_, gl_: lambda v: v.tensor_reduce(
                    gl_,
                    psv(4)[:, tc_ * GSL:tc_ * GSL + GSL].rearrange(
                        "p (k e) -> p e k", e=12),
                    axis=AX.X, op=ALU.add))(tc, gl),
                    r=["psK4", "psK4b"], w=[f"glog{tc}"], tag=f"gacc_{th}_{tc}")
                P.op("dve", (lambda gb_, gl_: lambda v: v.reduce_max(
                    gsc[:, gb_:gb_ + 1], gl_, axis=AX.X))(gb, gl),
                    r=[f"glog{tc}"], w=[f"g_m1{tc}"], tag=f"g_m1_{th}_{tc}")
                P.op("pool", (lambda gb_, gl_: lambda v: v.tensor_scalar(
                    gsc[:, gb_ + 8:gb_ + 20], gl_,
                    gsc[:, gb_:gb_ + 1], None, op0=ALU.is_equal))(gb, gl),
                    r=[f"glog{tc}", f"g_m1{tc}"], w=[f"g_mk1{tc}"],
                    tag=f"g_mk1_{th}_{tc}")
                P.op("pool", (lambda gb_: lambda v: v.tensor_scalar(
                    gsc[:, gb_ + 20:gb_ + 32], gsc[:, gb_ + 8:gb_ + 20],
                    -1e30, None, op0=ALU.mult))(gb),
                    r=[f"g_mk1{tc}"], w=[f"g_msk{tc}"], tag=f"g_msk_{th}_{tc}")
                P.op("pool", (lambda gb_, gl_: lambda v: v.tensor_tensor(
                    gsc[:, gb_ + 20:gb_ + 32], gsc[:, gb_ + 20:gb_ + 32],
                    gl_, op=ALU.add))(gb, gl),
                    r=[f"g_msk{tc}", f"glog{tc}"], w=[f"g_msk{tc}"],
                    tag=f"g_msk2_{th}_{tc}")
                P.op("dve", (lambda gb_: lambda v: v.reduce_max(
                    gsc[:, gb_ + 1:gb_ + 2], gsc[:, gb_ + 20:gb_ + 32],
                    axis=AX.X))(gb),
                    r=[f"g_msk{tc}"], w=[f"g_m2{tc}"], tag=f"g_m2_{th}_{tc}")
                P.op("pool", (lambda gb_: lambda v: v.tensor_tensor(
                    gsc[:, gb_ + 2:gb_ + 3], gsc[:, gb_:gb_ + 1],
                    gsc[:, gb_ + 1:gb_ + 2], op=ALU.subtract))(gb),
                    r=[f"g_m1{tc}", f"g_m2{tc}"], w=[f"g_d12{tc}"],
                    tag=f"g_d12_{th}_{tc}")
                P.op("act", (lambda gb_: lambda s: s.activation(
                    gsc[:, gb_ + 3:gb_ + 4], gsc[:, gb_ + 2:gb_ + 3],
                    AF.Sigmoid))(gb),
                    r=[f"g_d12{tc}"], w=[f"g_w1{tc}"], tag=f"g_w1_{th}_{tc}")
                P.op("pool", (lambda gb_: lambda v: v.tensor_scalar(
                    gsc[:, gb_ + 4:gb_ + 5], gsc[:, gb_ + 3:gb_ + 4],
                    -1.0, 1.0, op0=ALU.mult, op1=ALU.add))(gb),
                    r=[f"g_w1{tc}"], w=[f"g_w2{tc}"], tag=f"g_w2_{th}_{tc}")
                P.op("pool", (lambda gb_, gl_: lambda v: v.tensor_scalar(
                    gsc[:, gb_ + 32:gb_ + 44], gl_,
                    gsc[:, gb_ + 1:gb_ + 2], None, op0=ALU.is_equal))(gb, gl),
                    r=[f"glog{tc}", f"g_m2{tc}"], w=[f"g_mk2{tc}"],
                    tag=f"g_mk2_{th}_{tc}")
                P.op("pool", (lambda gb_: lambda v: v.tensor_scalar(
                    gsc[:, gb_ + 8:gb_ + 20], gsc[:, gb_ + 8:gb_ + 20],
                    gsc[:, gb_ + 3:gb_ + 4], None, op0=ALU.mult))(gb),
                    r=[f"g_mk1{tc}", f"g_w1{tc}"], w=[f"g_mk1{tc}"],
                    tag=f"g_a_{th}_{tc}")
                P.op("pool", (lambda gb_: lambda v: v.tensor_scalar(
                    gsc[:, gb_ + 32:gb_ + 44], gsc[:, gb_ + 32:gb_ + 44],
                    gsc[:, gb_ + 4:gb_ + 5], None, op0=ALU.mult))(gb),
                    r=[f"g_mk2{tc}", f"g_w2{tc}"], w=[f"g_mk2{tc}"],
                    tag=f"g_b_{th}_{tc}")
                P.op("pool", (lambda gb_: lambda v: v.tensor_tensor(
                    gsc[:, gb_ + 8:gb_ + 20], gsc[:, gb_ + 8:gb_ + 20],
                    gsc[:, gb_ + 32:gb_ + 44], op=ALU.add))(gb),
                    r=[f"g_mk1{tc}", f"g_mk2{tc}"], w=[f"g_mk1{tc}"],
                    tag=f"g_w12_{th}_{tc}")
                P.op("pool", (lambda tc_, gb_: lambda v: v.tensor_copy(
                    wt16[:, tc_ * 16:tc_ * 16 + 8], gsc[:, gb_ + 8:gb_ + 16]))(tc, gb),
                    r=[f"g_mk1{tc}"], w=[f"wt16_{tc}"], war=["wt16z"],
                    tag=f"g_cp_{th}_{tc}")
                P.op("dve", (lambda tc_, gb_: lambda v: v.reduce_sum(
                    wt16[:, tc_ * 16 + 8:tc_ * 16 + 9],
                    gsc[:, gb_ + 16:gb_ + 20], axis=AX.X))(tc, gb),
                    r=[f"g_mk1{tc}"], w=[f"wt16b_{tc}"], tag=f"g_wz_{th}_{tc}")
        elif stage == 35:
            for tc in range(4):
                P.op("pe", (lambda tc_: lambda t: t.transpose(
                    psv(5)[0:16, 0:128], wt16[:, tc_ * 16:(tc_ + 1) * 16],
                    eye_f[:, :]))(tc),
                    r=[f"wt16_{tc}", f"wt16b_{tc}", "wt16z", "eye_f"], w=["psK5"],
                    war=["psK5"], tag=f"g_tr_{th}_{tc}")
                P.op("act", (lambda wfm_t, tc_: lambda s: s.activation(
                    wfm_t[:, tc_ * 128:(tc_ + 1) * 128], psv(5)[0:16, 0:128],
                    AF.Identity))(wfm, tc),
                    r=["psK5"], w=[f"wfm{pt}_{tc}"], tag=f"ev_wfm_{th}_{tc}")
        elif stage == 4:
            wfmr = [f"wfm{pt}_{tc}" for tc in range(4)]
            P.op("pe", (lambda wfm_t: lambda t: t.matmul(
                psv(5), sel_s[:, 8 * 128:9 * 128], wfm_t[:, :],
                start=True, stop=True))(wfm),
                r=wfmr + ["sel_s"], w=["psK5"], war=["psK5"], tag=f"bc_wz{th}")
            P.op("act", (lambda p_: lambda s: s.activation(
                wzb_[p_][:, :], psv(5), AF.Identity))(pt),
                r=["psK5"], w=[f"wzb{pt}"], tag=f"ev_wz{th}")

    for k in (0, 1, "2a", "25a", "2b", "25b", 3, 35, 4):
        prep_stage(0, k)

    wpf_done = set()

    def emit_wdma(th, e, fq):
        if (th, e, fq) in wpf_done:
            return
        wpf_done.add((th, e, fq))
        par = fq % 2
        P.dma(st_w1, "sp", (lambda e_, fq_, par_: lambda g: g.dma_start(
            out=w1b[par_][:, :].rearrange("p (c f) -> p c f", f=512),
            in_=w1c[e_].rearrange("(c p) f -> p c f", p=128)[:, :, fq_ * 512:(fq_ + 1) * 512]))(e, fq, par),
            r=[f"w1c_{e}"], w=[f"w1b{par}"], tag=f"ld_w1_{th}_{e}_{fq}")
        P.dma(st_w2, "sp", (lambda e_, fq_, par_: lambda g: g.dma_start(
            out=w2b[par_][:, :].rearrange("p (c d) -> p c d", d=1024),
            in_=w2c[e_][fq_ * 512:(fq_ + 1) * 512, :].rearrange(
                "(c p) d -> p c d", p=128)))(e, fq, par),
            r=[f"w2c_{e}"], w=[f"w2b{par}"], tag=f"ld_w2_{th}_{e}_{fq}")

    for th in range(NTH):
        q0 = th * TH
        pt = th % 2
        xv = xres[:, :].rearrange("p (c t) -> p c t", t=T)[:, :, q0:q0 + TH]
        h2v = h2b[pt][:, :].rearrange("p (c t) -> p c t", t=TH)
        wfm = wfm_[pt]
        wfmr = [f"wfm{pt}_{tc}" for tc in range(4)]
        for c in range(NCH):
            P.op("dve", (lambda xv_, c_: lambda v: v.tensor_scalar(
                yac[:, c_ * TH:(c_ + 1) * TH], xv_[:, c_, :], invgb[:, c_:c_ + 1],
                None, op0=ALU.mult))(xv, c),
                r=[f"xg{c}", "invgb"], w=[f"yac{c}"], tag=f"y0_{th}_{c}")
            P.op("dve", (lambda h2v_, c_, p_: lambda v: v.tensor_tensor(
                ytmp[:, :], h2v_[:, c_, :], wzb_[p_][:, :], op=ALU.mult))(h2v, c, pt),
                r=[f"h2{pt}_{c}", f"wzb{pt}"], w=["t1f0"], tag=f"yz_{th}_{c}")
            P.op("pool", (lambda c_: lambda g: g.tensor_tensor(
                yac[:, c_ * TH:(c_ + 1) * TH], yac[:, c_ * TH:(c_ + 1) * TH],
                ytmp[:, :], op=ALU.add))(c),
                r=[f"yac{c}", "t1f0"], w=[f"yac{c}"], tag=f"yz2_{th}_{c}")
        for e in range(NE):
            if th + 1 < NTH and 1 <= e <= 7:
                for st_k in ((0, 1), ("2a",), ("25a", "2b"), ("25b",), (3,),
                             (35,), (4,))[e - 1]:
                    prep_stage(th + 1, st_k)
            P.op("pe", (lambda e_, wfm_t: lambda t: t.matmul(
                ps_g[:, :], sel_s[:, e_ * 128:(e_ + 1) * 128], wfm_t[:, :],
                start=True, stop=True))(e, wfm),
                r=wfmr + ["sel_s"], w=["psK6"], war=["psK6"], tag=f"bc_wb{th}_{e}")
            P.op("act", lambda s: s.activation(wbb[:, :], ps_g[:, :], AF.Identity),
                 r=["psK6"], w=["wbb"], tag=f"ev_wb{th}_{e}")

            def emit_h(fq):
                par = fq % 2
                w1v = w1b[par][:, :].rearrange("p (c f) -> p c f", f=512)
                hev = heb[par][:, :].rearrange("p (c t) -> p c t", t=TH)
                emit_wdma(th, e, fq)
                for fc in range(4):
                    pb = fc % 2
                    for dc in range(NCH):
                        P.op("pe", (lambda w1v_, h2v_, fc_, dc_, pb_: lambda t: t.matmul(
                            psv(pb_), w1v_[:, dc_, fc_ * 128:(fc_ + 1) * 128],
                            h2v_[:, dc_, :], start=(dc_ == 0), stop=(dc_ == NCH - 1)))(
                                w1v, h2v, fc, dc, pb),
                            r=[f"w1b{par}", f"h2{pt}_{dc}"],
                            w=[f"psK{pb}"] if dc == NCH - 1 else (),
                            war=[f"psK{pb}"] if dc == 0 else (),
                            tag=f"mm_h{th}_{e}_{fq}_{fc}_{dc}")
                    P.op("act", (lambda hev_, fc_, pb_: lambda s: s.activation(
                        hev_[:, fc_, :], psv(pb_), AF.Silu))(hev, fc, pb),
                        r=[f"psK{pb}"], w=[f"he{par}_{fc}"], tag=f"silu{th}_{e}_{fq}_{fc}")
                    P.op("dve", (lambda hev_, fc_: lambda v: v.tensor_tensor(
                        hev_[:, fc_, :], hev_[:, fc_, :], wbb[:, :], op=ALU.mult))(hev, fc),
                        r=[f"he{par}_{fc}", "wbb"], w=[f"he{par}_{fc}"],
                        tag=f"hsc{th}_{e}_{fq}_{fc}")

            def emit_y(fq):
                par = fq % 2
                w2v = w2b[par][:, :].rearrange("p (c d) -> p c d", d=1024)
                hev = heb[par][:, :].rearrange("p (c t) -> p c t", t=TH)
                for dc in range(NCH):
                    pb = 2 + dc % 2
                    for fc in range(4):
                        P.op("pe", (lambda w2v_, hev_, fc_, dc_, pb_: lambda t: t.matmul(
                            psv(pb_), w2v_[:, fc_, dc_ * 128:(dc_ + 1) * 128],
                            hev_[:, fc_, :], start=(fc_ == 0), stop=(fc_ == 3)))(
                                w2v, hev, fc, dc, pb),
                            r=[f"w2b{par}", f"he{par}_{fc}"],
                            w=[f"psK{pb}"] if fc == 3 else (),
                            war=[f"psK{pb}"] if fc == 0 else (),
                            tag=f"mm_y{th}_{e}_{fq}_{dc}_{fc}")
                    P.op("dve", (lambda dc_, pb_: lambda v: v.tensor_tensor(
                        yac[:, dc_ * TH:(dc_ + 1) * TH], psv(pb_),
                        yac[:, dc_ * TH:(dc_ + 1) * TH], op=ALU.add))(dc, pb),
                        r=[f"psK{pb}", f"yac{dc}"], w=[f"yac{dc}"],
                        tag=f"ya{th}_{e}_{fq}_{dc}")

            emit_h(0)
            for fq in range(4):
                if fq < 3:
                    emit_h(fq + 1)
                emit_y(fq)
            if e == NE - 1 and th + 1 < NTH:
                emit_wdma(th + 1, 0, 0)
                emit_wdma(th + 1, 0, 1)
        for c in range(NCH):
            P.op("dve", (lambda c_: lambda v: v.tensor_scalar(
                yac[:, c_ * TH:(c_ + 1) * TH], yac[:, c_ * TH:(c_ + 1) * TH],
                gbv[:, c_:c_ + 1], None, op0=ALU.mult))(c),
                r=[f"yac{c}", "gbv"], w=[f"yac{c}"], tag=f"gbm_{th}_{c}")
        P.dma(st_out, "pool", (lambda q0_: lambda e: e.dma_start(
            out=r3(out_fm)[:, :, q0_:q0_ + TH],
            in_=yac[:, :].rearrange("p (c t) -> p c t", t=TH)))(q0),
            r=[f"yac{c}" for c in range(NCH)], w=[f"od_y{th}"], tag=f"st_y{th}")

    P.set_barrier()

    outdeps = []
    if DEBUG_OUTS:
        outdeps += ["od_kt", "od_cq", "od_ctm", "od_h2", "od_wfm", "od_yini", "od_xa"]
        P.dma(st_out, "sp", lambda e: e.dma_start(out=dbg["mod"], in_=mods[:, :]),
              r=["mods", "mods_b"], w=["od_mod"], tag="st_mod")
        outdeps.append("od_mod")
    outdeps += [f"od_y{th}" for th in range(NTH)]
    P.barrier("sp", r=outdeps, tag="final")

    ctx_e.close()
    with ctx:
        block = ctx.enter_context(nc.Block())
        P.lower(nc, ctx, block)
    return nc



def make_inputs(full):
    eye = np.eye(128, dtype=np.float32)
    w1T = np.ascontiguousarray(full["L0_ffn_w1"].transpose(0, 2, 1))
    w2T = np.ascontiguousarray(full["L0_ffn_w2"].transpose(0, 2, 1))
    gwt = np.ascontiguousarray(full["L0_ffn_gate_w"].T)
    _ = full["L0_attn_b_UK"]
    common = dict(
        wdq=np.ascontiguousarray(full["L0_attn_W_DQ"]),
        bdq=np.ascontiguousarray(full["L0_attn_b_DQ"].reshape(DC, 1)),
        wdkv=np.ascontiguousarray(full["L0_attn_W_DKV"]),
        bdkv=np.ascontiguousarray(full["L0_attn_b_DKV"].reshape(DC, 1)),
        bdkvr=np.ascontiguousarray(full["L0_attn_b_DKV"].reshape(1, DC)),
        wkr=np.ascontiguousarray(full["L0_attn_W_KR"]),
        bkr=np.ascontiguousarray(full["L0_attn_b_KR"].reshape(DR, 1)),
        wuq=np.ascontiguousarray(full["L0_attn_W_UQ"]),
        buq=np.ascontiguousarray(full["L0_attn_b_UQ"].reshape(H, 128).T),
        wuk=np.ascontiguousarray(full["L0_attn_W_UK"]),
        wuv=np.ascontiguousarray(full["L0_attn_W_UV"]),
        buv=np.ascontiguousarray(full["L0_attn_b_UV"].reshape(H, 128).T),
        wqr=np.ascontiguousarray(full["L0_attn_W_QR"]),
        bqr=np.ascontiguousarray(full["L0_attn_b_QR"].reshape(H, DR).T),
        wo=np.ascontiguousarray(full["L0_attn_W_O"]),
        bo=np.ascontiguousarray(full["L0_attn_b_O"].reshape(NCH, 128).T),
        gwt=gwt,
        geb=np.ascontiguousarray(full["L0_ffn_expert_bias"].reshape(1, 12)),
        w1t=w1T, w2t=w2T,
        wmod=np.ascontiguousarray(full["L0_wmod"]),
        eye=eye,
        sel=np.ascontiguousarray(
            (np.arange(16)[:, None] == (np.arange(9 * 128) // 128)[None, :])
            .astype(np.float32)),
    )
    in_maps = []
    for core in range(8):
        b, half = core // 2, core % 2
        xT = np.ascontiguousarray(full["L0_attn_h_t"][b].T)
        if half == 1:
            xT = np.ascontiguousarray(np.roll(xT, -T, axis=1))
        m = dict(common)
        m["xfm"] = xT
        m["cfm"] = np.ascontiguousarray(full["L0_c"][b].reshape(NCH, 128).T)
        in_maps.append(m)
    return in_maps


def kernel(**inputs):
    nc = build_program()
    in_maps = make_inputs(inputs)
    res = run_bass_kernel_spmd(nc, in_maps, core_ids=list(range(8)))
    out = np.empty((B, S, D), dtype=np.float32)
    for core in range(8):
        b, half = core // 2, core % 2
        out[b, half * T:(half + 1) * T, :] = res.results[core]["out_fm"].T
    return out


# version 30
# speedup vs baseline: 1.0302x; 1.0302x over previous
import math
import numpy as np
from contextlib import ExitStack

import concourse.bass as bass
import concourse.mybir as mybir
from concourse.bass_utils import run_bass_kernel_spmd

F32 = mybir.dt.float32
BF16 = mybir.dt.bfloat16
AF = mybir.ActivationFunctionType
ALU = mybir.AluOpType
AX = mybir.AxisListType

B, S, D = 4, 4096, 1024
T = 2048
NCH = D // 128
H, DK, DR, DC = 8, 128, 32, 32
NE, DFF = 8, 2048
SCALE = 1.0 / math.sqrt(DK + DR)
EPS = 1e-5
TH = 512
NTH = T // TH
DEBUG_OUTS = False
PHASES = "AB"

ENGINES = ("pe", "act", "dve", "pool", "sp")


class _Op:
    __slots__ = ("eng", "fn", "deps", "inc", "val", "dma_sem", "dma_val", "tag")

    def __init__(self, eng, fn, tag=""):
        self.eng = eng
        self.fn = fn
        self.deps = []
        self.inc = False
        self.val = None
        self.dma_sem = None
        self.dma_val = None
        self.tag = tag


class Stream:

    def __init__(self, name, n_sems):
        self.name = name
        self.n = n_sems
        self.uses = [0] * n_sems
        self.last_op = [None] * n_sems
        self.total = 0
        self.sems = None


class Prog:
    def __init__(self):
        self.ops = {e: [] for e in ENGINES}
        self.bufs = {}
        self.streams = []

    def stream(self, name, n_sems):
        s = Stream(name, n_sems)
        self.streams.append(s)
        return s

    def _dep(self, op, d):
        if d is None or d is op:
            return
        if d.eng == "pe" and op.eng == "pe" and d.dma_sem is None and op.dma_sem is None:
            return
        if d not in op.deps:
            op.deps.append(d)
            d.inc = True

    def _track(self, op, r, w, war):
        for name in r:
            ent = self.bufs.get(name)
            if ent is not None:
                self._dep(op, ent[0])
        for name in tuple(w) + tuple(war):
            ent = self.bufs.get(name)
            if ent is not None:
                self._dep(op, ent[0])
                for rd in ent[1]:
                    self._dep(op, rd)
        for name in r:
            ent = self.bufs.setdefault(name, [None, []])
            if op not in ent[1]:
                ent[1].append(op)
        for name in w:
            self.bufs[name] = [op, []]

    def op(self, eng, fn, r=(), w=(), war=(), tag=""):
        o = _Op(eng, fn, tag)
        self._track(o, r, w, war)
        for b in getattr(self, "_barrier", ()):
            self._dep(o, b)
        self.ops[eng].append(o)
        return o

    def dma(self, stream, eng, fn, r=(), w=(), war=(), tag=""):
        o = _Op(eng, fn, tag)
        idx = stream.total % stream.n
        stream.total += 1
        prev = stream.last_op[idx]
        if prev is not None:
            self._dep(o, prev)
        self._track(o, r, w, war)
        for b in getattr(self, "_barrier", ()):
            self._dep(o, b)
        stream.uses[idx] += 1
        o.dma_sem = (stream, idx)
        o.dma_val = 16 * stream.uses[idx]
        stream.last_op[idx] = o
        self.ops[eng].append(o)
        return o

    def barrier(self, eng, r, tag="barrier"):
        return self.op(eng, None, r=r, tag=tag)

    def frontier(self):
        f = []
        for e in ENGINES:
            for o in reversed(self.ops[e]):
                if o.fn is not None and o.dma_sem is None:
                    f.append(o)
                    break
        for st_ in self.streams:
            for o in st_.last_op:
                if o is not None:
                    f.append(o)
        return f

    def set_barrier(self):
        self._barrier = self.frontier()

    def lower(self, nc, ctx, block):
        E = ctx.enter_context
        esem = {e: E(nc.semaphore(f"sem_{e}")) for e in ENGINES}
        for s in self.streams:
            s.sems = [E(nc.semaphore(f"sem_{s.name}{i}")) for i in range(s.n)]
        for e in ENGINES:
            cnt = 0
            for o in self.ops[e]:
                if o.dma_sem is None and o.inc:
                    cnt += 1
                    o.val = cnt

        def wait_list(o):
            tgts = {}
            for d in o.deps:
                if d.dma_sem is not None:
                    sem = d.dma_sem[0].sems[d.dma_sem[1]]
                    v = d.dma_val
                else:
                    assert d.inc and d.val is not None, (d.tag, o.tag)
                    sem = esem[d.eng]
                    v = d.val
                k = id(sem)
                if k not in tgts or tgts[k][1] < v:
                    tgts[k] = (sem, v)
            return list(tgts.values())

        def run_engine(engobj, e):
            water = {}
            if e == "sp":
                self._water_sp = water
            for o in self.ops[e]:
                for sem, v in wait_list(o):
                    k = id(sem)
                    if water.get(k, 0) < v:
                        engobj.wait_ge(sem, v)
                        water[k] = v
                if o.fn is None:
                    assert not o.inc, o.tag
                    continue
                ins = o.fn(engobj)
                if o.dma_sem is not None:
                    sem = o.dma_sem[0].sems[o.dma_sem[1]]
                    ins.then_inc(sem, 16)
                elif o.inc:
                    ins.then_inc(esem[e], 1)

        @block.tensor
        def _(t):
            run_engine(t, "pe")

        @block.scalar
        def _(s):
            run_engine(s, "act")

        @block.vector
        def _(v):
            run_engine(v, "dve")

        @block.gpsimd
        def _(g):
            run_engine(g, "pool")

        @block.sync
        def _(sy):
            run_engine(sy, "sp")
            water = self._water_sp
            for st_ in self.streams:
                for j in range(st_.n):
                    if st_.uses[j]:
                        v = 16 * st_.uses[j]
                        k = id(st_.sems[j])
                        if water.get(k, 0) < v:
                            sy.wait_ge(st_.sems[j], v)




def dbg_pre_free(P, nc, dbg, kt, cqs, ckvtm, st_out):
    if not DEBUG_OUTS:
        return
    P.dma(st_out, "sp", lambda e: e.dma_start(out=dbg["kt"], in_=kt[:, :]),
          r=[f"kt_c{i}" for i in range(8)] + [f"kt_r{i}" for i in range(8)]
            + ["kt_ones"], w=["od_kt"], tag="st_kt")
    P.dma(st_out, "sp", lambda e: e.dma_start(out=dbg["cq"], in_=cqs[:, :]),
          r=[f"cq{i}" for i in range(4)], w=["od_cq"], tag="st_cq")
    P.dma(st_out, "sp", lambda e: e.dma_start(out=dbg["ctm"], in_=ckvtm[:, :]),
          r=[f"ckvtm{k}" for k in range(32)] + ["ckvtm_ones"], w=["od_ctm"],
          tag="st_ctm")


def build_program():
    nc = bass.Bass("TRN2", target_bir_lowering=False)
    P = Prog()

    def din(name, shape, dt=F32):
        return nc.dram_tensor(name, list(shape), dt, kind="ExternalInput").ap()

    xfm = din("xfm", (D, S))
    cfm = din("cfm", (128, NCH))
    wmod = din("wmod", (D, 6 * D))
    wdq = din("wdq", (D, DC))
    bdq = din("bdq", (DC, 1))
    wdkv = din("wdkv", (D, DC))
    bdkv = din("bdkv", (DC, 1))
    bdkvr = din("bdkvr", (1, DC))
    wkr = din("wkr", (D, DR))
    bkr = din("bkr", (DR, 1))
    wuq = din("wuq", (DC, H * DK))
    buq = din("buq", (128, H))
    wuk = din("wuk", (DC, H * DK))
    wuv = din("wuv", (DC, H * DK))
    buv = din("buv", (128, H))
    wqr = din("wqr", (DC, H * DR))
    bqr = din("bqr", (DR, H))
    wo = din("wo", (D, D))
    bo = din("bo", (128, NCH))
    gwt = din("gwt", (D, 12))
    geb = din("geb", (1, 12))
    w1t = din("w1t", (NE, D, DFF))
    w2t = din("w2t", (NE, DFF, D))
    eye = din("eye", (128, 128))
    sel = din("sel", (16, 9 * 128))

    out_fm = nc.dram_tensor("out_fm", [D, T], F32, kind="ExternalOutput").ap()
    w1c = nc.dram_tensor("w1c", [NE, D, DFF], BF16, kind="Internal").ap()
    w2c = nc.dram_tensor("w2c", [NE, DFF, D], BF16, kind="Internal").ap()
    dbg = {}
    if DEBUG_OUTS:
        dbg["kt"] = nc.dram_tensor("dbg_kt", [65, S], BF16, kind="ExternalOutput").ap()
        dbg["cq"] = nc.dram_tensor("dbg_cq", [DC, T], BF16, kind="ExternalOutput").ap()
        dbg["ctm"] = nc.dram_tensor("dbg_ctm", [128, 34 * (S // 128)], BF16,
                                    kind="ExternalOutput").ap()
        dbg["mod"] = nc.dram_tensor("dbg_mod", [128, 48], F32, kind="ExternalOutput").ap()
        dbg["h2"] = nc.dram_tensor("dbg_h2", [128, NCH * TH], BF16, kind="ExternalOutput").ap()
        dbg["wfm"] = nc.dram_tensor("dbg_wfm", [16, TH], F32, kind="ExternalOutput").ap()
        dbg["yini"] = nc.dram_tensor("dbg_yini", [128, NCH * TH], F32, kind="ExternalOutput").ap()
        dbg["xa"] = nc.dram_tensor("dbg_xa", [128, NCH * T], F32, kind="ExternalOutput").ap()

    ctx = ExitStack()
    E = ctx.enter_context

    xres = E(nc.sbuf_tensor("xres", [128, NCH * T], F32))
    mods = E(nc.sbuf_tensor("mods", [128, 48], F32))
    c1a = E(nc.sbuf_tensor("c1a", [128, NCH], F32))
    sav = E(nc.sbuf_tensor("sav", [128, NCH], F32))
    c1b = E(nc.sbuf_tensor("c1b", [128, NCH], F32))
    sbv = E(nc.sbuf_tensor("sbv", [128, NCH], F32))
    gav = E(nc.sbuf_tensor("gav", [128, NCH], F32))
    gbv = E(nc.sbuf_tensor("gbv", [128, NCH], F32))
    invgb = E(nc.sbuf_tensor("invgb", [128, NCH], F32))
    gab = E(nc.sbuf_tensor("gab", [128, NCH], F32))
    battn = E(nc.sbuf_tensor("battn", [128, NCH], F32))
    gtmp = E(nc.sbuf_tensor("gtmp", [128, 24], F32))
    cst = E(nc.sbuf_tensor("cst", [128, 2], F32))
    ones_c_f = E(nc.sbuf_tensor("ones_c_f", [128, 1], F32))
    ones_c_b = E(nc.sbuf_tensor("ones_c_b", [128, 1], BF16))
    ones128 = E(nc.sbuf_tensor("ones128", [128, 128], BF16))
    ones_r_f = E(nc.sbuf_tensor("ones_r_f", [1, 128], F32))
    ones_r_b = E(nc.sbuf_tensor("ones_r_b", [1, 128], BF16))
    eye_f = E(nc.sbuf_tensor("eye_f", [128, 128], F32))
    eye_b = E(nc.sbuf_tensor("eye_b", [128, 128], BF16))
    wdq_s = E(nc.sbuf_tensor("wdq_s", [128, NCH * DC], BF16))
    wdkv_s = E(nc.sbuf_tensor("wdkv_s", [128, NCH * DC], BF16))
    wkr_s = E(nc.sbuf_tensor("wkr_s", [128, NCH * DR], BF16))
    bdq_s = E(nc.sbuf_tensor("bdq_s", [DC, 1], F32))
    bdkv_s = E(nc.sbuf_tensor("bdkv_s", [DC, 1], F32))
    bdkvr_s = E(nc.sbuf_tensor("bdkvr_s", [1, DC], BF16))
    bkr_s = E(nc.sbuf_tensor("bkr_s", [DR, 1], F32))
    wqr_s = E(nc.sbuf_tensor("wqr_s", [DC, H * DR], BF16))
    buq_b = E(nc.sbuf_tensor("buq_b", [128, H], BF16))
    buv_b = E(nc.sbuf_tensor("buv_b", [128, H], BF16))
    bqr_sc = E(nc.sbuf_tensor("bqr_sc", [DR, H], F32))
    bo_s = E(nc.sbuf_tensor("bo_s", [128, NCH], F32))
    cfm_s = E(nc.sbuf_tensor("cfm_s", [128, NCH], F32))
    silu_b = E(nc.sbuf_tensor("silu_b", [128, NCH], BF16))
    gwt_s = E(nc.sbuf_tensor("gwt_s", [128, NCH * 12], F32))
    geb_s = E(nc.sbuf_tensor("geb_s", [1, 12], F32))
    bqr_s = E(nc.sbuf_tensor("bqr_s", [DR, H], F32))
    wuqT = E(nc.sbuf_tensor("wuqT", [128, H * DC], BF16))
    wukT = E(nc.sbuf_tensor("wukT", [128, H * DC], BF16))
    wuvT = E(nc.sbuf_tensor("wuvT", [128, H * DC], BF16))
    u1s = E(nc.sbuf_tensor("u1s", [DC, H], F32))
    t1f = [E(nc.sbuf_tensor(f"t1f{i}", [128, 512], F32)) for i in range(2)]
    rowa_ = [E(nc.sbuf_tensor(f"rowa{i}", [1, 512], F32)) for i in range(2)]
    rowb_ = [E(nc.sbuf_tensor(f"rowb{i}", [1, 512], F32)) for i in range(2)]
    rowc_ = [E(nc.sbuf_tensor(f"rowc{i}", [1, 512], F32)) for i in range(2)]
    rb_s = [E(nc.sbuf_tensor(f"rb_s{i}", [128, 512], F32)) for i in range(2)]
    mb_s = [E(nc.sbuf_tensor(f"mb_s{i}", [128, 512], F32)) for i in range(2)]
    mxk = E(nc.sbuf_tensor("mxk", [1, 20], F32))
    negm = E(nc.sbuf_tensor("negm", [128, 1], F32))
    negm2 = E(nc.sbuf_tensor("negm2", [128, 1], F32))
    sqk = E(nc.sbuf_tensor("sqk", [64, 2048], F32))
    sel_s = E(nc.sbuf_tensor("sel_s", [16, 9 * 128], F32))
    ctx_bc = ExitStack()
    Ebc = ctx_bc.enter_context
    cqs2 = None
    kt2 = None
    cqs = Ebc(nc.sbuf_tensor("cqs", [128, T], BF16))
    bst = Ebc(nc.sbuf_tensor("bst", [128, 4 * 128], BF16))
    kt = Ebc(nc.sbuf_tensor("kt", [128, S], BF16))
    ckvtm = Ebc(nc.sbuf_tensor("ckvtm", [128, 34 * (S // 128)], BF16))
    m2st = [Ebc(nc.sbuf_tensor(f"m2st{i}", [64, D], BF16)) for i in range(4)]
    ctx_ab = ExitStack()
    Eab = ctx_ab.enter_context
    wo_s = Eab(nc.sbuf_tensor("wo_s", [128, NCH * D], BF16))
    wuf = Eab(nc.sbuf_tensor("wuf", [DC, H * DK], F32))
    wmod_t = [Eab(nc.sbuf_tensor(f"wmod_t{i}", [128, NCH * 512], BF16)) for i in range(2)]
    xoth = [Eab(nc.sbuf_tensor("xoth0", [128, NCH * 512], F32))] * 2
    xbf = [Eab(nc.sbuf_tensor(f"xbf{i}", [128, 512], BF16)) for i in range(2)]
    xsq = [Eab(nc.sbuf_tensor(f"xsq{i}", [128, 512], BF16)) for i in range(2)]
    hbf = [Eab(nc.sbuf_tensor(f"hbf{i}", [128, NCH * 512], BF16)) for i in range(2)]

    psall = E(nc.psum_tensor("psall", [128, 8 * 512], F32))

    def psv(k, n=1):
        return psall[:, k * 512:(k + n) * 512]

    ps_mod = psv(0)
    ps_a = psv(1)
    ps_b = psv(2)
    ps_c = psv(3)
    ps_d = psv(4)
    ps_e = psv(5)
    ps_f = psv(6)
    ps_g = psv(6)

    st_small = P.stream("sm", 2)
    st_gp = P.stream("gp", 3)
    st_gp2 = P.stream("g2", 3)
    st_w1 = P.stream("w1", 3)
    st_w2 = P.stream("w2", 3)
    st_wb = P.stream("wb", 2)
    st_pp = P.stream("pp", 2)
    st_x = P.stream("x", 3)
    st_out = P.stream("o", 2)

    def r3(ap, p=128):
        return ap.rearrange("(c p) m -> p c m", p=p)

    P.op("dve", lambda v: v.memset(cst[:, 0:1], EPS), w=["cst"])
    P.op("dve", lambda v: v.memset(cst[:, 1:2], SCALE), w=["cst_s"])
    P.op("dve", lambda v: v.memset(ones_c_f[:, :], 1.0), w=["ones_c_f"])
    P.op("dve", lambda v: v.memset(ones_c_b[:, :], 1.0), w=["ones_c_b"])
    P.op("dve", lambda v: v.memset(ones128[:, :], 1.0), w=["ones128"])
    P.op("dve", lambda v: v.memset(ones_r_f[:, :], 1.0), w=["ones_r_f"])
    P.op("dve", lambda v: v.memset(ones_r_b[:, :], 1.0), w=["ones_r_b"])
    P.op("dve", lambda v: v.memset(
        ckvtm[:, :].rearrange("p (n c) -> p n c", c=34)[:, :, 32:34], 1.0),
        w=["ckvtm_ones"])
    P.op("dve", lambda v: v.memset(kt[64:128, :], 0.0), w=["ktz_"], tag="ktz_pre")
    P.op("dve", lambda v: v.memset(cqs[32:64, :], 0.0), w=["cqz"], tag="cqz1")
    P.op("dve", lambda v: v.memset(cqs[64:128, :], 0.0), r=["cqz"], w=["cqz"],
         tag="cqz2")
    P.op("dve", lambda v: v.memset(bst[32:64, :], 0.0), w=["bstz"], tag="bstz1")
    P.op("dve", lambda v: v.memset(bst[64:128, :], 0.0), r=["bstz"], w=["bstz"],
         tag="bstz2")
    P.op("dve", lambda v: v.memset(kt[64:65, :], 1.0), r=["ktz_"], w=["kt_ones"])

    for dst, src, nm in [
        (bdq_s[:, :], bdq, "bdq_s"), (bdkv_s[:, :], bdkv, "bdkv_s"),
        (bkr_s[:, :], bkr, "bkr_s"), (bqr_s[:, :], bqr, "bqr_s"),
        (bo_s[:, :], bo, "bo_s"), (cfm_s[:, :], cfm, "cfm_s"),
        (geb_s[:, :], geb, "geb_s"), (eye_f[:, :], eye, "eye_f"),
        (gwt_s[:, :].rearrange("p (c e) -> p c e", e=12), r3(gwt), "gwt_s"),
        (sel_s[:, :], sel, "sel_s"),
    ]:
        P.dma(st_small, "sp", (lambda d, s: lambda e: e.dma_start(out=d, in_=s))(dst, src),
              w=[nm], tag="ld_" + nm)

    for dst, src, nm in [
        (wdq_s[:, :].rearrange("p (c m) -> p c m", m=DC), r3(wdq), "wdq_s"),
        (wdkv_s[:, :].rearrange("p (c m) -> p c m", m=DC), r3(wdkv), "wdkv_s"),
        (wkr_s[:, :].rearrange("p (c m) -> p c m", m=DR), r3(wkr), "wkr_s"),
        (wqr_s[:, :], wqr, "wqr_s"),
        (buq_b[:, :], buq, "buq_b"), (buv_b[:, :], buv, "buv_b"),
        (bdkvr_s[:, :], bdkvr, "bdkvr_s"), (eye_b[:, :], eye, "eye_b"),
        (wo_s[:, :].rearrange("p (c n) -> p c n", n=D), r3(wo), "wo_s"),
    ]:
        P.dma(st_gp, "pool", (lambda d, s: lambda e: e.dma_start(out=d, in_=s))(dst, src),
              w=[nm], tag="cld_" + nm)

    P.op("act", lambda s: s.activation(silu_b[:, :], cfm_s[:, :], AF.Silu),
         r=["cfm_s"], w=["silu_b"], tag="siluc")

    for sl in range(12):
        par = sl % 2
        wt = wmod_t[par]
        src = wmod.rearrange("(c p) (s j) -> p c s j", p=128, j=512)[:, :, sl, :]
        P.dma(st_gp, "pool",
              (lambda d, s_: lambda e: e.dma_start(out=d, in_=s_))(
                  wt[:, :].rearrange("p (c j) -> p c j", j=512), src),
              w=[f"wmod_t{par}"], tag=f"ld_wmod{sl}")
        for mj in range(4):
            m = sl * 4 + mj
            for c in range(NCH):
                P.op("pe", (lambda wt_, c_, mj_, m_: lambda t: t.matmul(
                    ps_mod[:, m_:m_ + 1],
                    wt_[:, :].rearrange("p (c j) -> p c j", j=512)[:, c_, mj_ * 128:(mj_ + 1) * 128],
                    silu_b[:, c_:c_ + 1],
                    start=(c_ == 0), stop=(c_ == NCH - 1)))(wt, c, mj, m),
                    r=[f"wmod_t{par}", "silu_b"],
                    w=["psK0"] if (c == NCH - 1 and m == 47) else
                      (["psK0a"] if (c == NCH - 1 and m == 23) else ()),
                    war=["psK0"] if (c == 0 and m == 0) else (),
                    tag=f"mm_mod{m}_{c}")
        if sl == 5:
            P.op("act", lambda s: s.activation(mods[:, 0:24], ps_mod[:, 0:24],
                                               AF.Identity),
                 r=["psK0a"], w=["mods"], tag="ev_mod_a")
    P.op("act", lambda s: s.activation(mods[:, 24:48], ps_mod[:, 24:48],
                                       AF.Identity),
         r=["psK0"], w=["mods_b"], tag="ev_mod")

    P.op("dve", lambda v: v.tensor_scalar_add(c1a[:, :], mods[:, 8:16], 1.0),
         r=["mods"], w=["c1a"], tag="c1a")
    P.op("dve", lambda v: v.tensor_copy(sav[:, :], mods[:, 0:8]), r=["mods"], w=["sav"],
         tag="sav")
    P.op("dve", lambda v: v.tensor_scalar_add(c1b[:, :], mods[:, 32:40], 1.0),
         r=["mods_b"], w=["c1b"], tag="c1b")
    P.op("dve", lambda v: v.tensor_copy(sbv[:, :], mods[:, 24:32]), r=["mods_b"],
         w=["sbv"], tag="sbv")

    def clamp_gate(dst, lo, nm, src="mods"):
        P.op("dve", (lambda lo_: lambda v: v.tensor_scalar(
            gtmp[:, 0:8], mods[:, lo_:lo_ + 8], 0.0, None, op0=ALU.is_ge))(lo),
            r=[src], w=["g_ge"], tag=nm + "_ge")
        P.op("dve", lambda v: v.tensor_scalar(gtmp[:, 8:16], gtmp[:, 0:8],
                                              2.0, -1.0, op0=ALU.mult, op1=ALU.add),
             r=["g_ge"], w=["g_sgn"], tag=nm + "_sgn")
        P.op("act", (lambda lo_: lambda s: s.activation(
            gtmp[:, 16:24], mods[:, lo_:lo_ + 8], AF.Abs))(lo),
            r=[src], w=["g_mag"], tag=nm + "_abs")
        P.op("dve", lambda v: v.tensor_scalar(gtmp[:, 16:24], gtmp[:, 16:24],
                                              0.25, None, op0=ALU.max),
             r=["g_mag"], w=["g_mag"], tag=nm + "_mag")
        P.op("dve", (lambda dst_: lambda v: v.tensor_tensor(
            dst_[:, :], gtmp[:, 8:16], gtmp[:, 16:24], op=ALU.mult))(dst),
            r=["g_sgn", "g_mag"], w=[nm], tag=nm + "_mul")

    clamp_gate(gav, 16, "gav")
    clamp_gate(gbv, 40, "gbv", src="mods_b")
    P.op("dve", lambda v: v.reciprocal(invgb[:, :], gbv[:, :]), r=["gbv"], w=["invgb"],
         tag="invgb")
    P.op("dve", lambda v: v.tensor_scalar_mul(bqr_sc[:, :], bqr_s[:, :], SCALE),
         r=["bqr_s"], w=["bqr_sc"], tag="bqr_sc")

    for srcdram, dstT, dn in [(wuq, wuqT, "wuqT"), (wuk, wukT, "wukT"),
                              (wuv, wuvT, "wuvT")]:
        P.dma(st_small, "sp", (lambda s_: lambda e: e.dma_start(
            out=wuf[:, :], in_=s_))(srcdram), w=["wuf"], tag=f"ld_wuf_{dn}")
        for h in range(H):
            P.op("pe", (lambda h_: lambda t: t.transpose(
                psv(7)[:, 0:DC], wuf[:, h_ * DK:(h_ + 1) * DK], eye_f[0:DC, 0:DC]))(h),
                r=["wuf", "eye_f"], w=["psK7"], war=["psK7"],
                tag=f"tr_{dn}{h}")
            P.op("act", (lambda dstT_, h_: lambda s: s.activation(
                dstT_[:, h_ * DC:(h_ + 1) * DC], psv(7)[:, 0:DC], AF.Identity))(dstT, h),
                r=["psK7"], w=[f"{dn}{h}"], tag=f"ev_{dn}{h}")
    for h in range(H):
        P.op("pe", (lambda h_: lambda t: t.matmul(
            ps_a[0:DC, 0:DC], wuqT[:, h_ * DC:(h_ + 1) * DC],
            wukT[:, h_ * DC:(h_ + 1) * DC], start=True, stop=True))(h),
            r=[f"wuqT{h}", f"wukT{h}"], w=["psK1"], war=["psK1"], tag=f"mm_A{h}")
        P.op("act", (lambda h_: lambda s: s.activation(
            bst[0:DC, (h_ // 2) * 128 + (h_ % 2) * DC:
                (h_ // 2) * 128 + (h_ % 2) * DC + DC],
            ps_a[0:DC, 0:DC], AF.Identity))(h),
            r=["psK1"], w=[f"bstA{h}"], tag=f"ev_A{h}")
        P.op("pe", (lambda h_: lambda t: t.matmul(
            ps_b[0:DC, 0:1], wukT[:, h_ * DC:(h_ + 1) * DC],
            buq_b[:, h_:h_ + 1], start=True, stop=True))(h),
            r=[f"wukT{h}", "buq_b"], w=["psK2"], war=["psK2"], tag=f"mm_u1{h}")
        P.op("act", (lambda h_: lambda s: s.activation(
            u1s[:, h_:h_ + 1], ps_b[0:DC, 0:1], AF.Identity,
            scale=cst[0:DC, 1:2]))(h),
            r=["psK2", "cst_s"], w=[f"u1s{h}"], tag=f"ev_u1{h}")
        stk, row = h // 2, (h % 2) * 32
        for half in range(2):
            P.op("pe", (lambda h_, half_: lambda t: t.matmul(
                ps_c[0:DC, :], wuvT[:, h_ * DC:(h_ + 1) * DC],
                wo_s[:, :].rearrange("p (c n) -> p c n", n=D)[:, h_, half_ * 512:(half_ + 1) * 512],
                start=True, stop=True))(h, half),
                r=[f"wuvT{h}", "wo_s"], w=["psK3"], war=["psK3"], tag=f"mm_M2{h}_{half}")
            P.op("act", (lambda h_, half_, stk_, row_: lambda s: s.activation(
                m2st[stk_][row_:row_ + 32, half_ * 512:(half_ + 1) * 512],
                ps_c[0:DC, :], AF.Identity))(h, half, stk, row),
                r=["psK3"], w=[f"m2st{stk}_{row}_{half}"], tag=f"ev_M2{h}_{half}")
    for c in range(NCH):
        for h in range(H):
            P.op("pe", (lambda c_, h_: lambda t: t.matmul(
                ps_d[:, c_:c_ + 1],
                wo_s[:, :].rearrange("p (c n) -> p c n", n=D)[:, h_, c_ * 128:(c_ + 1) * 128],
                buv_b[:, h_:h_ + 1], start=(h_ == 0), stop=(h_ == H - 1)))(c, h),
                r=["wo_s", "buv_b"],
                w=["psK4"] if (h == H - 1 and c == NCH - 1) else (),
                war=["psK4"] if (h == 0 and c == 0) else (),
                tag=f"mm_bv{c}_{h}")
    P.op("dve", lambda v: v.tensor_tensor(battn[:, :], ps_d[:, 0:NCH], bo_s[:, :],
                                          op=ALU.add),
         r=["psK4", "bo_s"], w=["battn"], tag="battn")
    P.op("dve", lambda v: v.tensor_tensor(gab[:, :], gav[:, :], battn[:, :], op=ALU.mult),
         r=["gav", "battn"], w=["gab"], tag="gab")
    for h in range(H):
        P.op("dve", (lambda h_: lambda v: v.tensor_copy(
            bst[0:DC, (h_ // 2) * 128 + 64 + (h_ % 2) * DR:
                (h_ // 2) * 128 + 64 + (h_ % 2) * DR + DR],
            wqr_s[:, h_ * DR:(h_ + 1) * DR]))(h),
            r=["wqr_s"], w=[f"bstR{h}"], tag=f"bst_r{h}")

    for i in range(8):
        own = i < 4
        par = i % 2
        t0 = i * 512
        if own:
            xview = xres[:, :].rearrange("p (c t) -> p c t", t=T)[:, :, t0:t0 + 512]
            xbufn = f"xres_t{i}"
        else:
            xview = xoth[par][:, :].rearrange("p (c t) -> p c t", t=512)
            xbufn = "xoth"
        P.dma(st_x, "sp", (lambda d, s_: lambda e: e.dma_start(out=d, in_=s_))(
            xview, r3(xfm)[:, :, t0:t0 + 512]), w=[xbufn], tag=f"ld_x{i}")

        for c in range(NCH):
            pc = (i * NCH + c) % 2
            P.op("dve", (lambda xr, c_, pc_: lambda v: v.tensor_copy(
                xbf[pc_][:, :], xr[:, c_, :]))(xview, c, pc),
                r=[xbufn], w=[f"xbf{pc}"], tag=f"cast_x{i}_{c}")
            P.op("act", (lambda xr, c_, pc_: lambda s: s.activation(
                xsq[pc_][:, :], xr[:, c_, :], AF.Square))(xview, c, pc),
                r=[xbufn], w=[f"xsq{pc}"], tag=f"sq_x{i}_{c}")
            P.op("pe", (lambda c_, pc_: lambda t: t.matmul(
                ps_e[:, :], ones128[:, :], xbf[pc_][:, :],
                start=(c_ == 0), stop=(c_ == NCH - 1)))(c, pc),
                r=[f"xbf{pc}", "ones128"],
                w=["psK5"] if c == NCH - 1 else (), war=["psK5"] if c == 0 else (),
                tag=f"mm_sum{i}_{c}")
            P.op("pe", (lambda c_, pc_: lambda t: t.matmul(
                ps_f[:, :], ones128[:, :], xsq[pc_][:, :],
                start=(c_ == 0), stop=(c_ == NCH - 1)))(c, pc),
                r=[f"xsq{pc}", "ones128"],
                w=["psK6"] if c == NCH - 1 else (), war=["psK6"] if c == 0 else (),
                tag=f"mm_sumsq{i}_{c}")
        P.op("dve", (lambda par_: lambda v: v.tensor_scalar_mul(
            mb_s[par_][:, :], ps_e[:, :], 1.0 / D))(par),
            r=["psK5"], w=[f"mb_s{par}"], tag=f"mean{i}")
        P.op("dve", (lambda par_: lambda v: v.tensor_scalar_mul(
            rb_s[par_][:, :], ps_f[:, :], 1.0 / D))(par),
            r=["psK6"], w=[f"rb_s{par}"], tag=f"msq{i}")
        P.op("dve", (lambda par_: lambda v: v.tensor_tensor(
            t1f[par_][:, :], mb_s[par_][:, :], mb_s[par_][:, :], op=ALU.mult))(par),
            r=[f"mb_s{par}"], w=[f"t1f{par}"], tag=f"m2_{i}")
        P.op("dve", (lambda par_: lambda v: v.tensor_tensor(
            rb_s[par_][:, :], rb_s[par_][:, :], t1f[par_][:, :], op=ALU.subtract))(par),
            r=[f"rb_s{par}", f"t1f{par}"], w=[f"rb_s{par}"], tag=f"var_{i}")
        P.op("act", (lambda par_: lambda s: s.activation(
            rb_s[par_][:, :], rb_s[par_][:, :], AF.Ln, bias=cst[:, 0:1]))(par),
            r=[f"rb_s{par}", "cst"], w=[f"rb_s{par}"], tag=f"lnv_{i}")
        P.op("act", (lambda par_: lambda s: s.activation(
            rb_s[par_][:, :], rb_s[par_][:, :], AF.Exp, scale=-0.5))(par),
            r=[f"rb_s{par}"], w=[f"rb_s{par}"], tag=f"rstd_{i}")
        for c in range(NCH):
            pc = (i * NCH + c) % 2
            P.op("pool", (lambda xr, c_, pc_, par_: lambda g: g.tensor_tensor(
                t1f[pc_][:, :], xr[:, c_, :], mb_s[par_][:, :], op=ALU.subtract))(
                    xview, c, pc, par),
                r=[xbufn, f"mb_s{par}"], w=[f"t1f{pc}"], tag=f"hn1_{i}_{c}")
            P.op("dve", (lambda pc_, par_: lambda v: v.tensor_tensor(
                t1f[pc_][:, :], t1f[pc_][:, :], rb_s[par_][:, :], op=ALU.mult))(pc, par),
                r=[f"t1f{pc}", f"rb_s{par}"], w=[f"t1f{pc}"], tag=f"hn2_{i}_{c}")
            P.op("act", (lambda c_, pc_, par_: lambda s: s.activation(
                hbf[par_][:, :].rearrange("p (c t) -> p c t", t=512)[:, c_, :],
                t1f[pc_][:, :], AF.Identity, scale=c1a[:, c_:c_ + 1],
                bias=sav[:, c_:c_ + 1]))(c, pc, par),
                r=[f"t1f{pc}", "c1a", "sav"], w=[f"hbf{par}_{c}"], tag=f"hmod_{i}_{c}")
        hview = hbf[par][:, :].rearrange("p (c t) -> p c t", t=512)
        wdkv_v = wdkv_s[:, :].rearrange("p (c m) -> p c m", m=DC)
        wkr_v = wkr_s[:, :].rearrange("p (c m) -> p c m", m=DR)
        wdq_v = wdq_s[:, :].rearrange("p (c m) -> p c m", m=DC)
        for c in range(NCH):
            P.op("pe", (lambda hv, c_: lambda t: t.matmul(
                ps_a[0:DC, :], wdkv_v[:, c_, :], hv[:, c_, :],
                start=(c_ == 0), stop=(c_ == NCH - 1)))(hview, c),
                r=[f"hbf{par}_{c}", "wdkv_s"],
                w=["psK1"] if c == NCH - 1 else (), war=["psK1"] if c == 0 else (),
                tag=f"mm_ckv{i}_{c}")
        P.op("act", (lambda t0_: lambda s: s.activation(
            kt[0:DC, t0_:t0_ + 512], ps_a[0:DC, :], AF.Identity, bias=bdkv_s[:, 0:1]))(t0),
            r=["psK1", "bdkv_s"], w=[f"kt_c{i}"], tag=f"ev_ckv{i}")
        for c in range(NCH):
            P.op("pe", (lambda hv, c_: lambda t: t.matmul(
                ps_b[0:DR, :], wkr_v[:, c_, :], hv[:, c_, :],
                start=(c_ == 0), stop=(c_ == NCH - 1)))(hview, c),
                r=[f"hbf{par}_{c}", "wkr_s"],
                w=["psK2"] if c == NCH - 1 else (), war=["psK2"] if c == 0 else (),
                tag=f"mm_kr{i}_{c}")
        P.op("act", (lambda t0_: lambda s: s.activation(
            kt[32:64, t0_:t0_ + 512], ps_b[0:DR, :], AF.Identity, bias=bkr_s[:, 0:1]))(t0),
            r=["psK2", "bkr_s"], w=[f"kt_r{i}"], tag=f"ev_kr{i}")
        for tc in range(4):
            kvch = i * 4 + tc
            for c in range(NCH):
                P.op("pe", (lambda hv, c_, tc_: lambda t: t.matmul(
                    ps_c[:, 0:DC], hv[:, c_, tc_ * 128:(tc_ + 1) * 128],
                    wdkv_v[:, c_, :], start=(c_ == 0), stop=False))(hview, c, tc),
                    r=[f"hbf{par}_{c}", "wdkv_s"], war=["psK3"] if c == 0 else (),
                    tag=f"mm_ctm{i}_{tc}_{c}")
            P.op("pe", lambda t: t.matmul(ps_c[:, 0:DC], ones_r_b[:, :], bdkvr_s[:, :],
                                          start=False, stop=True),
                 r=["ones_r_b", "bdkvr_s"], w=["psK3"], tag=f"mm_ctmb{i}_{tc}")
            P.op("act", (lambda kvch_: lambda s: s.activation(
                ckvtm[:, kvch_ * 34:kvch_ * 34 + 32], ps_c[:, 0:DC], AF.Identity))(kvch),
                r=["psK3"], w=[f"ckvtm{kvch}"], tag=f"ev_ctm{kvch}")
        if own:
            for c in range(NCH):
                P.op("pe", (lambda hv, c_: lambda t: t.matmul(
                    ps_d[0:DC, :], wdq_v[:, c_, :], hv[:, c_, :],
                    start=(c_ == 0), stop=(c_ == NCH - 1)))(hview, c),
                    r=[f"hbf{par}_{c}", "wdq_s"],
                    w=["psK4"] if c == NCH - 1 else (), war=["psK4"] if c == 0 else (),
                    tag=f"mm_cq{i}_{c}")
            P.op("act", (lambda t0_: lambda s: s.activation(
                cqs[0:DC, t0_:t0_ + 512], ps_d[0:DC, :], AF.Identity,
                bias=bdq_s[:, 0:1]))(t0),
                r=["psK4", "bdq_s"], w=[f"cq{i}"], tag=f"ev_cq{i}")

    for c8 in range(8):
        P.op("act", (lambda c_: lambda s: s.activation(
            sqk[:, 0:512], kt[0:64, c_ * 512:(c_ + 1) * 512], AF.Square))(c8),
            r=[f"kt_c{c8}", f"kt_r{c8}"], w=["sqk"], tag=f"sq_k{c8}")
        P.op("pe", lambda t: t.matmul(ps_e[0:1, :], ones_c_f[0:64, :], sqk[:, 0:512],
                                      start=True, stop=True),
             r=["sqk", "ones_c_f"], w=["psK5"], tag=f"mm_n2k{c8}")
        P.op("dve", (lambda c_: lambda v: v.reduce_max(
            mxk[0:1, c_:c_ + 1], ps_e[0:1, :], axis=AX.X))(c8),
            r=["psK5"], w=[f"mxk{c8}"], tag=f"max_k{c8}")
    P.op("dve", lambda v: v.reduce_max(mxk[0:1, 8:9], mxk[0:1, 0:8], axis=AX.X),
         r=[f"mxk{c}" for c in range(8)], w=["mxk8"], tag="maxk_all")
    P.op("act", lambda s: s.activation(mxk[0:1, 8:9], mxk[0:1, 8:9], AF.Sqrt),
         r=["mxk8"], w=["mxk8"], tag="maxk_sqrt")

    ctx_ab.close()
    ctx_c = ExitStack()
    Ec = ctx_c.enter_context
    qt_s = Ec(nc.sbuf_tensor("qt_s", [128, T], BF16))
    qt_s2 = Ec(nc.sbuf_tensor("qt_s2", [128, T], BF16))
    qt_s3 = Ec(nc.sbuf_tensor("qt_s3", [128, T], BF16))
    qt_s4 = Ec(nc.sbuf_tensor("qt_s4", [128, T], BF16))
    pt_s = [Ec(nc.sbuf_tensor(f"pt_s{i}", [128, 1024], BF16)) for i in range(2)]
    zst = [Ec(nc.sbuf_tensor(f"zst{i}", [64, T], BF16)) for i in range(4)]
    dinv_bc = Ec(nc.sbuf_tensor("dinv_bc", [128, 512], F32))
    zraw = Ec(nc.sbuf_tensor("zraw", [128, 512], F32))
    P.set_barrier()
    P.op("dve", lambda v: v.memset(qt_s[64:128, :], 0.0), w=["qtA_z"], tag="qz_A")
    P.op("dve", lambda v: v.memset(qt_s2[64:128, :], 0.0), w=["qtB_z"], tag="qz_B")
    P.op("dve", lambda v: v.memset(qt_s3[64:128, :], 0.0), w=["qtA_z"], tag="qz_A2")
    P.op("dve", lambda v: v.memset(qt_s4[64:128, :], 0.0), w=["qtB_z"], tag="qz_B2")

    qts2 = [[qt_s, qt_s2], [qt_s3, qt_s4]]
    negm_ = [negm, negm2]

    def build_steps(hp, dense=False):
        pp = hp % 2
        bq = 6 if dense else 7
        steps = []
        bstr = ([f"bstA{2 * hp}", f"bstA{2 * hp + 1}",
                 f"bstR{2 * hp}", f"bstR{2 * hp + 1}", "bstz", "cqz"])
        for qt in range(4):
            q0 = qt * 512
            bqq = (6 + qt % 2) if dense else 7
            steps.append((lambda hp=hp, q0=q0, qt=qt, bqq=bqq: P.op(
                "pe", (lambda hp_, q0_: lambda t: t.matmul(
                    psv(bqq), bst[:, hp_ * 128:(hp_ + 1) * 128],
                    cqs[:, q0_:q0_ + 512], start=True, stop=True))(hp, q0),
                r=bstr + [f"cq{qt}"], w=[f"psK{bqq}"], war=[f"psK{bqq}"],
                tag=f"mm_q{hp}_{qt}")))
            for hx in range(2):
                h = 2 * hp + hx
                qtt = qts2[pp][hx]
                pfx = "AB"[hx] + str(pp)
                steps.append((lambda h=h, q0=q0, qt=qt, qtt=qtt, pfx=pfx,
                              hx=hx, bqq=bqq: P.op(
                    "act", (lambda qtt_, h_, q0_, hx_: lambda s: s.activation(
                        qtt_[0:DC, q0_:q0_ + 512],
                        psv(bqq)[hx_ * DC:(hx_ + 1) * DC, :], AF.Identity,
                        scale=cst[0:DC, 1:2], bias=u1s[:, h_:h_ + 1]))(qtt, h, q0, hx),
                    r=[f"psK{bqq}", "cst_s", f"u1s{h}"], w=[f"qt{pfx}_c{qt}"],
                    tag=f"ev_qc{h}_{qt}")))
                steps.append((lambda h=h, q0=q0, qt=qt, qtt=qtt, pfx=pfx,
                              hx=hx, bqq=bqq: P.op(
                    "act", (lambda qtt_, h_, q0_, hx_: lambda s: s.activation(
                        qtt_[32:64, q0_:q0_ + 512],
                        psv(bqq)[64 + hx_ * DR:64 + (hx_ + 1) * DR, :], AF.Identity,
                        scale=cst[0:DR, 1:2], bias=bqr_sc[:, h_:h_ + 1]))(qtt, h, q0, hx),
                    r=[f"psK{bqq}", "cst_s", "bqr_sc"], w=[f"qt{pfx}_r{qt}"],
                    tag=f"ev_qr{h}_{qt}")))

        for hx in range(2):
            h = 2 * hp + hx
            qtt = qts2[pp][hx]
            pfx = "AB"[hx] + str(pp)
            steps.append((lambda h=h, qtt=qtt, pfx=pfx: P.op(
                "act", (lambda qtt_: lambda s: s.activation(
                    sqk[:, :], qtt_[0:64, :], AF.Square))(qtt),
                r=[f"qt{pfx}_c{q}" for q in range(4)] +
                  [f"qt{pfx}_r{q}" for q in range(4)],
                w=["sqk"], tag=f"sqq{h}")))
            for qt in range(4):
                nb = 6 + qt % 2 if dense else 7
                steps.append((lambda h=h, qt=qt, nb=nb: P.op(
                    "pe", (lambda qt_, nb_: lambda t: t.matmul(
                        psv(nb_)[0:1, :], ones_c_f[0:64, :],
                        sqk[:, qt_ * 512:(qt_ + 1) * 512], start=True, stop=True))(qt, nb),
                    r=["sqk", "ones_c_f"], w=[f"psK{nb}"], war=[f"psK{nb}"],
                    tag=f"mm_n2q{h}_{qt}")))
                steps.append((lambda h=h, qt=qt, nb=nb, hx=hx: P.op(
                    "dve", (lambda nb_, hx_, qt_: lambda v: v.reduce_max(
                        mxk[0:1, 9 + hx_ * 4 + qt_:10 + hx_ * 4 + qt_],
                        psv(nb_)[0:1, :], axis=AX.X))(nb, hx, qt),
                    r=[f"psK{nb}"], w=[f"mxq{hx}_{qt}"], tag=f"mxq{h}_{qt}")))

        steps.append((lambda hp=hp: P.op(
            "dve", lambda v: v.reduce_max(mxk[0:1, 18:19], mxk[0:1, 9:17],
                                          axis=AX.X),
            r=[f"mxq{hx_}_{qt_}" for hx_ in range(2) for qt_ in range(4)],
            w=["mxp"], tag=f"mxp{hp}")))
        steps.append((lambda hp=hp: P.op(
            "act", lambda s: s.activation(mxk[0:1, 18:19], mxk[0:1, 18:19],
                                          AF.Sqrt),
            r=["mxp"], w=["mxp"], tag=f"mxps{hp}")))
        steps.append((lambda hp=hp: P.op(
            "dve", lambda v: v.tensor_scalar(mxk[0:1, 18:19], mxk[0:1, 18:19],
                                             mxk[0:1, 8:9], -1.0,
                                             op0=ALU.mult, op1=ALU.mult),
            r=["mxp", "mxk8"], w=["mxp"], tag=f"mxpn{hp}")))
        steps.append((lambda hp=hp: P.op(
            "pe", lambda t: t.matmul(psv(bq)[:, 0:1], ones_r_f[:, :],
                                     mxk[0:1, 18:19], start=True, stop=True),
            r=["mxp", "ones_r_f"], w=[f"psK{bq}"], war=[f"psK{bq}"],
            tag=f"bc_mp{hp}")))
        steps.append((lambda hp=hp, pp=pp: P.op(
            "act", (lambda pp_: lambda s: s.activation(
                negm_[pp_][:, :], psv(bq)[:, 0:1], AF.Identity))(pp),
            r=[f"psK{bq}"], w=[f"negm{pp}"], tag=f"ev_mp{hp}")))
        return steps

    BUILD_INTERLEAVE = True
    epi_defer = []
    for st_f in build_steps(0, dense=True):
        st_f()

    for hp in range(4):
        pp = hp % 2
        if hp == 1:
            for e_i in range(NE):
                P.dma(st_pp, "pool", (lambda e_: lambda g: g.dma_start(
                    out=w1c[e_], in_=w1t[e_]))(e_i),
                    w=[f"w1c_{e_i}"], tag=f"pp_w1_{e_i}")
                P.dma(st_pp, "pool", (lambda e_: lambda g: g.dma_start(
                    out=w2c[e_], in_=w2t[e_]))(e_i),
                    w=[f"w2c_{e_i}"], tag=f"pp_w2_{e_i}")
        nxt = build_steps(hp + 1) if hp + 1 < 4 else []
        if not BUILD_INTERLEAVE:
            for st_f in nxt:
                st_f()
            nxt = []
        cur = 0
        for qt in range(4):
            q0 = qt * 512
            qrd = [f"qt{p_}{pp}_{k_}{qt}" for p_ in "AB" for k_ in ("c", "r")]

            def emit_scores(c32):
                par = c32 % 2
                i8 = c32 // 4
                ktn = [f"kt_c{i8}", f"kt_r{i8}", "kt_ones"]
                for hx in range(2):
                    P.op("pe", (lambda qtt_, c_, q0_, b_: lambda t: t.matmul(
                        psv(b_), kt[:, c_ * 128:(c_ + 1) * 128],
                        qtt_[:, q0_:q0_ + 512], start=True, stop=True))(
                            qts2[pp][hx], c32, q0, 2 * par + hx),
                        r=ktn + qrd + ["ktz_", "qtA_z", "qtB_z"],
                        w=[f"psK{2 * par + hx}"],
                        war=[f"psK{2 * par + hx}"],
                        tag=f"mm_s{hp}_{qt}_{c32}_{hx}")
                P.op("act", (lambda b_, par_, pp_: lambda s: s.activation(
                    pt_s[par_][:, :], psv(b_, 2), AF.Exp,
                    bias=negm_[pp_][:, 0:1]))(2 * par, par, pp),
                    r=[f"psK{2 * par}", f"psK{2 * par + 1}", f"negm{pp}"],
                    w=[f"pt{par}"], tag=f"exp{hp}_{qt}_{c32}")

            def emit_pv(c32):
                par = c32 % 2
                P.op("pe", (lambda c_, par_: lambda t: t.matmul(
                    psall[0:33, 4 * 512:4 * 512 + 512],
                    ckvtm[:, c_ * 34:c_ * 34 + 33],
                    pt_s[par_][:, 0:512],
                    start=(c_ == 0), stop=(c_ == 31)))(c32, par),
                    r=[f"ckvtm{c32}", "ckvtm_ones", f"pt{par}"],
                    w=["psK4"] if c32 == 31 else (),
                    war=["psK4"] if c32 == 0 else (),
                    tag=f"mm_pv{hp}_{qt}_{c32}_0")
                P.op("pe", (lambda c_, par_: lambda t: t.matmul(
                    psall[64:97, 5 * 512:5 * 512 + 512],
                    ckvtm[:, c_ * 34:c_ * 34 + 33],
                    pt_s[par_][:, 512:1024],
                    start=(c_ == 0), stop=(c_ == 31),
                    tile_position=(0, 64)))(c32, par),
                    r=[f"ckvtm{c32}", "ckvtm_ones", f"pt{par}"],
                    w=["psK5"] if c32 == 31 else (),
                    war=["psK5"] if c32 == 0 else (),
                    tag=f"mm_pv{hp}_{qt}_{c32}_1")

            emit_scores(0)
            for c32 in range(32):
                if c32 < 31:
                    emit_scores(c32 + 1)
                emit_pv(c32)
                if epi_defer:
                    epi_defer.pop(0)()
                tgt = ((qt * 32 + c32 + 1) * len(nxt)) // 126
                while cur < min(tgt, len(nxt)):
                    nxt[cur]()
                    cur += 1
            for hx in range(2):
                h = 2 * hp + hx
                row = hx * 32
                zb = 4 + hx
                zrow = 32 if hx == 0 else 96
                zlo = 0 if hx == 0 else 64
                pq = hx
                rq, rqn = rowa_[pq], f"rowa{pq}"
                P.op("act", (lambda rq, zb_, zr_: lambda s: s.activation(
                    rq[:, :], psall[zr_:zr_ + 1, zb_ * 512:zb_ * 512 + 512],
                    AF.Ln))(rq, zb, zrow),
                    r=[f"psK{zb}"], w=[rqn], tag=f"dln{h}_{qt}")
                P.op("dve", (lambda zb_, zl_, hx_: lambda v: v.tensor_copy(
                    zraw[zl_:zl_ + 32, :],
                    psall[zl_:zl_ + 32, zb_ * 512:zb_ * 512 + 512]))(zb, zlo, hx),
                    r=[f"psK{zb}"], w=[f"zraw{hx}"], tag=f"zcp{h}_{qt}")
                epi_defer.append((lambda rq=rq, rqn=rqn, h=h, qt=qt: P.op(
                    "pe", (lambda rq_: lambda t: t.matmul(
                        psv(6), ones_r_f[:, :], rq_[:, :],
                        start=True, stop=True))(rq),
                    r=[rqn, "ones_r_f"], w=["psK6"], war=["psK6"],
                    tag=f"bc_d{h}_{qt}")))
                epi_defer.append((lambda h=h, qt=qt: P.op(
                    "act", lambda s: s.activation(dinv_bc[:, :], psv(6), AF.Exp,
                                                  scale=-1.0),
                    r=["psK6"], w=["dinv_bc"], tag=f"ev_d{h}_{qt}")))
                epi_defer.append((lambda hp=hp, row=row, qt=qt, zlo=zlo, hx=hx,
                                  h=h: P.op(
                    "dve", (lambda hp_, row_, qt_, zl_: lambda v: v.tensor_tensor(
                        zst[hp_][row_:row_ + 32, qt_ * 512:(qt_ + 1) * 512],
                        zraw[zl_:zl_ + 32, :], dinv_bc[zl_:zl_ + 32, :],
                        op=ALU.mult))(hp, row, qt, zlo),
                    r=[f"zraw{hx}", "dinv_bc"], w=[f"zst{hp}_{row}_{qt}"],
                    tag=f"ev_z{h}_{qt}")))

    for fn in epi_defer:
        fn()
    epi_defer = []

    for c in range(NCH):
        P.op("dve", (lambda c_: lambda v: v.tensor_scalar_add(
            xres[:, c_ * T:(c_ + 1) * T], xres[:, c_ * T:(c_ + 1) * T],
            gab[:, c_:c_ + 1]))(c),
            r=["gab"] + [f"xres_t{i}" for i in range(4)],
            w=[f"xg{c}"], tag=f"gabadd{c}")
    znames = [f"zst{st_}_{rw}_{q}" for st_ in range(4) for rw in (0, 32) for q in range(4)]
    for c in range(NCH):
        for qt in range(4):
            bank = qt % 2
            for sti in range(4):
                P.op("pe", (lambda c_, qt_, sti_, bank_: lambda t: t.matmul(
                    psv(bank_), m2st[sti_][:, c_ * 128:(c_ + 1) * 128],
                    zst[sti_][:, qt_ * 512:(qt_ + 1) * 512],
                    start=(sti_ == 0), stop=(sti_ == 3)))(c, qt, sti, bank),
                    r=[f"m2st{sti}_0_0", f"m2st{sti}_0_1", f"m2st{sti}_32_0",
                       f"m2st{sti}_32_1", f"zst{sti}_0_{qt}", f"zst{sti}_32_{qt}"],
                    w=[f"psK{bank}"] if sti == 3 else (),
                    war=[f"psK{bank}"] if sti == 0 else (),
                    tag=f"mm_ao{c}_{qt}_{sti}")
            P.op("dve", (lambda c_, qt_, bank_: lambda v: v.scalar_tensor_tensor(
                xres[:, c_ * T + qt_ * 512:c_ * T + (qt_ + 1) * 512],
                psv(bank_), gav[:, c_:c_ + 1],
                xres[:, c_ * T + qt_ * 512:c_ * T + (qt_ + 1) * 512],
                op0=ALU.mult, op1=ALU.add))(c, qt, bank),
                r=[f"psK{bank}", "gav", f"xg{c}"], w=[f"xg{c}"],
                tag=f"res_ao{c}_{qt}")

    dbg_pre_free(P, nc, dbg, kt, cqs, ckvtm, st_out)
    ctx_c.close()
    ctx_bc.close()
    ytmp = t1f[0]
    ctx_e = ExitStack()
    Ee = ctx_e.enter_context
    h2b = [Ee(nc.sbuf_tensor(f"h2b{i}", [128, NCH * TH], BF16)) for i in range(2)]
    w1b = [Ee(nc.sbuf_tensor(f"w1b{i}", [128, NCH * 512], BF16)) for i in range(2)]
    w2b = [Ee(nc.sbuf_tensor(f"w2b{i}", [128, 4 * 1024], BF16)) for i in range(2)]
    heb = [Ee(nc.sbuf_tensor(f"heb{i}", [128, 4 * TH], BF16)) for i in range(2)]
    yac = Ee(nc.sbuf_tensor("yac", [128, NCH * TH], F32))
    t2f = t1f
    xb2 = [Ee(nc.sbuf_tensor(f"xb2{i}", [128, TH], BF16)) for i in range(2)]
    xs2 = [Ee(nc.sbuf_tensor(f"xs2{i}", [128, TH], BF16)) for i in range(2)]
    wfm_ = [Ee(nc.sbuf_tensor(f"wfm{i}", [16, TH], F32)) for i in range(2)]
    wt16 = Ee(nc.sbuf_tensor("wt16", [128, 64], F32))
    glog = Ee(nc.sbuf_tensor("glog", [128, 48], F32))
    gsc = Ee(nc.sbuf_tensor("gsc", [128, 192], F32))
    wzb_ = [Ee(nc.sbuf_tensor(f"wzb{i}", [128, TH], BF16)) for i in range(2)]
    wbb = Ee(nc.sbuf_tensor("wbb", [128, TH], F32))
    h2f = Ee(nc.sbuf_tensor("h2f", [128, 4 * TH], F32))
    P.set_barrier()
    P.op("dve", lambda v: v.memset(wt16[:, :], 0.0), w=["wt16z"], tag="z_wt16")
    if DEBUG_OUTS:
        P.dma(st_out, "sp", lambda e: e.dma_start(out=dbg["xa"], in_=xres[:, :]),
              r=[f"xg{c}" for c in range(NCH)], w=["od_xa"], tag="st_xa")

    GSL = 108

    def prep_stage(th, stage):
        pt = th % 2
        q0 = th * TH
        xv = xres[:, :].rearrange("p (c t) -> p c t", t=T)[:, :, q0:q0 + TH]
        h2v = h2b[pt][:, :].rearrange("p (c t) -> p c t", t=TH)
        rowa, rowb = rowa_[pt], rowb_[pt]
        rowc = rowc_[pt]
        ran, rbn, rcn = f"rowa{pt}", f"rowb{pt}", f"rowc{pt}"
        wfm = wfm_[pt]
        if stage == 0:
            for c in range(NCH):
                pcx = c % 2
                P.op("dve", (lambda xv_, c_, pc_: lambda v: v.tensor_copy(
                    xb2[pc_][:, :], xv_[:, c_, :]))(xv, c, pcx),
                    r=[f"xg{c}"], w=[f"xb2{pcx}"], tag=f"cast2_{th}_{c}")
                P.op("act", (lambda xv_, c_, pc_: lambda s: s.activation(
                    xs2[pc_][:, :], xv_[:, c_, :], AF.Square))(xv, c, pcx),
                    r=[f"xg{c}"], w=[f"xs2{pcx}"], tag=f"sq2_{th}_{c}")
                P.op("pe", (lambda c_, pc_: lambda t: t.matmul(
                    psv(4), ones128[:, :], xb2[pc_][:, :],
                    start=(c_ == 0), stop=(c_ == NCH - 1)))(c, pcx),
                    r=[f"xb2{pcx}", "ones128"],
                    w=["psK4"] if c == NCH - 1 else (), war=["psK4"] if c == 0 else (),
                    tag=f"mm2_sum{th}_{c}")
                P.op("pe", (lambda c_, pc_: lambda t: t.matmul(
                    psv(5), ones128[:, :], xs2[pc_][:, :],
                    start=(c_ == 0), stop=(c_ == NCH - 1)))(c, pcx),
                    r=[f"xs2{pcx}", "ones128"],
                    w=["psK5"] if c == NCH - 1 else (), war=["psK5"] if c == 0 else (),
                    tag=f"mm2_sumsq{th}_{c}")
        elif stage == 1:
            P.op("dve", (lambda p_: lambda v: v.tensor_scalar_mul(
                mb_s[p_][:, :], psv(4), 1.0 / D))(pt),
                r=["psK4"], w=[f"mb_s{pt}"], tag=f"mean2_{th}")
            P.op("dve", (lambda p_: lambda v: v.tensor_scalar_mul(
                rb_s[p_][:, :], psv(5), 1.0 / D))(pt),
                r=["psK5"], w=[f"rb_s{pt}"], tag=f"msq2_{th}")
            P.op("dve", (lambda p_: lambda v: v.tensor_tensor(
                t1f[p_][:, :], mb_s[p_][:, :], mb_s[p_][:, :], op=ALU.mult))(pt),
                r=[f"mb_s{pt}"], w=[f"t1f{pt}"], tag=f"m2sq_{th}")
            P.op("dve", (lambda p_: lambda v: v.tensor_tensor(
                rb_s[p_][:, :], rb_s[p_][:, :], t1f[p_][:, :], op=ALU.subtract))(pt),
                r=[f"rb_s{pt}", f"t1f{pt}"], w=[f"rb_s{pt}"], tag=f"var2_{th}")
            P.op("act", (lambda p_: lambda s: s.activation(
                rb_s[p_][:, :], rb_s[p_][:, :], AF.Ln, bias=cst[:, 0:1]))(pt),
                r=[f"rb_s{pt}", "cst"], w=[f"rb_s{pt}"], tag=f"lnv2_{th}")
            P.op("act", (lambda p_: lambda s: s.activation(
                rb_s[p_][:, :], rb_s[p_][:, :], AF.Exp, scale=-0.5))(pt),
                r=[f"rb_s{pt}"], w=[f"rb_s{pt}"], tag=f"rstd2_{th}")
        elif stage in ("2a", "2b"):
            crange = range(0, 4) if stage == "2a" else range(4, 8)
            for c in crange:
                pc = c % 2
                P.op("pool", (lambda xv_, c_, pc_, p_: lambda g: g.tensor_tensor(
                    t1f[pc_][:, :], xv_[:, c_, :], mb_s[p_][:, :], op=ALU.subtract))(
                        xv, c, pc, pt),
                    r=[f"xg{c}", f"mb_s{pt}"], w=[f"t1f{pc}"], tag=f"h2n1_{th}_{c}")
                P.op("dve", (lambda pc_, p_: lambda v: v.tensor_tensor(
                    t1f[pc_][:, :], t1f[pc_][:, :], rb_s[p_][:, :], op=ALU.mult))(pc, pt),
                    r=[f"t1f{pc}", f"rb_s{pt}"], w=[f"t1f{pc}"], tag=f"h2n2_{th}_{c}")
                P.op("dve", (lambda c_, pc_: lambda v: v.tensor_scalar(
                    h2f[:, (c_ % 4) * TH:(c_ % 4 + 1) * TH], t1f[pc_][:, :],
                    c1b[:, c_:c_ + 1], sbv[:, c_:c_ + 1],
                    op0=ALU.mult, op1=ALU.add))(c, pc),
                    r=[f"t1f{pc}", "c1b", "sbv"], w=[f"h2fs_{c % 4}"],
                    tag=f"h2mod_{th}_{c}")
                P.op("act", (lambda h2v_, c_: lambda s: s.activation(
                    h2v_[:, c_, :], h2f[:, (c_ % 4) * TH:(c_ % 4 + 1) * TH],
                    AF.Identity))(h2v, c),
                    r=[f"h2fs_{c % 4}"], w=[f"h2{pt}_{c}"], tag=f"h2cast_{th}_{c}")
        elif stage in ("25a", "25b"):
            crange = range(0, 4) if stage == "25a" else range(4, 8)
            for c in crange:
                for tc in range(4):
                    P.op("pe", (lambda c_, tc_: lambda t: t.matmul(
                        psv(4)[:, tc_ * GSL + c_ * 12:tc_ * GSL + (c_ + 1) * 12],
                        h2f[:, (c_ % 4) * TH + tc_ * 128:(c_ % 4) * TH + (tc_ + 1) * 128],
                        gwt_s[:, :].rearrange("p (c e) -> p c e", e=12)[:, c_, :],
                        start=True, stop=True))(c, tc),
                        r=[f"h2fs_{c % 4}", "gwt_s"],
                        w=["psK4"] if (c == NCH - 1 and tc == 3) else (),
                        war=["psK4"] if (c == 0 and tc == 0) else (),
                        tag=f"mm_g{th}_{c}_{tc}")
            if stage == "25b":
                for tc in range(4):
                    P.op("pe", (lambda tc_: lambda t: t.matmul(
                        psv(4)[:, tc_ * GSL + 96:tc_ * GSL + 108],
                        ones_r_f[:, :], geb_s[:, :], start=True, stop=True))(tc),
                        r=["ones_r_f", "geb_s"], w=["psK4b"] if tc == 3 else (),
                        tag=f"mm_gb{th}_{tc}")
        elif stage in ("3a", "3b", "3c"):
            for tc in range(4):
                gb = 48 * tc
                gl = glog[:, 12 * tc:12 * tc + 12]
                if stage == "3a":
                    P.op("dve", (lambda tc_, gl_: lambda v: v.tensor_reduce(
                        gl_,
                        psv(4)[:, tc_ * GSL:tc_ * GSL + GSL].rearrange(
                            "p (k e) -> p e k", e=12),
                        axis=AX.X, op=ALU.add))(tc, gl),
                        r=["psK4", "psK4b"], w=[f"glog{tc}"], tag=f"gacc_{th}_{tc}")
                    P.op("dve", (lambda gb_, gl_: lambda v: v.reduce_max(
                        gsc[:, gb_:gb_ + 1], gl_, axis=AX.X))(gb, gl),
                        r=[f"glog{tc}"], w=[f"g_m1{tc}"], tag=f"g_m1_{th}_{tc}")
                    P.op("pool", (lambda gb_, gl_: lambda v: v.tensor_scalar(
                        gsc[:, gb_ + 8:gb_ + 20], gl_,
                        gsc[:, gb_:gb_ + 1], None, op0=ALU.is_equal))(gb, gl),
                        r=[f"glog{tc}", f"g_m1{tc}"], w=[f"g_mk1{tc}"],
                        tag=f"g_mk1_{th}_{tc}")
                    P.op("pool", (lambda gb_: lambda v: v.tensor_scalar(
                        gsc[:, gb_ + 20:gb_ + 32], gsc[:, gb_ + 8:gb_ + 20],
                        -1e30, None, op0=ALU.mult))(gb),
                        r=[f"g_mk1{tc}"], w=[f"g_msk{tc}"], tag=f"g_msk_{th}_{tc}")
                    P.op("pool", (lambda gb_, gl_: lambda v: v.tensor_tensor(
                        gsc[:, gb_ + 20:gb_ + 32], gsc[:, gb_ + 20:gb_ + 32],
                        gl_, op=ALU.add))(gb, gl),
                        r=[f"g_msk{tc}", f"glog{tc}"], w=[f"g_msk{tc}"],
                        tag=f"g_msk2_{th}_{tc}")
                    P.op("dve", (lambda gb_: lambda v: v.reduce_max(
                        gsc[:, gb_ + 1:gb_ + 2], gsc[:, gb_ + 20:gb_ + 32],
                        axis=AX.X))(gb),
                        r=[f"g_msk{tc}"], w=[f"g_m2{tc}"], tag=f"g_m2_{th}_{tc}")
                    P.op("pool", (lambda gb_: lambda v: v.tensor_tensor(
                        gsc[:, gb_ + 2:gb_ + 3], gsc[:, gb_:gb_ + 1],
                        gsc[:, gb_ + 1:gb_ + 2], op=ALU.subtract))(gb),
                        r=[f"g_m1{tc}", f"g_m2{tc}"], w=[f"g_d12{tc}"],
                        tag=f"g_d12_{th}_{tc}")
                elif stage == "3b":
                    P.op("act", (lambda gb_: lambda s: s.activation(
                        gsc[:, gb_ + 3:gb_ + 4], gsc[:, gb_ + 2:gb_ + 3],
                        AF.Sigmoid))(gb),
                        r=[f"g_d12{tc}"], w=[f"g_w1{tc}"], tag=f"g_w1_{th}_{tc}")
                else:
                    P.op("pool", (lambda gb_: lambda v: v.tensor_scalar(
                        gsc[:, gb_ + 4:gb_ + 5], gsc[:, gb_ + 3:gb_ + 4],
                        -1.0, 1.0, op0=ALU.mult, op1=ALU.add))(gb),
                        r=[f"g_w1{tc}"], w=[f"g_w2{tc}"], tag=f"g_w2_{th}_{tc}")
                    P.op("pool", (lambda gb_, gl_: lambda v: v.tensor_scalar(
                        gsc[:, gb_ + 32:gb_ + 44], gl_,
                        gsc[:, gb_ + 1:gb_ + 2], None, op0=ALU.is_equal))(gb, gl),
                        r=[f"glog{tc}", f"g_m2{tc}"], w=[f"g_mk2{tc}"],
                        tag=f"g_mk2_{th}_{tc}")
                    P.op("pool", (lambda gb_: lambda v: v.tensor_scalar(
                        gsc[:, gb_ + 8:gb_ + 20], gsc[:, gb_ + 8:gb_ + 20],
                        gsc[:, gb_ + 3:gb_ + 4], None, op0=ALU.mult))(gb),
                        r=[f"g_mk1{tc}", f"g_w1{tc}"], w=[f"g_mk1{tc}"],
                        tag=f"g_a_{th}_{tc}")
                    P.op("pool", (lambda gb_: lambda v: v.tensor_scalar(
                        gsc[:, gb_ + 32:gb_ + 44], gsc[:, gb_ + 32:gb_ + 44],
                        gsc[:, gb_ + 4:gb_ + 5], None, op0=ALU.mult))(gb),
                        r=[f"g_mk2{tc}", f"g_w2{tc}"], w=[f"g_mk2{tc}"],
                        tag=f"g_b_{th}_{tc}")
                    P.op("pool", (lambda gb_: lambda v: v.tensor_tensor(
                        gsc[:, gb_ + 8:gb_ + 20], gsc[:, gb_ + 8:gb_ + 20],
                        gsc[:, gb_ + 32:gb_ + 44], op=ALU.add))(gb),
                        r=[f"g_mk1{tc}", f"g_mk2{tc}"], w=[f"g_mk1{tc}"],
                        tag=f"g_w12_{th}_{tc}")
                    P.op("pool", (lambda tc_, gb_: lambda v: v.tensor_copy(
                        wt16[:, tc_ * 16:tc_ * 16 + 8], gsc[:, gb_ + 8:gb_ + 16]))(tc, gb),
                        r=[f"g_mk1{tc}"], w=[f"wt16_{tc}"], war=["wt16z"],
                        tag=f"g_cp_{th}_{tc}")
                    P.op("dve", (lambda tc_, gb_: lambda v: v.reduce_sum(
                        wt16[:, tc_ * 16 + 8:tc_ * 16 + 9],
                        gsc[:, gb_ + 16:gb_ + 20], axis=AX.X))(tc, gb),
                        r=[f"g_mk1{tc}"], w=[f"wt16b_{tc}"], tag=f"g_wz_{th}_{tc}")
        elif stage == 35:
            for tc in range(4):
                P.op("pe", (lambda tc_: lambda t: t.transpose(
                    psv(5)[0:16, 0:128], wt16[:, tc_ * 16:(tc_ + 1) * 16],
                    eye_f[:, :]))(tc),
                    r=[f"wt16_{tc}", f"wt16b_{tc}", "wt16z", "eye_f"], w=["psK5"],
                    war=["psK5"], tag=f"g_tr_{th}_{tc}")
                P.op("act", (lambda wfm_t, tc_: lambda s: s.activation(
                    wfm_t[:, tc_ * 128:(tc_ + 1) * 128], psv(5)[0:16, 0:128],
                    AF.Identity))(wfm, tc),
                    r=["psK5"], w=[f"wfm{pt}_{tc}"], tag=f"ev_wfm_{th}_{tc}")
        elif stage == 4:
            wfmr = [f"wfm{pt}_{tc}" for tc in range(4)]
            P.op("pe", (lambda wfm_t: lambda t: t.matmul(
                psv(5), sel_s[:, 8 * 128:9 * 128], wfm_t[:, :],
                start=True, stop=True))(wfm),
                r=wfmr + ["sel_s"], w=["psK5"], war=["psK5"], tag=f"bc_wz{th}")
            P.op("act", (lambda p_: lambda s: s.activation(
                wzb_[p_][:, :], psv(5), AF.Identity))(pt),
                r=["psK5"], w=[f"wzb{pt}"], tag=f"ev_wz{th}")

    for k in (0, 1, "2a", "25a", "2b", "25b", "3a", "3b", "3c", 35, 4):
        prep_stage(0, k)

    wpf_done = set()

    def emit_wdma(th, e, fq):
        if (th, e, fq) in wpf_done:
            return
        wpf_done.add((th, e, fq))
        par = fq % 2
        P.dma(st_w1, "sp", (lambda e_, fq_, par_: lambda g: g.dma_start(
            out=w1b[par_][:, :].rearrange("p (c f) -> p c f", f=512),
            in_=w1c[e_].rearrange("(c p) f -> p c f", p=128)[:, :, fq_ * 512:(fq_ + 1) * 512]))(e, fq, par),
            r=[f"w1c_{e}"], w=[f"w1b{par}"], tag=f"ld_w1_{th}_{e}_{fq}")
        P.dma(st_w2, "sp", (lambda e_, fq_, par_: lambda g: g.dma_start(
            out=w2b[par_][:, :].rearrange("p (c d) -> p c d", d=1024),
            in_=w2c[e_][fq_ * 512:(fq_ + 1) * 512, :].rearrange(
                "(c p) d -> p c d", p=128)))(e, fq, par),
            r=[f"w2c_{e}"], w=[f"w2b{par}"], tag=f"ld_w2_{th}_{e}_{fq}")

    for th in range(NTH):
        q0 = th * TH
        pt = th % 2
        xv = xres[:, :].rearrange("p (c t) -> p c t", t=T)[:, :, q0:q0 + TH]
        h2v = h2b[pt][:, :].rearrange("p (c t) -> p c t", t=TH)
        wfm = wfm_[pt]
        wfmr = [f"wfm{pt}_{tc}" for tc in range(4)]
        for c in range(NCH):
            P.op("dve", (lambda xv_, c_: lambda v: v.tensor_scalar(
                yac[:, c_ * TH:(c_ + 1) * TH], xv_[:, c_, :], invgb[:, c_:c_ + 1],
                None, op0=ALU.mult))(xv, c),
                r=[f"xg{c}", "invgb"], w=[f"yac{c}"], tag=f"y0_{th}_{c}")
            P.op("dve", (lambda h2v_, c_, p_: lambda v: v.tensor_tensor(
                ytmp[:, :], h2v_[:, c_, :], wzb_[p_][:, :], op=ALU.mult))(h2v, c, pt),
                r=[f"h2{pt}_{c}", f"wzb{pt}"], w=["t1f0"], tag=f"yz_{th}_{c}")
            P.op("pool", (lambda c_: lambda g: g.tensor_tensor(
                yac[:, c_ * TH:(c_ + 1) * TH], yac[:, c_ * TH:(c_ + 1) * TH],
                ytmp[:, :], op=ALU.add))(c),
                r=[f"yac{c}", "t1f0"], w=[f"yac{c}"], tag=f"yz2_{th}_{c}")
        for e in range(NE):
            if th + 1 < NTH and 1 <= e <= 7:
                for st_k in ((0, 1), ("2a",), ("25a", "2b"), ("25b", "3a"),
                             ("3b", "3c"), (35,), (4,))[e - 1]:
                    prep_stage(th + 1, st_k)
            P.op("pe", (lambda e_, wfm_t: lambda t: t.matmul(
                ps_g[:, :], sel_s[:, e_ * 128:(e_ + 1) * 128], wfm_t[:, :],
                start=True, stop=True))(e, wfm),
                r=wfmr + ["sel_s"], w=["psK6"], war=["psK6"], tag=f"bc_wb{th}_{e}")
            P.op("act", lambda s: s.activation(wbb[:, :], ps_g[:, :], AF.Identity),
                 r=["psK6"], w=["wbb"], tag=f"ev_wb{th}_{e}")

            def emit_h(fq):
                par = fq % 2
                w1v = w1b[par][:, :].rearrange("p (c f) -> p c f", f=512)
                hev = heb[par][:, :].rearrange("p (c t) -> p c t", t=TH)
                emit_wdma(th, e, fq)
                for fc in range(4):
                    pb = fc % 2
                    for dc in range(NCH):
                        P.op("pe", (lambda w1v_, h2v_, fc_, dc_, pb_: lambda t: t.matmul(
                            psv(pb_), w1v_[:, dc_, fc_ * 128:(fc_ + 1) * 128],
                            h2v_[:, dc_, :], start=(dc_ == 0), stop=(dc_ == NCH - 1)))(
                                w1v, h2v, fc, dc, pb),
                            r=[f"w1b{par}", f"h2{pt}_{dc}"],
                            w=[f"psK{pb}"] if dc == NCH - 1 else (),
                            war=[f"psK{pb}"] if dc == 0 else (),
                            tag=f"mm_h{th}_{e}_{fq}_{fc}_{dc}")
                    P.op("act", (lambda hev_, fc_, pb_: lambda s: s.activation(
                        hev_[:, fc_, :], psv(pb_), AF.Silu))(hev, fc, pb),
                        r=[f"psK{pb}"], w=[f"he{par}_{fc}"], tag=f"silu{th}_{e}_{fq}_{fc}")
                    P.op("dve", (lambda hev_, fc_: lambda v: v.tensor_tensor(
                        hev_[:, fc_, :], hev_[:, fc_, :], wbb[:, :], op=ALU.mult))(hev, fc),
                        r=[f"he{par}_{fc}", "wbb"], w=[f"he{par}_{fc}"],
                        tag=f"hsc{th}_{e}_{fq}_{fc}")

            def emit_y(fq):
                par = fq % 2
                w2v = w2b[par][:, :].rearrange("p (c d) -> p c d", d=1024)
                hev = heb[par][:, :].rearrange("p (c t) -> p c t", t=TH)
                for dc in range(NCH):
                    pb = 2 + dc % 2
                    for fc in range(4):
                        P.op("pe", (lambda w2v_, hev_, fc_, dc_, pb_: lambda t: t.matmul(
                            psv(pb_), w2v_[:, fc_, dc_ * 128:(dc_ + 1) * 128],
                            hev_[:, fc_, :], start=(fc_ == 0), stop=(fc_ == 3)))(
                                w2v, hev, fc, dc, pb),
                            r=[f"w2b{par}", f"he{par}_{fc}"],
                            w=[f"psK{pb}"] if fc == 3 else (),
                            war=[f"psK{pb}"] if fc == 0 else (),
                            tag=f"mm_y{th}_{e}_{fq}_{dc}_{fc}")
                    P.op("dve", (lambda dc_, pb_: lambda v: v.tensor_tensor(
                        yac[:, dc_ * TH:(dc_ + 1) * TH], psv(pb_),
                        yac[:, dc_ * TH:(dc_ + 1) * TH], op=ALU.add))(dc, pb),
                        r=[f"psK{pb}", f"yac{dc}"], w=[f"yac{dc}"],
                        tag=f"ya{th}_{e}_{fq}_{dc}")

            emit_h(0)
            for fq in range(4):
                if fq < 3:
                    emit_h(fq + 1)
                emit_y(fq)
            if e == NE - 1 and th + 1 < NTH:
                emit_wdma(th + 1, 0, 0)
                emit_wdma(th + 1, 0, 1)
        for c in range(NCH):
            P.op("dve", (lambda c_: lambda v: v.tensor_scalar(
                yac[:, c_ * TH:(c_ + 1) * TH], yac[:, c_ * TH:(c_ + 1) * TH],
                gbv[:, c_:c_ + 1], None, op0=ALU.mult))(c),
                r=[f"yac{c}", "gbv"], w=[f"yac{c}"], tag=f"gbm_{th}_{c}")
        P.dma(st_out, "pool", (lambda q0_: lambda e: e.dma_start(
            out=r3(out_fm)[:, :, q0_:q0_ + TH],
            in_=yac[:, :].rearrange("p (c t) -> p c t", t=TH)))(q0),
            r=[f"yac{c}" for c in range(NCH)], w=[f"od_y{th}"], tag=f"st_y{th}")

    P.set_barrier()

    outdeps = []
    if DEBUG_OUTS:
        outdeps += ["od_kt", "od_cq", "od_ctm", "od_h2", "od_wfm", "od_yini", "od_xa"]
        P.dma(st_out, "sp", lambda e: e.dma_start(out=dbg["mod"], in_=mods[:, :]),
              r=["mods", "mods_b"], w=["od_mod"], tag="st_mod")
        outdeps.append("od_mod")
    outdeps += [f"od_y{th}" for th in range(NTH)]
    P.barrier("sp", r=outdeps, tag="final")

    ctx_e.close()
    with ctx:
        block = ctx.enter_context(nc.Block())
        P.lower(nc, ctx, block)
    return nc



def make_inputs(full):
    eye = np.eye(128, dtype=np.float32)
    w1T = np.ascontiguousarray(full["L0_ffn_w1"].transpose(0, 2, 1))
    w2T = np.ascontiguousarray(full["L0_ffn_w2"].transpose(0, 2, 1))
    gwt = np.ascontiguousarray(full["L0_ffn_gate_w"].T)
    _ = full["L0_attn_b_UK"]
    common = dict(
        wdq=np.ascontiguousarray(full["L0_attn_W_DQ"]),
        bdq=np.ascontiguousarray(full["L0_attn_b_DQ"].reshape(DC, 1)),
        wdkv=np.ascontiguousarray(full["L0_attn_W_DKV"]),
        bdkv=np.ascontiguousarray(full["L0_attn_b_DKV"].reshape(DC, 1)),
        bdkvr=np.ascontiguousarray(full["L0_attn_b_DKV"].reshape(1, DC)),
        wkr=np.ascontiguousarray(full["L0_attn_W_KR"]),
        bkr=np.ascontiguousarray(full["L0_attn_b_KR"].reshape(DR, 1)),
        wuq=np.ascontiguousarray(full["L0_attn_W_UQ"]),
        buq=np.ascontiguousarray(full["L0_attn_b_UQ"].reshape(H, 128).T),
        wuk=np.ascontiguousarray(full["L0_attn_W_UK"]),
        wuv=np.ascontiguousarray(full["L0_attn_W_UV"]),
        buv=np.ascontiguousarray(full["L0_attn_b_UV"].reshape(H, 128).T),
        wqr=np.ascontiguousarray(full["L0_attn_W_QR"]),
        bqr=np.ascontiguousarray(full["L0_attn_b_QR"].reshape(H, DR).T),
        wo=np.ascontiguousarray(full["L0_attn_W_O"]),
        bo=np.ascontiguousarray(full["L0_attn_b_O"].reshape(NCH, 128).T),
        gwt=gwt,
        geb=np.ascontiguousarray(full["L0_ffn_expert_bias"].reshape(1, 12)),
        w1t=w1T, w2t=w2T,
        wmod=np.ascontiguousarray(full["L0_wmod"]),
        eye=eye,
        sel=np.ascontiguousarray(
            (np.arange(16)[:, None] == (np.arange(9 * 128) // 128)[None, :])
            .astype(np.float32)),
    )
    in_maps = []
    for core in range(8):
        b, half = core // 2, core % 2
        xT = np.ascontiguousarray(full["L0_attn_h_t"][b].T)
        if half == 1:
            xT = np.ascontiguousarray(np.roll(xT, -T, axis=1))
        m = dict(common)
        m["xfm"] = xT
        m["cfm"] = np.ascontiguousarray(full["L0_c"][b].reshape(NCH, 128).T)
        in_maps.append(m)
    return in_maps


def kernel(**inputs):
    nc = build_program()
    in_maps = make_inputs(inputs)
    res = run_bass_kernel_spmd(nc, in_maps, core_ids=list(range(8)))
    out = np.empty((B, S, D), dtype=np.float32)
    for core in range(8):
        b, half = core // 2, core % 2
        out[b, half * T:(half + 1) * T, :] = res.results[core]["out_fm"].T
    return out


# version 31 (final)
# speedup vs baseline: 1.0370x; 1.0066x over previous
import math
import numpy as np
from contextlib import ExitStack

import concourse.bass as bass
import concourse.mybir as mybir
from concourse.bass_utils import run_bass_kernel_spmd

F32 = mybir.dt.float32
BF16 = mybir.dt.bfloat16
AF = mybir.ActivationFunctionType
ALU = mybir.AluOpType
AX = mybir.AxisListType

B, S, D = 4, 4096, 1024
T = 2048
NCH = D // 128
H, DK, DR, DC = 8, 128, 32, 32
NE, DFF = 8, 2048
SCALE = 1.0 / math.sqrt(DK + DR)
EPS = 1e-5
TH = 512
NTH = T // TH
DEBUG_OUTS = False
PHASES = "AB"

ENGINES = ("pe", "act", "dve", "pool", "sp")


class _Op:
    __slots__ = ("eng", "fn", "deps", "inc", "val", "dma_sem", "dma_val", "tag")

    def __init__(self, eng, fn, tag=""):
        self.eng = eng
        self.fn = fn
        self.deps = []
        self.inc = False
        self.val = None
        self.dma_sem = None
        self.dma_val = None
        self.tag = tag


class Stream:

    def __init__(self, name, n_sems):
        self.name = name
        self.n = n_sems
        self.uses = [0] * n_sems
        self.last_op = [None] * n_sems
        self.total = 0
        self.sems = None


class Prog:
    def __init__(self):
        self.ops = {e: [] for e in ENGINES}
        self.bufs = {}
        self.streams = []

    def stream(self, name, n_sems):
        s = Stream(name, n_sems)
        self.streams.append(s)
        return s

    def _dep(self, op, d):
        if d is None or d is op:
            return
        if d.eng == "pe" and op.eng == "pe" and d.dma_sem is None and op.dma_sem is None:
            return
        if d not in op.deps:
            op.deps.append(d)
            d.inc = True

    def _track(self, op, r, w, war):
        for name in r:
            ent = self.bufs.get(name)
            if ent is not None:
                self._dep(op, ent[0])
        for name in tuple(w) + tuple(war):
            ent = self.bufs.get(name)
            if ent is not None:
                self._dep(op, ent[0])
                for rd in ent[1]:
                    self._dep(op, rd)
        for name in r:
            ent = self.bufs.setdefault(name, [None, []])
            if op not in ent[1]:
                ent[1].append(op)
        for name in w:
            self.bufs[name] = [op, []]

    def op(self, eng, fn, r=(), w=(), war=(), tag=""):
        o = _Op(eng, fn, tag)
        self._track(o, r, w, war)
        for b in getattr(self, "_barrier", ()):
            self._dep(o, b)
        self.ops[eng].append(o)
        return o

    def dma(self, stream, eng, fn, r=(), w=(), war=(), tag=""):
        o = _Op(eng, fn, tag)
        idx = stream.total % stream.n
        stream.total += 1
        prev = stream.last_op[idx]
        if prev is not None:
            self._dep(o, prev)
        self._track(o, r, w, war)
        for b in getattr(self, "_barrier", ()):
            self._dep(o, b)
        stream.uses[idx] += 1
        o.dma_sem = (stream, idx)
        o.dma_val = 16 * stream.uses[idx]
        stream.last_op[idx] = o
        self.ops[eng].append(o)
        return o

    def barrier(self, eng, r, tag="barrier"):
        return self.op(eng, None, r=r, tag=tag)

    def frontier(self):
        f = []
        for e in ENGINES:
            for o in reversed(self.ops[e]):
                if o.fn is not None and o.dma_sem is None:
                    f.append(o)
                    break
        for st_ in self.streams:
            for o in st_.last_op:
                if o is not None:
                    f.append(o)
        return f

    def set_barrier(self):
        self._barrier = self.frontier()

    def lower(self, nc, ctx, block):
        E = ctx.enter_context
        esem = {e: E(nc.semaphore(f"sem_{e}")) for e in ENGINES}
        for s in self.streams:
            s.sems = [E(nc.semaphore(f"sem_{s.name}{i}")) for i in range(s.n)]
        for e in ENGINES:
            cnt = 0
            for o in self.ops[e]:
                if o.dma_sem is None and o.inc:
                    cnt += 1
                    o.val = cnt

        def wait_list(o):
            tgts = {}
            for d in o.deps:
                if d.dma_sem is not None:
                    sem = d.dma_sem[0].sems[d.dma_sem[1]]
                    v = d.dma_val
                else:
                    assert d.inc and d.val is not None, (d.tag, o.tag)
                    sem = esem[d.eng]
                    v = d.val
                k = id(sem)
                if k not in tgts or tgts[k][1] < v:
                    tgts[k] = (sem, v)
            return list(tgts.values())

        def run_engine(engobj, e):
            water = {}
            if e == "sp":
                self._water_sp = water
            for o in self.ops[e]:
                for sem, v in wait_list(o):
                    k = id(sem)
                    if water.get(k, 0) < v:
                        engobj.wait_ge(sem, v)
                        water[k] = v
                if o.fn is None:
                    assert not o.inc, o.tag
                    continue
                ins = o.fn(engobj)
                if o.dma_sem is not None:
                    sem = o.dma_sem[0].sems[o.dma_sem[1]]
                    ins.then_inc(sem, 16)
                elif o.inc:
                    ins.then_inc(esem[e], 1)

        @block.tensor
        def _(t):
            run_engine(t, "pe")

        @block.scalar
        def _(s):
            run_engine(s, "act")

        @block.vector
        def _(v):
            run_engine(v, "dve")

        @block.gpsimd
        def _(g):
            run_engine(g, "pool")

        @block.sync
        def _(sy):
            run_engine(sy, "sp")
            water = self._water_sp
            for st_ in self.streams:
                for j in range(st_.n):
                    if st_.uses[j]:
                        v = 16 * st_.uses[j]
                        k = id(st_.sems[j])
                        if water.get(k, 0) < v:
                            sy.wait_ge(st_.sems[j], v)




def dbg_pre_free(P, nc, dbg, kt, cqs, ckvtm, st_out):
    if not DEBUG_OUTS:
        return
    P.dma(st_out, "sp", lambda e: e.dma_start(out=dbg["kt"], in_=kt[:, :]),
          r=[f"kt_c{i}" for i in range(8)] + [f"kt_r{i}" for i in range(8)]
            + ["kt_ones"], w=["od_kt"], tag="st_kt")
    P.dma(st_out, "sp", lambda e: e.dma_start(out=dbg["cq"], in_=cqs[:, :]),
          r=[f"cq{i}" for i in range(4)], w=["od_cq"], tag="st_cq")
    P.dma(st_out, "sp", lambda e: e.dma_start(out=dbg["ctm"], in_=ckvtm[:, :]),
          r=[f"ckvtm{k}" for k in range(32)] + ["ckvtm_ones"], w=["od_ctm"],
          tag="st_ctm")


def build_program():
    nc = bass.Bass("TRN2", target_bir_lowering=False)
    P = Prog()

    def din(name, shape, dt=F32):
        return nc.dram_tensor(name, list(shape), dt, kind="ExternalInput").ap()

    xfm = din("xfm", (D, S))
    cfm = din("cfm", (128, NCH))
    wmod = din("wmod", (D, 6 * D))
    wdq = din("wdq", (D, DC))
    bdq = din("bdq", (DC, 1))
    wdkv = din("wdkv", (D, DC))
    bdkv = din("bdkv", (DC, 1))
    bdkvr = din("bdkvr", (1, DC))
    wkr = din("wkr", (D, DR))
    bkr = din("bkr", (DR, 1))
    wuq = din("wuq", (DC, H * DK))
    buq = din("buq", (128, H))
    wuk = din("wuk", (DC, H * DK))
    wuv = din("wuv", (DC, H * DK))
    buv = din("buv", (128, H))
    wqr = din("wqr", (DC, H * DR))
    bqr = din("bqr", (DR, H))
    wo = din("wo", (D, D))
    bo = din("bo", (128, NCH))
    gwt = din("gwt", (D, 12))
    geb = din("geb", (1, 12))
    w1t = din("w1t", (NE, D, DFF))
    w2t = din("w2t", (NE, DFF, D))
    eye = din("eye", (128, 128))
    sel = din("sel", (16, 9 * 128))

    out_fm = nc.dram_tensor("out_fm", [D, T], F32, kind="ExternalOutput").ap()
    w1c = nc.dram_tensor("w1c", [NE, D, DFF], BF16, kind="Internal").ap()
    w2c = nc.dram_tensor("w2c", [NE, DFF, D], BF16, kind="Internal").ap()
    dbg = {}
    if DEBUG_OUTS:
        dbg["kt"] = nc.dram_tensor("dbg_kt", [65, S], BF16, kind="ExternalOutput").ap()
        dbg["cq"] = nc.dram_tensor("dbg_cq", [DC, T], BF16, kind="ExternalOutput").ap()
        dbg["ctm"] = nc.dram_tensor("dbg_ctm", [128, 34 * (S // 128)], BF16,
                                    kind="ExternalOutput").ap()
        dbg["mod"] = nc.dram_tensor("dbg_mod", [128, 48], F32, kind="ExternalOutput").ap()
        dbg["h2"] = nc.dram_tensor("dbg_h2", [128, NCH * TH], BF16, kind="ExternalOutput").ap()
        dbg["wfm"] = nc.dram_tensor("dbg_wfm", [16, TH], F32, kind="ExternalOutput").ap()
        dbg["yini"] = nc.dram_tensor("dbg_yini", [128, NCH * TH], F32, kind="ExternalOutput").ap()
        dbg["xa"] = nc.dram_tensor("dbg_xa", [128, NCH * T], F32, kind="ExternalOutput").ap()

    ctx = ExitStack()
    E = ctx.enter_context

    xres = E(nc.sbuf_tensor("xres", [128, NCH * T], F32))
    mods = E(nc.sbuf_tensor("mods", [128, 48], F32))
    c1a = E(nc.sbuf_tensor("c1a", [128, NCH], F32))
    sav = E(nc.sbuf_tensor("sav", [128, NCH], F32))
    c1b = E(nc.sbuf_tensor("c1b", [128, NCH], F32))
    sbv = E(nc.sbuf_tensor("sbv", [128, NCH], F32))
    gav = E(nc.sbuf_tensor("gav", [128, NCH], F32))
    gbv = E(nc.sbuf_tensor("gbv", [128, NCH], F32))
    invgb = E(nc.sbuf_tensor("invgb", [128, NCH], F32))
    gab = E(nc.sbuf_tensor("gab", [128, NCH], F32))
    battn = E(nc.sbuf_tensor("battn", [128, NCH], F32))
    gtmp = E(nc.sbuf_tensor("gtmp", [128, 24], F32))
    cst = E(nc.sbuf_tensor("cst", [128, 2], F32))
    ones_c_f = E(nc.sbuf_tensor("ones_c_f", [128, 1], F32))
    ones_c_b = E(nc.sbuf_tensor("ones_c_b", [128, 1], BF16))
    ones128 = E(nc.sbuf_tensor("ones128", [128, 128], BF16))
    ones_r_f = E(nc.sbuf_tensor("ones_r_f", [1, 128], F32))
    ones_r_b = E(nc.sbuf_tensor("ones_r_b", [1, 128], BF16))
    eye_f = E(nc.sbuf_tensor("eye_f", [128, 128], F32))
    eye_b = E(nc.sbuf_tensor("eye_b", [128, 128], BF16))
    wdq_s = E(nc.sbuf_tensor("wdq_s", [128, NCH * DC], BF16))
    wdkv_s = E(nc.sbuf_tensor("wdkv_s", [128, NCH * DC], BF16))
    wkr_s = E(nc.sbuf_tensor("wkr_s", [128, NCH * DR], BF16))
    bdq_s = E(nc.sbuf_tensor("bdq_s", [DC, 1], F32))
    bdkv_s = E(nc.sbuf_tensor("bdkv_s", [DC, 1], F32))
    bdkvr_s = E(nc.sbuf_tensor("bdkvr_s", [1, DC], BF16))
    bkr_s = E(nc.sbuf_tensor("bkr_s", [DR, 1], F32))
    wqr_s = E(nc.sbuf_tensor("wqr_s", [DC, H * DR], BF16))
    buq_b = E(nc.sbuf_tensor("buq_b", [128, H], BF16))
    buv_b = E(nc.sbuf_tensor("buv_b", [128, H], BF16))
    bqr_sc = E(nc.sbuf_tensor("bqr_sc", [DR, H], F32))
    bo_s = E(nc.sbuf_tensor("bo_s", [128, NCH], F32))
    cfm_s = E(nc.sbuf_tensor("cfm_s", [128, NCH], F32))
    silu_b = E(nc.sbuf_tensor("silu_b", [128, NCH], BF16))
    gwt_s = E(nc.sbuf_tensor("gwt_s", [128, NCH * 12], F32))
    geb_s = E(nc.sbuf_tensor("geb_s", [1, 12], F32))
    bqr_s = E(nc.sbuf_tensor("bqr_s", [DR, H], F32))
    wuqT = E(nc.sbuf_tensor("wuqT", [128, H * DC], BF16))
    wukT = E(nc.sbuf_tensor("wukT", [128, H * DC], BF16))
    wuvT = E(nc.sbuf_tensor("wuvT", [128, H * DC], BF16))
    u1s = E(nc.sbuf_tensor("u1s", [DC, H], F32))
    t1f = [E(nc.sbuf_tensor(f"t1f{i}", [128, 512], F32)) for i in range(2)]
    rowa_ = [E(nc.sbuf_tensor(f"rowa{i}", [1, 512], F32)) for i in range(2)]
    rowb_ = [E(nc.sbuf_tensor(f"rowb{i}", [1, 512], F32)) for i in range(2)]
    rowc_ = [E(nc.sbuf_tensor(f"rowc{i}", [1, 512], F32)) for i in range(2)]
    rb_s = [E(nc.sbuf_tensor(f"rb_s{i}", [128, 512], F32)) for i in range(2)]
    mb_s = [E(nc.sbuf_tensor(f"mb_s{i}", [128, 512], F32)) for i in range(2)]
    mxk = E(nc.sbuf_tensor("mxk", [1, 20], F32))
    negm = E(nc.sbuf_tensor("negm", [128, 1], F32))
    negm2 = E(nc.sbuf_tensor("negm2", [128, 1], F32))
    sqk = E(nc.sbuf_tensor("sqk", [64, 2048], F32))
    sel_s = E(nc.sbuf_tensor("sel_s", [16, 9 * 128], F32))
    ctx_bc = ExitStack()
    Ebc = ctx_bc.enter_context
    cqs2 = None
    kt2 = None
    cqs = Ebc(nc.sbuf_tensor("cqs", [128, T], BF16))
    bst = Ebc(nc.sbuf_tensor("bst", [128, 4 * 128], BF16))
    kt = Ebc(nc.sbuf_tensor("kt", [128, S], BF16))
    ckvtm = Ebc(nc.sbuf_tensor("ckvtm", [128, 34 * (S // 128)], BF16))
    m2st = [Ebc(nc.sbuf_tensor(f"m2st{i}", [64, D], BF16)) for i in range(4)]
    ctx_ab = ExitStack()
    Eab = ctx_ab.enter_context
    wo_s = Eab(nc.sbuf_tensor("wo_s", [128, NCH * D], BF16))
    wuf = Eab(nc.sbuf_tensor("wuf", [DC, H * DK], F32))
    wmod_t = [Eab(nc.sbuf_tensor(f"wmod_t{i}", [128, NCH * 512], BF16)) for i in range(2)]
    xoth = [Eab(nc.sbuf_tensor("xoth0", [128, NCH * 512], F32))] * 2
    xbf = [Eab(nc.sbuf_tensor(f"xbf{i}", [128, 512], BF16)) for i in range(2)]
    xsq = [Eab(nc.sbuf_tensor(f"xsq{i}", [128, 512], BF16)) for i in range(2)]
    hbf = [Eab(nc.sbuf_tensor(f"hbf{i}", [128, NCH * 512], BF16)) for i in range(2)]

    psall = E(nc.psum_tensor("psall", [128, 8 * 512], F32))

    def psv(k, n=1):
        return psall[:, k * 512:(k + n) * 512]

    ps_mod = psv(0)
    ps_a = psv(1)
    ps_b = psv(2)
    ps_c = psv(3)
    ps_d = psv(4)
    ps_e = psv(5)
    ps_f = psv(6)
    ps_g = psv(6)

    st_small = P.stream("sm", 2)
    st_gp = P.stream("gp", 3)
    st_gp2 = P.stream("g2", 3)
    st_w1 = P.stream("w1", 3)
    st_w2 = P.stream("w2", 3)
    st_wb = P.stream("wb", 2)
    st_pp = P.stream("pp", 2)
    st_x = P.stream("x", 3)
    st_out = P.stream("o", 2)

    def r3(ap, p=128):
        return ap.rearrange("(c p) m -> p c m", p=p)

    P.op("dve", lambda v: v.memset(cst[:, 0:1], EPS), w=["cst"])
    P.op("dve", lambda v: v.memset(cst[:, 1:2], SCALE), w=["cst_s"])
    P.op("dve", lambda v: v.memset(ones_c_f[:, :], 1.0), w=["ones_c_f"])
    P.op("dve", lambda v: v.memset(ones_c_b[:, :], 1.0), w=["ones_c_b"])
    P.op("dve", lambda v: v.memset(ones128[:, :], 1.0), w=["ones128"])
    P.op("dve", lambda v: v.memset(ones_r_f[:, :], 1.0), w=["ones_r_f"])
    P.op("dve", lambda v: v.memset(ones_r_b[:, :], 1.0), w=["ones_r_b"])
    P.op("dve", lambda v: v.memset(
        ckvtm[:, :].rearrange("p (n c) -> p n c", c=34)[:, :, 32:34], 1.0),
        w=["ckvtm_ones"])
    P.op("dve", lambda v: v.memset(kt[64:128, :], 0.0), w=["ktz_"], tag="ktz_pre")
    P.op("dve", lambda v: v.memset(cqs[32:64, :], 0.0), w=["cqz"], tag="cqz1")
    P.op("dve", lambda v: v.memset(cqs[64:128, :], 0.0), r=["cqz"], w=["cqz"],
         tag="cqz2")
    P.op("dve", lambda v: v.memset(bst[32:64, :], 0.0), w=["bstz"], tag="bstz1")
    P.op("dve", lambda v: v.memset(bst[64:128, :], 0.0), r=["bstz"], w=["bstz"],
         tag="bstz2")
    P.op("dve", lambda v: v.memset(kt[64:65, :], 1.0), r=["ktz_"], w=["kt_ones"])

    for dst, src, nm in [
        (bdq_s[:, :], bdq, "bdq_s"), (bdkv_s[:, :], bdkv, "bdkv_s"),
        (bkr_s[:, :], bkr, "bkr_s"), (bqr_s[:, :], bqr, "bqr_s"),
        (bo_s[:, :], bo, "bo_s"), (cfm_s[:, :], cfm, "cfm_s"),
        (geb_s[:, :], geb, "geb_s"), (eye_f[:, :], eye, "eye_f"),
        (gwt_s[:, :].rearrange("p (c e) -> p c e", e=12), r3(gwt), "gwt_s"),
        (sel_s[:, :], sel, "sel_s"),
    ]:
        P.dma(st_small, "sp", (lambda d, s: lambda e: e.dma_start(out=d, in_=s))(dst, src),
              w=[nm], tag="ld_" + nm)

    for dst, src, nm in [
        (wdq_s[:, :].rearrange("p (c m) -> p c m", m=DC), r3(wdq), "wdq_s"),
        (wdkv_s[:, :].rearrange("p (c m) -> p c m", m=DC), r3(wdkv), "wdkv_s"),
        (wkr_s[:, :].rearrange("p (c m) -> p c m", m=DR), r3(wkr), "wkr_s"),
        (wqr_s[:, :], wqr, "wqr_s"),
        (buq_b[:, :], buq, "buq_b"), (buv_b[:, :], buv, "buv_b"),
        (bdkvr_s[:, :], bdkvr, "bdkvr_s"), (eye_b[:, :], eye, "eye_b"),
        (wo_s[:, :].rearrange("p (c n) -> p c n", n=D), r3(wo), "wo_s"),
    ]:
        P.dma(st_gp, "pool", (lambda d, s: lambda e: e.dma_start(out=d, in_=s))(dst, src),
              w=[nm], tag="cld_" + nm)

    P.op("act", lambda s: s.activation(silu_b[:, :], cfm_s[:, :], AF.Silu),
         r=["cfm_s"], w=["silu_b"], tag="siluc")

    for sl in range(12):
        par = sl % 2
        wt = wmod_t[par]
        src = wmod.rearrange("(c p) (s j) -> p c s j", p=128, j=512)[:, :, sl, :]
        P.dma(st_gp, "pool",
              (lambda d, s_: lambda e: e.dma_start(out=d, in_=s_))(
                  wt[:, :].rearrange("p (c j) -> p c j", j=512), src),
              w=[f"wmod_t{par}"], tag=f"ld_wmod{sl}")
        for mj in range(4):
            m = sl * 4 + mj
            for c in range(NCH):
                P.op("pe", (lambda wt_, c_, mj_, m_: lambda t: t.matmul(
                    ps_mod[:, m_:m_ + 1],
                    wt_[:, :].rearrange("p (c j) -> p c j", j=512)[:, c_, mj_ * 128:(mj_ + 1) * 128],
                    silu_b[:, c_:c_ + 1],
                    start=(c_ == 0), stop=(c_ == NCH - 1)))(wt, c, mj, m),
                    r=[f"wmod_t{par}", "silu_b"],
                    w=["psK0"] if (c == NCH - 1 and m == 47) else
                      (["psK0a"] if (c == NCH - 1 and m == 23) else ()),
                    war=["psK0"] if (c == 0 and m == 0) else (),
                    tag=f"mm_mod{m}_{c}")
        if sl == 5:
            P.op("act", lambda s: s.activation(mods[:, 0:24], ps_mod[:, 0:24],
                                               AF.Identity),
                 r=["psK0a"], w=["mods"], tag="ev_mod_a")
    P.op("act", lambda s: s.activation(mods[:, 24:48], ps_mod[:, 24:48],
                                       AF.Identity),
         r=["psK0"], w=["mods_b"], tag="ev_mod")

    P.op("dve", lambda v: v.tensor_scalar_add(c1a[:, :], mods[:, 8:16], 1.0),
         r=["mods"], w=["c1a"], tag="c1a")
    P.op("dve", lambda v: v.tensor_copy(sav[:, :], mods[:, 0:8]), r=["mods"], w=["sav"],
         tag="sav")
    P.op("dve", lambda v: v.tensor_scalar_add(c1b[:, :], mods[:, 32:40], 1.0),
         r=["mods_b"], w=["c1b"], tag="c1b")
    P.op("dve", lambda v: v.tensor_copy(sbv[:, :], mods[:, 24:32]), r=["mods_b"],
         w=["sbv"], tag="sbv")

    def clamp_gate(dst, lo, nm, src="mods"):
        P.op("dve", (lambda lo_: lambda v: v.tensor_scalar(
            gtmp[:, 0:8], mods[:, lo_:lo_ + 8], 0.0, None, op0=ALU.is_ge))(lo),
            r=[src], w=["g_ge"], tag=nm + "_ge")
        P.op("dve", lambda v: v.tensor_scalar(gtmp[:, 8:16], gtmp[:, 0:8],
                                              2.0, -1.0, op0=ALU.mult, op1=ALU.add),
             r=["g_ge"], w=["g_sgn"], tag=nm + "_sgn")
        P.op("act", (lambda lo_: lambda s: s.activation(
            gtmp[:, 16:24], mods[:, lo_:lo_ + 8], AF.Abs))(lo),
            r=[src], w=["g_mag"], tag=nm + "_abs")
        P.op("dve", lambda v: v.tensor_scalar(gtmp[:, 16:24], gtmp[:, 16:24],
                                              0.25, None, op0=ALU.max),
             r=["g_mag"], w=["g_mag"], tag=nm + "_mag")
        P.op("dve", (lambda dst_: lambda v: v.tensor_tensor(
            dst_[:, :], gtmp[:, 8:16], gtmp[:, 16:24], op=ALU.mult))(dst),
            r=["g_sgn", "g_mag"], w=[nm], tag=nm + "_mul")

    clamp_gate(gav, 16, "gav")
    clamp_gate(gbv, 40, "gbv", src="mods_b")
    P.op("dve", lambda v: v.reciprocal(invgb[:, :], gbv[:, :]), r=["gbv"], w=["invgb"],
         tag="invgb")
    P.op("dve", lambda v: v.tensor_scalar_mul(bqr_sc[:, :], bqr_s[:, :], SCALE),
         r=["bqr_s"], w=["bqr_sc"], tag="bqr_sc")

    for srcdram, dstT, dn in [(wuq, wuqT, "wuqT"), (wuk, wukT, "wukT"),
                              (wuv, wuvT, "wuvT")]:
        P.dma(st_small, "sp", (lambda s_: lambda e: e.dma_start(
            out=wuf[:, :], in_=s_))(srcdram), w=["wuf"], tag=f"ld_wuf_{dn}")
        for h in range(H):
            P.op("pe", (lambda h_: lambda t: t.transpose(
                psv(7)[:, 0:DC], wuf[:, h_ * DK:(h_ + 1) * DK], eye_f[0:DC, 0:DC]))(h),
                r=["wuf", "eye_f"], w=["psK7"], war=["psK7"],
                tag=f"tr_{dn}{h}")
            P.op("act", (lambda dstT_, h_: lambda s: s.activation(
                dstT_[:, h_ * DC:(h_ + 1) * DC], psv(7)[:, 0:DC], AF.Identity))(dstT, h),
                r=["psK7"], w=[f"{dn}{h}"], tag=f"ev_{dn}{h}")
    for h in range(H):
        P.op("pe", (lambda h_: lambda t: t.matmul(
            ps_a[0:DC, 0:DC], wuqT[:, h_ * DC:(h_ + 1) * DC],
            wukT[:, h_ * DC:(h_ + 1) * DC], start=True, stop=True))(h),
            r=[f"wuqT{h}", f"wukT{h}"], w=["psK1"], war=["psK1"], tag=f"mm_A{h}")
        P.op("act", (lambda h_: lambda s: s.activation(
            bst[0:DC, (h_ // 2) * 128 + (h_ % 2) * DC:
                (h_ // 2) * 128 + (h_ % 2) * DC + DC],
            ps_a[0:DC, 0:DC], AF.Identity))(h),
            r=["psK1"], w=[f"bstA{h}"], tag=f"ev_A{h}")
        P.op("pe", (lambda h_: lambda t: t.matmul(
            ps_b[0:DC, 0:1], wukT[:, h_ * DC:(h_ + 1) * DC],
            buq_b[:, h_:h_ + 1], start=True, stop=True))(h),
            r=[f"wukT{h}", "buq_b"], w=["psK2"], war=["psK2"], tag=f"mm_u1{h}")
        P.op("act", (lambda h_: lambda s: s.activation(
            u1s[:, h_:h_ + 1], ps_b[0:DC, 0:1], AF.Identity,
            scale=cst[0:DC, 1:2]))(h),
            r=["psK2", "cst_s"], w=[f"u1s{h}"], tag=f"ev_u1{h}")
        stk, row = h // 2, (h % 2) * 32
        for half in range(2):
            P.op("pe", (lambda h_, half_: lambda t: t.matmul(
                ps_c[0:DC, :], wuvT[:, h_ * DC:(h_ + 1) * DC],
                wo_s[:, :].rearrange("p (c n) -> p c n", n=D)[:, h_, half_ * 512:(half_ + 1) * 512],
                start=True, stop=True))(h, half),
                r=[f"wuvT{h}", "wo_s"], w=["psK3"], war=["psK3"], tag=f"mm_M2{h}_{half}")
            P.op("act", (lambda h_, half_, stk_, row_: lambda s: s.activation(
                m2st[stk_][row_:row_ + 32, half_ * 512:(half_ + 1) * 512],
                ps_c[0:DC, :], AF.Identity))(h, half, stk, row),
                r=["psK3"], w=[f"m2st{stk}_{row}_{half}"], tag=f"ev_M2{h}_{half}")
    for c in range(NCH):
        for h in range(H):
            P.op("pe", (lambda c_, h_: lambda t: t.matmul(
                ps_d[:, c_:c_ + 1],
                wo_s[:, :].rearrange("p (c n) -> p c n", n=D)[:, h_, c_ * 128:(c_ + 1) * 128],
                buv_b[:, h_:h_ + 1], start=(h_ == 0), stop=(h_ == H - 1)))(c, h),
                r=["wo_s", "buv_b"],
                w=["psK4"] if (h == H - 1 and c == NCH - 1) else (),
                war=["psK4"] if (h == 0 and c == 0) else (),
                tag=f"mm_bv{c}_{h}")
    P.op("dve", lambda v: v.tensor_tensor(battn[:, :], ps_d[:, 0:NCH], bo_s[:, :],
                                          op=ALU.add),
         r=["psK4", "bo_s"], w=["battn"], tag="battn")
    P.op("dve", lambda v: v.tensor_tensor(gab[:, :], gav[:, :], battn[:, :], op=ALU.mult),
         r=["gav", "battn"], w=["gab"], tag="gab")
    for h in range(H):
        P.op("dve", (lambda h_: lambda v: v.tensor_copy(
            bst[0:DC, (h_ // 2) * 128 + 64 + (h_ % 2) * DR:
                (h_ // 2) * 128 + 64 + (h_ % 2) * DR + DR],
            wqr_s[:, h_ * DR:(h_ + 1) * DR]))(h),
            r=["wqr_s"], w=[f"bstR{h}"], tag=f"bst_r{h}")

    for i in range(8):
        own = i < 4
        par = i % 2
        t0 = i * 512
        if own:
            xview = xres[:, :].rearrange("p (c t) -> p c t", t=T)[:, :, t0:t0 + 512]
            xbufn = f"xres_t{i}"
        else:
            xview = xoth[par][:, :].rearrange("p (c t) -> p c t", t=512)
            xbufn = "xoth"
        P.dma(st_x, "sp", (lambda d, s_: lambda e: e.dma_start(out=d, in_=s_))(
            xview, r3(xfm)[:, :, t0:t0 + 512]), w=[xbufn], tag=f"ld_x{i}")

        for c in range(NCH):
            pc = (i * NCH + c) % 2
            P.op("dve", (lambda xr, c_, pc_: lambda v: v.tensor_copy(
                xbf[pc_][:, :], xr[:, c_, :]))(xview, c, pc),
                r=[xbufn], w=[f"xbf{pc}"], tag=f"cast_x{i}_{c}")
            P.op("act", (lambda xr, c_, pc_: lambda s: s.activation(
                xsq[pc_][:, :], xr[:, c_, :], AF.Square))(xview, c, pc),
                r=[xbufn], w=[f"xsq{pc}"], tag=f"sq_x{i}_{c}")
            P.op("pe", (lambda c_, pc_: lambda t: t.matmul(
                ps_e[:, :], ones128[:, :], xbf[pc_][:, :],
                start=(c_ == 0), stop=(c_ == NCH - 1)))(c, pc),
                r=[f"xbf{pc}", "ones128"],
                w=["psK5"] if c == NCH - 1 else (), war=["psK5"] if c == 0 else (),
                tag=f"mm_sum{i}_{c}")
            P.op("pe", (lambda c_, pc_: lambda t: t.matmul(
                ps_f[:, :], ones128[:, :], xsq[pc_][:, :],
                start=(c_ == 0), stop=(c_ == NCH - 1)))(c, pc),
                r=[f"xsq{pc}", "ones128"],
                w=["psK6"] if c == NCH - 1 else (), war=["psK6"] if c == 0 else (),
                tag=f"mm_sumsq{i}_{c}")
        P.op("dve", (lambda par_: lambda v: v.tensor_scalar_mul(
            mb_s[par_][:, :], ps_e[:, :], 1.0 / D))(par),
            r=["psK5"], w=[f"mb_s{par}"], tag=f"mean{i}")
        P.op("dve", (lambda par_: lambda v: v.tensor_scalar_mul(
            rb_s[par_][:, :], ps_f[:, :], 1.0 / D))(par),
            r=["psK6"], w=[f"rb_s{par}"], tag=f"msq{i}")
        P.op("dve", (lambda par_: lambda v: v.tensor_tensor(
            t1f[par_][:, :], mb_s[par_][:, :], mb_s[par_][:, :], op=ALU.mult))(par),
            r=[f"mb_s{par}"], w=[f"t1f{par}"], tag=f"m2_{i}")
        P.op("dve", (lambda par_: lambda v: v.tensor_tensor(
            rb_s[par_][:, :], rb_s[par_][:, :], t1f[par_][:, :], op=ALU.subtract))(par),
            r=[f"rb_s{par}", f"t1f{par}"], w=[f"rb_s{par}"], tag=f"var_{i}")
        P.op("act", (lambda par_: lambda s: s.activation(
            rb_s[par_][:, :], rb_s[par_][:, :], AF.Ln, bias=cst[:, 0:1]))(par),
            r=[f"rb_s{par}", "cst"], w=[f"rb_s{par}"], tag=f"lnv_{i}")
        P.op("act", (lambda par_: lambda s: s.activation(
            rb_s[par_][:, :], rb_s[par_][:, :], AF.Exp, scale=-0.5))(par),
            r=[f"rb_s{par}"], w=[f"rb_s{par}"], tag=f"rstd_{i}")
        for c in range(NCH):
            pc = (i * NCH + c) % 2
            P.op("pool", (lambda xr, c_, pc_, par_: lambda g: g.tensor_tensor(
                t1f[pc_][:, :], xr[:, c_, :], mb_s[par_][:, :], op=ALU.subtract))(
                    xview, c, pc, par),
                r=[xbufn, f"mb_s{par}"], w=[f"t1f{pc}"], tag=f"hn1_{i}_{c}")
            P.op("dve", (lambda pc_, par_: lambda v: v.tensor_tensor(
                t1f[pc_][:, :], t1f[pc_][:, :], rb_s[par_][:, :], op=ALU.mult))(pc, par),
                r=[f"t1f{pc}", f"rb_s{par}"], w=[f"t1f{pc}"], tag=f"hn2_{i}_{c}")
            P.op("act", (lambda c_, pc_, par_: lambda s: s.activation(
                hbf[par_][:, :].rearrange("p (c t) -> p c t", t=512)[:, c_, :],
                t1f[pc_][:, :], AF.Identity, scale=c1a[:, c_:c_ + 1],
                bias=sav[:, c_:c_ + 1]))(c, pc, par),
                r=[f"t1f{pc}", "c1a", "sav"], w=[f"hbf{par}_{c}"], tag=f"hmod_{i}_{c}")
        hview = hbf[par][:, :].rearrange("p (c t) -> p c t", t=512)
        wdkv_v = wdkv_s[:, :].rearrange("p (c m) -> p c m", m=DC)
        wkr_v = wkr_s[:, :].rearrange("p (c m) -> p c m", m=DR)
        wdq_v = wdq_s[:, :].rearrange("p (c m) -> p c m", m=DC)
        for c in range(NCH):
            P.op("pe", (lambda hv, c_: lambda t: t.matmul(
                ps_a[0:DC, :], wdkv_v[:, c_, :], hv[:, c_, :],
                start=(c_ == 0), stop=(c_ == NCH - 1)))(hview, c),
                r=[f"hbf{par}_{c}", "wdkv_s"],
                w=["psK1"] if c == NCH - 1 else (), war=["psK1"] if c == 0 else (),
                tag=f"mm_ckv{i}_{c}")
        P.op("act", (lambda t0_: lambda s: s.activation(
            kt[0:DC, t0_:t0_ + 512], ps_a[0:DC, :], AF.Identity, bias=bdkv_s[:, 0:1]))(t0),
            r=["psK1", "bdkv_s"], w=[f"kt_c{i}"], tag=f"ev_ckv{i}")
        for c in range(NCH):
            P.op("pe", (lambda hv, c_: lambda t: t.matmul(
                ps_b[0:DR, :], wkr_v[:, c_, :], hv[:, c_, :],
                start=(c_ == 0), stop=(c_ == NCH - 1)))(hview, c),
                r=[f"hbf{par}_{c}", "wkr_s"],
                w=["psK2"] if c == NCH - 1 else (), war=["psK2"] if c == 0 else (),
                tag=f"mm_kr{i}_{c}")
        P.op("act", (lambda t0_: lambda s: s.activation(
            kt[32:64, t0_:t0_ + 512], ps_b[0:DR, :], AF.Identity, bias=bkr_s[:, 0:1]))(t0),
            r=["psK2", "bkr_s"], w=[f"kt_r{i}"], tag=f"ev_kr{i}")
        for tc in range(4):
            kvch = i * 4 + tc
            for c in range(NCH):
                P.op("pe", (lambda hv, c_, tc_: lambda t: t.matmul(
                    ps_c[:, 0:DC], hv[:, c_, tc_ * 128:(tc_ + 1) * 128],
                    wdkv_v[:, c_, :], start=(c_ == 0), stop=False))(hview, c, tc),
                    r=[f"hbf{par}_{c}", "wdkv_s"], war=["psK3"] if c == 0 else (),
                    tag=f"mm_ctm{i}_{tc}_{c}")
            P.op("pe", lambda t: t.matmul(ps_c[:, 0:DC], ones_r_b[:, :], bdkvr_s[:, :],
                                          start=False, stop=True),
                 r=["ones_r_b", "bdkvr_s"], w=["psK3"], tag=f"mm_ctmb{i}_{tc}")
            P.op("act", (lambda kvch_: lambda s: s.activation(
                ckvtm[:, kvch_ * 34:kvch_ * 34 + 32], ps_c[:, 0:DC], AF.Identity))(kvch),
                r=["psK3"], w=[f"ckvtm{kvch}"], tag=f"ev_ctm{kvch}")
        if own:
            for c in range(NCH):
                P.op("pe", (lambda hv, c_: lambda t: t.matmul(
                    ps_d[0:DC, :], wdq_v[:, c_, :], hv[:, c_, :],
                    start=(c_ == 0), stop=(c_ == NCH - 1)))(hview, c),
                    r=[f"hbf{par}_{c}", "wdq_s"],
                    w=["psK4"] if c == NCH - 1 else (), war=["psK4"] if c == 0 else (),
                    tag=f"mm_cq{i}_{c}")
            P.op("act", (lambda t0_: lambda s: s.activation(
                cqs[0:DC, t0_:t0_ + 512], ps_d[0:DC, :], AF.Identity,
                bias=bdq_s[:, 0:1]))(t0),
                r=["psK4", "bdq_s"], w=[f"cq{i}"], tag=f"ev_cq{i}")

    for c8 in range(8):
        P.op("act", (lambda c_: lambda s: s.activation(
            sqk[:, 0:512], kt[0:64, c_ * 512:(c_ + 1) * 512], AF.Square))(c8),
            r=[f"kt_c{c8}", f"kt_r{c8}"], w=["sqk"], tag=f"sq_k{c8}")
        P.op("pe", lambda t: t.matmul(ps_e[0:1, :], ones_c_f[0:64, :], sqk[:, 0:512],
                                      start=True, stop=True),
             r=["sqk", "ones_c_f"], w=["psK5"], tag=f"mm_n2k{c8}")
        P.op("dve", (lambda c_: lambda v: v.reduce_max(
            mxk[0:1, c_:c_ + 1], ps_e[0:1, :], axis=AX.X))(c8),
            r=["psK5"], w=[f"mxk{c8}"], tag=f"max_k{c8}")
    P.op("dve", lambda v: v.reduce_max(mxk[0:1, 8:9], mxk[0:1, 0:8], axis=AX.X),
         r=[f"mxk{c}" for c in range(8)], w=["mxk8"], tag="maxk_all")
    P.op("act", lambda s: s.activation(mxk[0:1, 8:9], mxk[0:1, 8:9], AF.Sqrt),
         r=["mxk8"], w=["mxk8"], tag="maxk_sqrt")

    ctx_ab.close()
    ctx_c = ExitStack()
    Ec = ctx_c.enter_context
    qt_s = Ec(nc.sbuf_tensor("qt_s", [128, T], BF16))
    qt_s2 = Ec(nc.sbuf_tensor("qt_s2", [128, T], BF16))
    qt_s3 = Ec(nc.sbuf_tensor("qt_s3", [128, T], BF16))
    qt_s4 = Ec(nc.sbuf_tensor("qt_s4", [128, T], BF16))
    pt_s = [Ec(nc.sbuf_tensor(f"pt_s{i}", [128, 1024], BF16)) for i in range(2)]
    zst = [Ec(nc.sbuf_tensor(f"zst{i}", [64, T], BF16)) for i in range(4)]
    dinv_bc = Ec(nc.sbuf_tensor("dinv_bc", [128, 512], F32))
    zraw = Ec(nc.sbuf_tensor("zraw", [128, 512], F32))
    P.set_barrier()
    P.op("dve", lambda v: v.memset(qt_s[64:128, :], 0.0), w=["qtA_z"], tag="qz_A")
    P.op("dve", lambda v: v.memset(qt_s2[64:128, :], 0.0), w=["qtB_z"], tag="qz_B")
    P.op("dve", lambda v: v.memset(qt_s3[64:128, :], 0.0), w=["qtA_z"], tag="qz_A2")
    P.op("dve", lambda v: v.memset(qt_s4[64:128, :], 0.0), w=["qtB_z"], tag="qz_B2")

    qts2 = [[qt_s, qt_s2], [qt_s3, qt_s4]]
    negm_ = [negm, negm2]

    def build_steps(hp, dense=False):
        pp = hp % 2
        bq = 6 if dense else 7
        steps = []
        bstr = ([f"bstA{2 * hp}", f"bstA{2 * hp + 1}",
                 f"bstR{2 * hp}", f"bstR{2 * hp + 1}", "bstz", "cqz"])
        for qt in range(4):
            q0 = qt * 512
            bqq = (6 + qt % 2) if dense else 7
            steps.append((lambda hp=hp, q0=q0, qt=qt, bqq=bqq: P.op(
                "pe", (lambda hp_, q0_: lambda t: t.matmul(
                    psv(bqq), bst[:, hp_ * 128:(hp_ + 1) * 128],
                    cqs[:, q0_:q0_ + 512], start=True, stop=True))(hp, q0),
                r=bstr + [f"cq{qt}"], w=[f"psK{bqq}"], war=[f"psK{bqq}"],
                tag=f"mm_q{hp}_{qt}")))
            for hx in range(2):
                h = 2 * hp + hx
                qtt = qts2[pp][hx]
                pfx = "AB"[hx] + str(pp)
                steps.append((lambda h=h, q0=q0, qt=qt, qtt=qtt, pfx=pfx,
                              hx=hx, bqq=bqq: P.op(
                    "act", (lambda qtt_, h_, q0_, hx_: lambda s: s.activation(
                        qtt_[0:DC, q0_:q0_ + 512],
                        psv(bqq)[hx_ * DC:(hx_ + 1) * DC, :], AF.Identity,
                        scale=cst[0:DC, 1:2], bias=u1s[:, h_:h_ + 1]))(qtt, h, q0, hx),
                    r=[f"psK{bqq}", "cst_s", f"u1s{h}"], w=[f"qt{pfx}_c{qt}"],
                    tag=f"ev_qc{h}_{qt}")))
                steps.append((lambda h=h, q0=q0, qt=qt, qtt=qtt, pfx=pfx,
                              hx=hx, bqq=bqq: P.op(
                    "act", (lambda qtt_, h_, q0_, hx_: lambda s: s.activation(
                        qtt_[32:64, q0_:q0_ + 512],
                        psv(bqq)[64 + hx_ * DR:64 + (hx_ + 1) * DR, :], AF.Identity,
                        scale=cst[0:DR, 1:2], bias=bqr_sc[:, h_:h_ + 1]))(qtt, h, q0, hx),
                    r=[f"psK{bqq}", "cst_s", "bqr_sc"], w=[f"qt{pfx}_r{qt}"],
                    tag=f"ev_qr{h}_{qt}")))

        for hx in range(2):
            h = 2 * hp + hx
            qtt = qts2[pp][hx]
            pfx = "AB"[hx] + str(pp)
            steps.append((lambda h=h, qtt=qtt, pfx=pfx: P.op(
                "act", (lambda qtt_: lambda s: s.activation(
                    sqk[:, :], qtt_[0:64, :], AF.Square))(qtt),
                r=[f"qt{pfx}_c{q}" for q in range(4)] +
                  [f"qt{pfx}_r{q}" for q in range(4)],
                w=["sqk"], tag=f"sqq{h}")))
            for qt in range(4):
                nb = 6 + qt % 2 if dense else 7
                steps.append((lambda h=h, qt=qt, nb=nb: P.op(
                    "pe", (lambda qt_, nb_: lambda t: t.matmul(
                        psv(nb_)[0:1, :], ones_c_f[0:64, :],
                        sqk[:, qt_ * 512:(qt_ + 1) * 512], start=True, stop=True))(qt, nb),
                    r=["sqk", "ones_c_f"], w=[f"psK{nb}"], war=[f"psK{nb}"],
                    tag=f"mm_n2q{h}_{qt}")))
                steps.append((lambda h=h, qt=qt, nb=nb, hx=hx: P.op(
                    "dve", (lambda nb_, hx_, qt_: lambda v: v.reduce_max(
                        mxk[0:1, 9 + hx_ * 4 + qt_:10 + hx_ * 4 + qt_],
                        psv(nb_)[0:1, :], axis=AX.X))(nb, hx, qt),
                    r=[f"psK{nb}"], w=[f"mxq{hx}_{qt}"], tag=f"mxq{h}_{qt}")))

        steps.append((lambda hp=hp: P.op(
            "dve", lambda v: v.reduce_max(mxk[0:1, 18:19], mxk[0:1, 9:17],
                                          axis=AX.X),
            r=[f"mxq{hx_}_{qt_}" for hx_ in range(2) for qt_ in range(4)],
            w=["mxp"], tag=f"mxp{hp}")))
        steps.append((lambda hp=hp: P.op(
            "act", lambda s: s.activation(mxk[0:1, 18:19], mxk[0:1, 18:19],
                                          AF.Sqrt),
            r=["mxp"], w=["mxp"], tag=f"mxps{hp}")))
        steps.append((lambda hp=hp: P.op(
            "dve", lambda v: v.tensor_scalar(mxk[0:1, 18:19], mxk[0:1, 18:19],
                                             mxk[0:1, 8:9], -1.0,
                                             op0=ALU.mult, op1=ALU.mult),
            r=["mxp", "mxk8"], w=["mxp"], tag=f"mxpn{hp}")))
        steps.append((lambda hp=hp: P.op(
            "pe", lambda t: t.matmul(psv(bq)[:, 0:1], ones_r_f[:, :],
                                     mxk[0:1, 18:19], start=True, stop=True),
            r=["mxp", "ones_r_f"], w=[f"psK{bq}"], war=[f"psK{bq}"],
            tag=f"bc_mp{hp}")))
        steps.append((lambda hp=hp, pp=pp: P.op(
            "act", (lambda pp_: lambda s: s.activation(
                negm_[pp_][:, :], psv(bq)[:, 0:1], AF.Identity))(pp),
            r=[f"psK{bq}"], w=[f"negm{pp}"], tag=f"ev_mp{hp}")))
        return steps

    BUILD_INTERLEAVE = True
    epi_defer = []
    for st_f in build_steps(0, dense=True):
        st_f()

    for hp in range(4):
        pp = hp % 2
        if hp == 1:
            for e_i in range(NE):
                P.dma(st_pp, "pool", (lambda e_: lambda g: g.dma_start(
                    out=w1c[e_], in_=w1t[e_]))(e_i),
                    w=[f"w1c_{e_i}"], tag=f"pp_w1_{e_i}")
                P.dma(st_pp, "pool", (lambda e_: lambda g: g.dma_start(
                    out=w2c[e_], in_=w2t[e_]))(e_i),
                    w=[f"w2c_{e_i}"], tag=f"pp_w2_{e_i}")
        nxt = build_steps(hp + 1) if hp + 1 < 4 else []
        if not BUILD_INTERLEAVE:
            for st_f in nxt:
                st_f()
            nxt = []
        cur = 0
        for qt in range(4):
            q0 = qt * 512
            qrd = [f"qt{p_}{pp}_{k_}{qt}" for p_ in "AB" for k_ in ("c", "r")]

            def emit_scores(c32):
                par = c32 % 2
                i8 = c32 // 4
                ktn = [f"kt_c{i8}", f"kt_r{i8}", "kt_ones"]
                for hx in range(2):
                    P.op("pe", (lambda qtt_, c_, q0_, b_: lambda t: t.matmul(
                        psv(b_), kt[:, c_ * 128:(c_ + 1) * 128],
                        qtt_[:, q0_:q0_ + 512], start=True, stop=True))(
                            qts2[pp][hx], c32, q0, 2 * par + hx),
                        r=ktn + qrd + ["ktz_", "qtA_z", "qtB_z"],
                        w=[f"psK{2 * par + hx}"],
                        war=[f"psK{2 * par + hx}"],
                        tag=f"mm_s{hp}_{qt}_{c32}_{hx}")
                P.op("act", (lambda b_, par_, pp_: lambda s: s.activation(
                    pt_s[par_][:, :], psv(b_, 2), AF.Exp,
                    bias=negm_[pp_][:, 0:1]))(2 * par, par, pp),
                    r=[f"psK{2 * par}", f"psK{2 * par + 1}", f"negm{pp}"],
                    w=[f"pt{par}"], tag=f"exp{hp}_{qt}_{c32}")

            def emit_pv(c32):
                par = c32 % 2
                P.op("pe", (lambda c_, par_: lambda t: t.matmul(
                    psall[0:33, 4 * 512:4 * 512 + 512],
                    ckvtm[:, c_ * 34:c_ * 34 + 33],
                    pt_s[par_][:, 0:512],
                    start=(c_ == 0), stop=(c_ == 31)))(c32, par),
                    r=[f"ckvtm{c32}", "ckvtm_ones", f"pt{par}"],
                    w=["psK4"] if c32 == 31 else (),
                    war=["psK4"] if c32 == 0 else (),
                    tag=f"mm_pv{hp}_{qt}_{c32}_0")
                P.op("pe", (lambda c_, par_: lambda t: t.matmul(
                    psall[64:97, 5 * 512:5 * 512 + 512],
                    ckvtm[:, c_ * 34:c_ * 34 + 33],
                    pt_s[par_][:, 512:1024],
                    start=(c_ == 0), stop=(c_ == 31),
                    tile_position=(0, 64)))(c32, par),
                    r=[f"ckvtm{c32}", "ckvtm_ones", f"pt{par}"],
                    w=["psK5"] if c32 == 31 else (),
                    war=["psK5"] if c32 == 0 else (),
                    tag=f"mm_pv{hp}_{qt}_{c32}_1")

            emit_scores(0)
            for c32 in range(32):
                if c32 < 31:
                    emit_scores(c32 + 1)
                emit_pv(c32)
                if epi_defer:
                    epi_defer.pop(0)()
                tgt = ((qt * 32 + c32 + 1) * len(nxt)) // 126
                while cur < min(tgt, len(nxt)):
                    nxt[cur]()
                    cur += 1
            for hx in range(2):
                h = 2 * hp + hx
                row = hx * 32
                zb = 4 + hx
                zrow = 32 if hx == 0 else 96
                zlo = 0 if hx == 0 else 64
                pq = hx
                rq, rqn = rowa_[pq], f"rowa{pq}"
                P.op("act", (lambda rq, zb_, zr_: lambda s: s.activation(
                    rq[:, :], psall[zr_:zr_ + 1, zb_ * 512:zb_ * 512 + 512],
                    AF.Ln))(rq, zb, zrow),
                    r=[f"psK{zb}"], w=[rqn], tag=f"dln{h}_{qt}")
                P.op("dve", (lambda zb_, zl_, hx_: lambda v: v.tensor_copy(
                    zraw[zl_:zl_ + 32, :],
                    psall[zl_:zl_ + 32, zb_ * 512:zb_ * 512 + 512]))(zb, zlo, hx),
                    r=[f"psK{zb}"], w=[f"zraw{hx}"], tag=f"zcp{h}_{qt}")
                epi_defer.append((lambda rq=rq, rqn=rqn, h=h, qt=qt: P.op(
                    "pe", (lambda rq_: lambda t: t.matmul(
                        psv(6), ones_r_f[:, :], rq_[:, :],
                        start=True, stop=True))(rq),
                    r=[rqn, "ones_r_f"], w=["psK6"], war=["psK6"],
                    tag=f"bc_d{h}_{qt}")))
                epi_defer.append((lambda h=h, qt=qt: P.op(
                    "act", lambda s: s.activation(dinv_bc[:, :], psv(6), AF.Exp,
                                                  scale=-1.0),
                    r=["psK6"], w=["dinv_bc"], tag=f"ev_d{h}_{qt}")))
                epi_defer.append((lambda hp=hp, row=row, qt=qt, zlo=zlo, hx=hx,
                                  h=h: P.op(
                    "dve", (lambda hp_, row_, qt_, zl_: lambda v: v.tensor_tensor(
                        zst[hp_][row_:row_ + 32, qt_ * 512:(qt_ + 1) * 512],
                        zraw[zl_:zl_ + 32, :], dinv_bc[zl_:zl_ + 32, :],
                        op=ALU.mult))(hp, row, qt, zlo),
                    r=[f"zraw{hx}", "dinv_bc"], w=[f"zst{hp}_{row}_{qt}"],
                    tag=f"ev_z{h}_{qt}")))

    for fn in epi_defer:
        fn()
    epi_defer = []

    for c in range(NCH):
        P.op("dve", (lambda c_: lambda v: v.tensor_scalar_add(
            xres[:, c_ * T:(c_ + 1) * T], xres[:, c_ * T:(c_ + 1) * T],
            gab[:, c_:c_ + 1]))(c),
            r=["gab"] + [f"xres_t{i}" for i in range(4)],
            w=[f"xg{c}"], tag=f"gabadd{c}")
    znames = [f"zst{st_}_{rw}_{q}" for st_ in range(4) for rw in (0, 32) for q in range(4)]
    for c in range(NCH):
        for qt in range(4):
            bank = qt % 2
            for sti in range(4):
                P.op("pe", (lambda c_, qt_, sti_, bank_: lambda t: t.matmul(
                    psv(bank_), m2st[sti_][:, c_ * 128:(c_ + 1) * 128],
                    zst[sti_][:, qt_ * 512:(qt_ + 1) * 512],
                    start=(sti_ == 0), stop=(sti_ == 3)))(c, qt, sti, bank),
                    r=[f"m2st{sti}_0_0", f"m2st{sti}_0_1", f"m2st{sti}_32_0",
                       f"m2st{sti}_32_1", f"zst{sti}_0_{qt}", f"zst{sti}_32_{qt}"],
                    w=[f"psK{bank}"] if sti == 3 else (),
                    war=[f"psK{bank}"] if sti == 0 else (),
                    tag=f"mm_ao{c}_{qt}_{sti}")
            P.op("dve", (lambda c_, qt_, bank_: lambda v: v.scalar_tensor_tensor(
                xres[:, c_ * T + qt_ * 512:c_ * T + (qt_ + 1) * 512],
                psv(bank_), gav[:, c_:c_ + 1],
                xres[:, c_ * T + qt_ * 512:c_ * T + (qt_ + 1) * 512],
                op0=ALU.mult, op1=ALU.add))(c, qt, bank),
                r=[f"psK{bank}", "gav", f"xg{c}"], w=[f"xg{c}"],
                tag=f"res_ao{c}_{qt}")

    dbg_pre_free(P, nc, dbg, kt, cqs, ckvtm, st_out)
    ctx_c.close()
    ctx_bc.close()
    ytmp = t1f[0]
    ctx_e = ExitStack()
    Ee = ctx_e.enter_context
    h2b = [Ee(nc.sbuf_tensor(f"h2b{i}", [128, NCH * TH], BF16)) for i in range(2)]
    w1b = [Ee(nc.sbuf_tensor(f"w1b{i}", [128, NCH * 512], BF16)) for i in range(2)]
    w2b = [Ee(nc.sbuf_tensor(f"w2b{i}", [128, 4 * 1024], BF16)) for i in range(2)]
    heb = [Ee(nc.sbuf_tensor(f"heb{i}", [128, 4 * TH], BF16)) for i in range(2)]
    yac = Ee(nc.sbuf_tensor("yac", [128, NCH * TH], F32))
    t2f = t1f
    xb2 = [Ee(nc.sbuf_tensor(f"xb2{i}", [128, TH], BF16)) for i in range(2)]
    xs2 = [Ee(nc.sbuf_tensor(f"xs2{i}", [128, TH], BF16)) for i in range(2)]
    wfm_ = [Ee(nc.sbuf_tensor(f"wfm{i}", [16, TH], F32)) for i in range(2)]
    wt16 = Ee(nc.sbuf_tensor("wt16", [128, 64], F32))
    glog = Ee(nc.sbuf_tensor("glog", [128, 48], F32))
    gsc = Ee(nc.sbuf_tensor("gsc", [128, 192], F32))
    wzb_ = [Ee(nc.sbuf_tensor(f"wzb{i}", [128, TH], BF16)) for i in range(2)]
    wbb = Ee(nc.sbuf_tensor("wbb", [128, TH], F32))
    h2f = Ee(nc.sbuf_tensor("h2f", [128, 4 * TH], F32))
    P.set_barrier()
    P.op("dve", lambda v: v.memset(wt16[:, :], 0.0), w=["wt16z"], tag="z_wt16")
    if DEBUG_OUTS:
        P.dma(st_out, "sp", lambda e: e.dma_start(out=dbg["xa"], in_=xres[:, :]),
              r=[f"xg{c}" for c in range(NCH)], w=["od_xa"], tag="st_xa")

    GSL = 108

    def prep_stage(th, stage):
        pt = th % 2
        q0 = th * TH
        xv = xres[:, :].rearrange("p (c t) -> p c t", t=T)[:, :, q0:q0 + TH]
        h2v = h2b[pt][:, :].rearrange("p (c t) -> p c t", t=TH)
        rowa, rowb = rowa_[pt], rowb_[pt]
        rowc = rowc_[pt]
        ran, rbn, rcn = f"rowa{pt}", f"rowb{pt}", f"rowc{pt}"
        wfm = wfm_[pt]
        if stage == 0:
            for c in range(NCH):
                pcx = c % 2
                P.op("dve", (lambda xv_, c_, pc_: lambda v: v.tensor_copy(
                    xb2[pc_][:, :], xv_[:, c_, :]))(xv, c, pcx),
                    r=[f"xg{c}"], w=[f"xb2{pcx}"], tag=f"cast2_{th}_{c}")
                P.op("act", (lambda xv_, c_, pc_: lambda s: s.activation(
                    xs2[pc_][:, :], xv_[:, c_, :], AF.Square))(xv, c, pcx),
                    r=[f"xg{c}"], w=[f"xs2{pcx}"], tag=f"sq2_{th}_{c}")
                P.op("pe", (lambda c_, pc_: lambda t: t.matmul(
                    psv(4), ones128[:, :], xb2[pc_][:, :],
                    start=(c_ == 0), stop=(c_ == NCH - 1)))(c, pcx),
                    r=[f"xb2{pcx}", "ones128"],
                    w=["psK4"] if c == NCH - 1 else (), war=["psK4"] if c == 0 else (),
                    tag=f"mm2_sum{th}_{c}")
                P.op("pe", (lambda c_, pc_: lambda t: t.matmul(
                    psv(5), ones128[:, :], xs2[pc_][:, :],
                    start=(c_ == 0), stop=(c_ == NCH - 1)))(c, pcx),
                    r=[f"xs2{pcx}", "ones128"],
                    w=["psK5"] if c == NCH - 1 else (), war=["psK5"] if c == 0 else (),
                    tag=f"mm2_sumsq{th}_{c}")
        elif stage == 1:
            P.op("dve", (lambda p_: lambda v: v.tensor_scalar_mul(
                mb_s[p_][:, :], psv(4), 1.0 / D))(pt),
                r=["psK4"], w=[f"mb_s{pt}"], tag=f"mean2_{th}")
            P.op("dve", (lambda p_: lambda v: v.tensor_scalar_mul(
                rb_s[p_][:, :], psv(5), 1.0 / D))(pt),
                r=["psK5"], w=[f"rb_s{pt}"], tag=f"msq2_{th}")
            P.op("dve", (lambda p_: lambda v: v.tensor_tensor(
                t1f[p_][:, :], mb_s[p_][:, :], mb_s[p_][:, :], op=ALU.mult))(pt),
                r=[f"mb_s{pt}"], w=[f"t1f{pt}"], tag=f"m2sq_{th}")
            P.op("dve", (lambda p_: lambda v: v.tensor_tensor(
                rb_s[p_][:, :], rb_s[p_][:, :], t1f[p_][:, :], op=ALU.subtract))(pt),
                r=[f"rb_s{pt}", f"t1f{pt}"], w=[f"rb_s{pt}"], tag=f"var2_{th}")
            P.op("act", (lambda p_: lambda s: s.activation(
                rb_s[p_][:, :], rb_s[p_][:, :], AF.Ln, bias=cst[:, 0:1]))(pt),
                r=[f"rb_s{pt}", "cst"], w=[f"rb_s{pt}"], tag=f"lnv2_{th}")
            P.op("act", (lambda p_: lambda s: s.activation(
                rb_s[p_][:, :], rb_s[p_][:, :], AF.Exp, scale=-0.5))(pt),
                r=[f"rb_s{pt}"], w=[f"rb_s{pt}"], tag=f"rstd2_{th}")
        elif stage in ("2a", "2b"):
            crange = range(0, 4) if stage == "2a" else range(4, 8)
            for c in crange:
                pc = c % 2
                P.op("pool", (lambda xv_, c_, pc_, p_: lambda g: g.tensor_tensor(
                    t1f[pc_][:, :], xv_[:, c_, :], mb_s[p_][:, :], op=ALU.subtract))(
                        xv, c, pc, pt),
                    r=[f"xg{c}", f"mb_s{pt}"], w=[f"t1f{pc}"], tag=f"h2n1_{th}_{c}")
                P.op("dve", (lambda pc_, p_: lambda v: v.tensor_tensor(
                    t1f[pc_][:, :], t1f[pc_][:, :], rb_s[p_][:, :], op=ALU.mult))(pc, pt),
                    r=[f"t1f{pc}", f"rb_s{pt}"], w=[f"t1f{pc}"], tag=f"h2n2_{th}_{c}")
                P.op("dve", (lambda c_, pc_: lambda v: v.tensor_scalar(
                    h2f[:, (c_ % 4) * TH:(c_ % 4 + 1) * TH], t1f[pc_][:, :],
                    c1b[:, c_:c_ + 1], sbv[:, c_:c_ + 1],
                    op0=ALU.mult, op1=ALU.add))(c, pc),
                    r=[f"t1f{pc}", "c1b", "sbv"], w=[f"h2fs_{c % 4}"],
                    tag=f"h2mod_{th}_{c}")
                P.op("act", (lambda h2v_, c_: lambda s: s.activation(
                    h2v_[:, c_, :], h2f[:, (c_ % 4) * TH:(c_ % 4 + 1) * TH],
                    AF.Identity))(h2v, c),
                    r=[f"h2fs_{c % 4}"], w=[f"h2{pt}_{c}"], tag=f"h2cast_{th}_{c}")
        elif stage in ("25a", "25b"):
            crange = range(0, 4) if stage == "25a" else range(4, 8)
            for c in crange:
                for tc in range(4):
                    P.op("pe", (lambda c_, tc_: lambda t: t.matmul(
                        psv(4)[:, tc_ * GSL + c_ * 12:tc_ * GSL + (c_ + 1) * 12],
                        h2f[:, (c_ % 4) * TH + tc_ * 128:(c_ % 4) * TH + (tc_ + 1) * 128],
                        gwt_s[:, :].rearrange("p (c e) -> p c e", e=12)[:, c_, :],
                        start=True, stop=True))(c, tc),
                        r=[f"h2fs_{c % 4}", "gwt_s"],
                        w=["psK4"] if (c == NCH - 1 and tc == 3) else (),
                        war=["psK4"] if (c == 0 and tc == 0) else (),
                        tag=f"mm_g{th}_{c}_{tc}")
            if stage == "25b":
                for tc in range(4):
                    P.op("pe", (lambda tc_: lambda t: t.matmul(
                        psv(4)[:, tc_ * GSL + 96:tc_ * GSL + 108],
                        ones_r_f[:, :], geb_s[:, :], start=True, stop=True))(tc),
                        r=["ones_r_f", "geb_s"], w=["psK4b"] if tc == 3 else (),
                        tag=f"mm_gb{th}_{tc}")
        elif stage in ("3a", "3b", "3c"):
            for tc in range(4):
                gb = 48 * tc
                gl = glog[:, 12 * tc:12 * tc + 12]
                if stage == "3a":
                    P.op("dve", (lambda tc_, gl_: lambda v: v.tensor_reduce(
                        gl_,
                        psv(4)[:, tc_ * GSL:tc_ * GSL + GSL].rearrange(
                            "p (k e) -> p e k", e=12),
                        axis=AX.X, op=ALU.add))(tc, gl),
                        r=["psK4", "psK4b"], w=[f"glog{tc}"], tag=f"gacc_{th}_{tc}")
                    P.op("dve", (lambda gb_, gl_: lambda v: v.reduce_max(
                        gsc[:, gb_:gb_ + 1], gl_, axis=AX.X))(gb, gl),
                        r=[f"glog{tc}"], w=[f"g_m1{tc}"], tag=f"g_m1_{th}_{tc}")
                    P.op("pool", (lambda gb_, gl_: lambda v: v.tensor_scalar(
                        gsc[:, gb_ + 8:gb_ + 20], gl_,
                        gsc[:, gb_:gb_ + 1], None, op0=ALU.is_equal))(gb, gl),
                        r=[f"glog{tc}", f"g_m1{tc}"], w=[f"g_mk1{tc}"],
                        tag=f"g_mk1_{th}_{tc}")
                    P.op("pool", (lambda gb_: lambda v: v.tensor_scalar(
                        gsc[:, gb_ + 20:gb_ + 32], gsc[:, gb_ + 8:gb_ + 20],
                        -1e30, None, op0=ALU.mult))(gb),
                        r=[f"g_mk1{tc}"], w=[f"g_msk{tc}"], tag=f"g_msk_{th}_{tc}")
                    P.op("pool", (lambda gb_, gl_: lambda v: v.tensor_tensor(
                        gsc[:, gb_ + 20:gb_ + 32], gsc[:, gb_ + 20:gb_ + 32],
                        gl_, op=ALU.add))(gb, gl),
                        r=[f"g_msk{tc}", f"glog{tc}"], w=[f"g_msk{tc}"],
                        tag=f"g_msk2_{th}_{tc}")
                    P.op("dve", (lambda gb_: lambda v: v.reduce_max(
                        gsc[:, gb_ + 1:gb_ + 2], gsc[:, gb_ + 20:gb_ + 32],
                        axis=AX.X))(gb),
                        r=[f"g_msk{tc}"], w=[f"g_m2{tc}"], tag=f"g_m2_{th}_{tc}")
                    P.op("pool", (lambda gb_: lambda v: v.tensor_tensor(
                        gsc[:, gb_ + 2:gb_ + 3], gsc[:, gb_:gb_ + 1],
                        gsc[:, gb_ + 1:gb_ + 2], op=ALU.subtract))(gb),
                        r=[f"g_m1{tc}", f"g_m2{tc}"], w=[f"g_d12{tc}"],
                        tag=f"g_d12_{th}_{tc}")
                elif stage == "3b":
                    P.op("act", (lambda gb_: lambda s: s.activation(
                        gsc[:, gb_ + 3:gb_ + 4], gsc[:, gb_ + 2:gb_ + 3],
                        AF.Sigmoid))(gb),
                        r=[f"g_d12{tc}"], w=[f"g_w1{tc}"], tag=f"g_w1_{th}_{tc}")
                else:
                    P.op("pool", (lambda gb_: lambda v: v.tensor_scalar(
                        gsc[:, gb_ + 4:gb_ + 5], gsc[:, gb_ + 3:gb_ + 4],
                        -1.0, 1.0, op0=ALU.mult, op1=ALU.add))(gb),
                        r=[f"g_w1{tc}"], w=[f"g_w2{tc}"], tag=f"g_w2_{th}_{tc}")
                    P.op("pool", (lambda gb_, gl_: lambda v: v.tensor_scalar(
                        gsc[:, gb_ + 32:gb_ + 44], gl_,
                        gsc[:, gb_ + 1:gb_ + 2], None, op0=ALU.is_equal))(gb, gl),
                        r=[f"glog{tc}", f"g_m2{tc}"], w=[f"g_mk2{tc}"],
                        tag=f"g_mk2_{th}_{tc}")
                    P.op("pool", (lambda gb_: lambda v: v.tensor_scalar(
                        gsc[:, gb_ + 8:gb_ + 20], gsc[:, gb_ + 8:gb_ + 20],
                        gsc[:, gb_ + 3:gb_ + 4], None, op0=ALU.mult))(gb),
                        r=[f"g_mk1{tc}", f"g_w1{tc}"], w=[f"g_mk1{tc}"],
                        tag=f"g_a_{th}_{tc}")
                    P.op("pool", (lambda gb_: lambda v: v.tensor_scalar(
                        gsc[:, gb_ + 32:gb_ + 44], gsc[:, gb_ + 32:gb_ + 44],
                        gsc[:, gb_ + 4:gb_ + 5], None, op0=ALU.mult))(gb),
                        r=[f"g_mk2{tc}", f"g_w2{tc}"], w=[f"g_mk2{tc}"],
                        tag=f"g_b_{th}_{tc}")
                    P.op("pool", (lambda gb_: lambda v: v.tensor_tensor(
                        gsc[:, gb_ + 8:gb_ + 20], gsc[:, gb_ + 8:gb_ + 20],
                        gsc[:, gb_ + 32:gb_ + 44], op=ALU.add))(gb),
                        r=[f"g_mk1{tc}", f"g_mk2{tc}"], w=[f"g_mk1{tc}"],
                        tag=f"g_w12_{th}_{tc}")
                    P.op("pool", (lambda tc_, gb_: lambda v: v.tensor_copy(
                        wt16[:, tc_ * 16:tc_ * 16 + 8], gsc[:, gb_ + 8:gb_ + 16]))(tc, gb),
                        r=[f"g_mk1{tc}"], w=[f"wt16_{tc}"], war=["wt16z"],
                        tag=f"g_cp_{th}_{tc}")
                    P.op("dve", (lambda tc_, gb_: lambda v: v.reduce_sum(
                        wt16[:, tc_ * 16 + 8:tc_ * 16 + 9],
                        gsc[:, gb_ + 16:gb_ + 20], axis=AX.X))(tc, gb),
                        r=[f"g_mk1{tc}"], w=[f"wt16b_{tc}"], tag=f"g_wz_{th}_{tc}")
        elif stage == 35:
            for tc in range(4):
                P.op("pe", (lambda tc_: lambda t: t.transpose(
                    psv(5)[0:16, 0:128], wt16[:, tc_ * 16:(tc_ + 1) * 16],
                    eye_f[:, :]))(tc),
                    r=[f"wt16_{tc}", f"wt16b_{tc}", "wt16z", "eye_f"], w=["psK5"],
                    war=["psK5"], tag=f"g_tr_{th}_{tc}")
                P.op("act", (lambda wfm_t, tc_: lambda s: s.activation(
                    wfm_t[:, tc_ * 128:(tc_ + 1) * 128], psv(5)[0:16, 0:128],
                    AF.Identity))(wfm, tc),
                    r=["psK5"], w=[f"wfm{pt}_{tc}"], tag=f"ev_wfm_{th}_{tc}")
        elif stage == 4:
            wfmr = [f"wfm{pt}_{tc}" for tc in range(4)]
            P.op("pe", (lambda wfm_t: lambda t: t.matmul(
                psv(5), sel_s[:, 8 * 128:9 * 128], wfm_t[:, :],
                start=True, stop=True))(wfm),
                r=wfmr + ["sel_s"], w=["psK5"], war=["psK5"], tag=f"bc_wz{th}")
            P.op("act", (lambda p_: lambda s: s.activation(
                wzb_[p_][:, :], psv(5), AF.Identity))(pt),
                r=["psK5"], w=[f"wzb{pt}"], tag=f"ev_wz{th}")

    for k in (0, 1, "2a", "25a", "2b", "25b", "3a", "3b", "3c", 35, 4):
        prep_stage(0, k)

    wpf_done = set()

    def emit_wdma(th, e, fq):
        if (th, e, fq) in wpf_done:
            return
        wpf_done.add((th, e, fq))
        par = fq % 2
        P.dma(st_w1, "sp", (lambda e_, fq_, par_: lambda g: g.dma_start(
            out=w1b[par_][:, :].rearrange("p (c f) -> p c f", f=512),
            in_=w1c[e_].rearrange("(c p) f -> p c f", p=128)[:, :, fq_ * 512:(fq_ + 1) * 512]))(e, fq, par),
            r=[f"w1c_{e}"], w=[f"w1b{par}"], tag=f"ld_w1_{th}_{e}_{fq}")
        P.dma(st_w2, "sp", (lambda e_, fq_, par_: lambda g: g.dma_start(
            out=w2b[par_][:, :].rearrange("p (c d) -> p c d", d=1024),
            in_=w2c[e_][fq_ * 512:(fq_ + 1) * 512, :].rearrange(
                "(c p) d -> p c d", p=128)))(e, fq, par),
            r=[f"w2c_{e}"], w=[f"w2b{par}"], tag=f"ld_w2_{th}_{e}_{fq}")

    for th in range(NTH):
        q0 = th * TH
        pt = th % 2
        xv = xres[:, :].rearrange("p (c t) -> p c t", t=T)[:, :, q0:q0 + TH]
        h2v = h2b[pt][:, :].rearrange("p (c t) -> p c t", t=TH)
        wfm = wfm_[pt]
        wfmr = [f"wfm{pt}_{tc}" for tc in range(4)]
        def emit_yinit():
            for c in range(NCH):
                tp_ = t2f[c % 2]
                P.op("pool", (lambda h2v_, c_, p_, tp2: lambda g: g.tensor_tensor(
                    tp2[:, :], h2v_[:, c_, :], wzb_[p_][:, :], op=ALU.mult))(
                        h2v, c, pt, tp_),
                    r=[f"h2{pt}_{c}", f"wzb{pt}"], w=[f"t1f{c % 2}"],
                    tag=f"yz_{th}_{c}")
                P.op("dve", (lambda xv_, c_, tp2: lambda v: v.scalar_tensor_tensor(
                    yac[:, c_ * TH:(c_ + 1) * TH], xv_[:, c_, :],
                    invgb[:, c_:c_ + 1], tp2[:, :],
                    op0=ALU.mult, op1=ALU.add))(xv, c, tp_),
                    r=[f"xg{c}", "invgb", f"t1f{c % 2}"], w=[f"yac{c}"],
                    tag=f"y0_{th}_{c}")
        for e in range(NE):
            if th + 1 < NTH and 1 <= e <= 7:
                for st_k in ((0, 1), ("2a",), ("25a", "2b"), ("25b", "3a"),
                             ("3b", "3c"), (35,), (4,))[e - 1]:
                    prep_stage(th + 1, st_k)
            P.op("pe", (lambda e_, wfm_t: lambda t: t.matmul(
                ps_g[:, :], sel_s[:, e_ * 128:(e_ + 1) * 128], wfm_t[:, :],
                start=True, stop=True))(e, wfm),
                r=wfmr + ["sel_s"], w=["psK6"], war=["psK6"], tag=f"bc_wb{th}_{e}")
            P.op("act", lambda s: s.activation(wbb[:, :], ps_g[:, :], AF.Identity),
                 r=["psK6"], w=["wbb"], tag=f"ev_wb{th}_{e}")

            def emit_h(fq):
                par = fq % 2
                w1v = w1b[par][:, :].rearrange("p (c f) -> p c f", f=512)
                hev = heb[par][:, :].rearrange("p (c t) -> p c t", t=TH)
                emit_wdma(th, e, fq)
                for fc in range(4):
                    pb = fc % 2
                    for dc in range(NCH):
                        P.op("pe", (lambda w1v_, h2v_, fc_, dc_, pb_: lambda t: t.matmul(
                            psv(pb_), w1v_[:, dc_, fc_ * 128:(fc_ + 1) * 128],
                            h2v_[:, dc_, :], start=(dc_ == 0), stop=(dc_ == NCH - 1)))(
                                w1v, h2v, fc, dc, pb),
                            r=[f"w1b{par}", f"h2{pt}_{dc}"],
                            w=[f"psK{pb}"] if dc == NCH - 1 else (),
                            war=[f"psK{pb}"] if dc == 0 else (),
                            tag=f"mm_h{th}_{e}_{fq}_{fc}_{dc}")
                    P.op("act", (lambda hev_, fc_, pb_: lambda s: s.activation(
                        hev_[:, fc_, :], psv(pb_), AF.Silu))(hev, fc, pb),
                        r=[f"psK{pb}"], w=[f"he{par}_{fc}"], tag=f"silu{th}_{e}_{fq}_{fc}")
                    P.op("dve", (lambda hev_, fc_: lambda v: v.tensor_tensor(
                        hev_[:, fc_, :], hev_[:, fc_, :], wbb[:, :], op=ALU.mult))(hev, fc),
                        r=[f"he{par}_{fc}", "wbb"], w=[f"he{par}_{fc}"],
                        tag=f"hsc{th}_{e}_{fq}_{fc}")

            def emit_y(fq):
                par = fq % 2
                w2v = w2b[par][:, :].rearrange("p (c d) -> p c d", d=1024)
                hev = heb[par][:, :].rearrange("p (c t) -> p c t", t=TH)
                for dc in range(NCH):
                    pb = 2 + dc % 2
                    for fc in range(4):
                        P.op("pe", (lambda w2v_, hev_, fc_, dc_, pb_: lambda t: t.matmul(
                            psv(pb_), w2v_[:, fc_, dc_ * 128:(dc_ + 1) * 128],
                            hev_[:, fc_, :], start=(fc_ == 0), stop=(fc_ == 3)))(
                                w2v, hev, fc, dc, pb),
                            r=[f"w2b{par}", f"he{par}_{fc}"],
                            w=[f"psK{pb}"] if fc == 3 else (),
                            war=[f"psK{pb}"] if fc == 0 else (),
                            tag=f"mm_y{th}_{e}_{fq}_{dc}_{fc}")
                    P.op("dve", (lambda dc_, pb_: lambda v: v.tensor_tensor(
                        yac[:, dc_ * TH:(dc_ + 1) * TH], psv(pb_),
                        yac[:, dc_ * TH:(dc_ + 1) * TH], op=ALU.add))(dc, pb),
                        r=[f"psK{pb}", f"yac{dc}"], w=[f"yac{dc}"],
                        tag=f"ya{th}_{e}_{fq}_{dc}")

            emit_h(0)
            if e == 0:
                emit_yinit()
            for fq in range(4):
                if fq < 3:
                    emit_h(fq + 1)
                emit_y(fq)
            if e == NE - 1 and th + 1 < NTH:
                emit_wdma(th + 1, 0, 0)
                emit_wdma(th + 1, 0, 1)
        for c in range(NCH):
            P.op("dve", (lambda c_: lambda v: v.tensor_scalar(
                yac[:, c_ * TH:(c_ + 1) * TH], yac[:, c_ * TH:(c_ + 1) * TH],
                gbv[:, c_:c_ + 1], None, op0=ALU.mult))(c),
                r=[f"yac{c}", "gbv"], w=[f"yac{c}"], tag=f"gbm_{th}_{c}")
        P.dma(st_out, "pool", (lambda q0_: lambda e: e.dma_start(
            out=r3(out_fm)[:, :, q0_:q0_ + TH],
            in_=yac[:, :].rearrange("p (c t) -> p c t", t=TH)))(q0),
            r=[f"yac{c}" for c in range(NCH)], w=[f"od_y{th}"], tag=f"st_y{th}")

    P.set_barrier()

    outdeps = []
    if DEBUG_OUTS:
        outdeps += ["od_kt", "od_cq", "od_ctm", "od_h2", "od_wfm", "od_yini", "od_xa"]
        P.dma(st_out, "sp", lambda e: e.dma_start(out=dbg["mod"], in_=mods[:, :]),
              r=["mods", "mods_b"], w=["od_mod"], tag="st_mod")
        outdeps.append("od_mod")
    outdeps += [f"od_y{th}" for th in range(NTH)]
    P.barrier("sp", r=outdeps, tag="final")

    ctx_e.close()
    with ctx:
        block = ctx.enter_context(nc.Block())
        P.lower(nc, ctx, block)
    return nc



def make_inputs(full):
    eye = np.eye(128, dtype=np.float32)
    w1T = np.ascontiguousarray(full["L0_ffn_w1"].transpose(0, 2, 1))
    w2T = np.ascontiguousarray(full["L0_ffn_w2"].transpose(0, 2, 1))
    gwt = np.ascontiguousarray(full["L0_ffn_gate_w"].T)
    _ = full["L0_attn_b_UK"]
    common = dict(
        wdq=np.ascontiguousarray(full["L0_attn_W_DQ"]),
        bdq=np.ascontiguousarray(full["L0_attn_b_DQ"].reshape(DC, 1)),
        wdkv=np.ascontiguousarray(full["L0_attn_W_DKV"]),
        bdkv=np.ascontiguousarray(full["L0_attn_b_DKV"].reshape(DC, 1)),
        bdkvr=np.ascontiguousarray(full["L0_attn_b_DKV"].reshape(1, DC)),
        wkr=np.ascontiguousarray(full["L0_attn_W_KR"]),
        bkr=np.ascontiguousarray(full["L0_attn_b_KR"].reshape(DR, 1)),
        wuq=np.ascontiguousarray(full["L0_attn_W_UQ"]),
        buq=np.ascontiguousarray(full["L0_attn_b_UQ"].reshape(H, 128).T),
        wuk=np.ascontiguousarray(full["L0_attn_W_UK"]),
        wuv=np.ascontiguousarray(full["L0_attn_W_UV"]),
        buv=np.ascontiguousarray(full["L0_attn_b_UV"].reshape(H, 128).T),
        wqr=np.ascontiguousarray(full["L0_attn_W_QR"]),
        bqr=np.ascontiguousarray(full["L0_attn_b_QR"].reshape(H, DR).T),
        wo=np.ascontiguousarray(full["L0_attn_W_O"]),
        bo=np.ascontiguousarray(full["L0_attn_b_O"].reshape(NCH, 128).T),
        gwt=gwt,
        geb=np.ascontiguousarray(full["L0_ffn_expert_bias"].reshape(1, 12)),
        w1t=w1T, w2t=w2T,
        wmod=np.ascontiguousarray(full["L0_wmod"]),
        eye=eye,
        sel=np.ascontiguousarray(
            (np.arange(16)[:, None] == (np.arange(9 * 128) // 128)[None, :])
            .astype(np.float32)),
    )
    in_maps = []
    for core in range(8):
        b, half = core // 2, core % 2
        xT = np.ascontiguousarray(full["L0_attn_h_t"][b].T)
        if half == 1:
            xT = np.ascontiguousarray(np.roll(xT, -T, axis=1))
        m = dict(common)
        m["xfm"] = xT
        m["cfm"] = np.ascontiguousarray(full["L0_c"][b].reshape(NCH, 128).T)
        in_maps.append(m)
    return in_maps


def kernel(**inputs):
    nc = build_program()
    in_maps = make_inputs(inputs)
    res = run_bass_kernel_spmd(nc, in_maps, core_ids=list(range(8)))
    out = np.empty((B, S, D), dtype=np.float32)
    for core in range(8):
        b, half = core // 2, core % 2
        out[b, half * T:(half + 1) * T, :] = res.results[core]["out_fm"].T
    return out
